# version 06
# baseline (speedup 1.0000x reference)
import contextlib
import numpy as np
import concourse.bass as bass
import concourse.mybir as mybir
from concourse.bass_utils import run_bass_kernel_spmd

F32 = mybir.dt.float32
F32R = mybir.dt.float32r
BF16 = mybir.dt.bfloat16
AF = mybir.ActivationFunctionType
ALU = mybir.AluOpType
AX = mybir.AxisListType

B, L, D = 4, 2048, 2048
TT = 1024
HFW = 512
CH = 128
NCH = TT // CH
KVH, QHN, HD = 8, 16, 128
DCH = 16
GATE = 16
FF = 8192
NE = 4
FBLK = 4
NBLK = FF // 128 // FBLK
QSCALE = float(HD) ** -0.5
EPS = 1e-5

ENGS = ("sync", "scalar", "vector", "tensor", "gpsimd")


class P:

    def __init__(self, nc, stack):
        self.nc = nc
        self.ops = {e: [] for e in ENGS}
        self.sems = {}
        self.cnt = {}
        self.stack = stack
        self.guards = {}

    def sem(self, name):
        if name not in self.sems:
            self.sems[name] = self.stack.enter_context(self.nc.semaphore(name))
            self.cnt[name] = 0
        return self.sems[name]

    def c(self, name):
        self.sem(name)
        return self.cnt[name]

    def wait(self, eng, name, val):
        if val is None or val <= 0:
            return
        h = self.sem(name)
        self.ops[eng].append(lambda e, h=h, v=val: e.wait_ge(h, v))

    def wcur(self, eng, name):
        self.wait(eng, name, self.c(name))

    def guard(self, eng, key):
        g = self.guards.get(key)
        if g is None:
            return
        for (nm, v) in (g if isinstance(g, list) else [g]):
            self.wait(eng, nm, v)

    def set_guard(self, key, name, val):
        self.guards[key] = (name, val)

    def dma(self, eng, name, out, in_, **kw):
        h = self.sem(name)
        self.cnt[name] += 16
        self.ops[eng].append(
            lambda e, o=out, i=in_, h=h, kw=kw:
            e.dma_start(o, i, **kw).then_inc(h, 16))
        return self.cnt[name]

    def op(self, eng, name, fn):
        if name is None:
            self.ops[eng].append(fn)
            return None
        h = self.sem(name)
        self.cnt[name] += 1
        self.ops[eng].append(lambda e, fn=fn, h=h: fn(e).then_inc(h, 1))
        return self.cnt[name]

    def mark(self, phase):
        if not hasattr(self, "marks"):
            self.marks = []
        self.marks.append((phase, {e: len(self.ops[e]) for e in ENGS}))

    def replay(self, eng, e):
        bounds = []
        for (ph, idx) in getattr(self, "marks", []):
            bounds.append((idx[eng], ph))
        cur = None
        for i, f in enumerate(self.ops[eng]):
            for (bi, ph) in bounds:
                if bi == i:
                    if cur is not None:
                        self.nc.leave_named_scope(cur[0], cur[1], False)
                    sid, _ = self.nc.enter_named_scope(ph + "_" + eng, False)
                    cur = (ph + "_" + eng, sid)
            f(e)
        if cur is not None:
            self.nc.leave_named_scope(cur[0], cur[1], False)


def r32(ap):
    return ap.bitcast(F32R)


def build_program(debug=False):
    nc = bass.Bass("TRN2", target_bir_lowering=False)
    st = contextlib.ExitStack()

    def din(name, shape):
        return nc.dram_tensor(name, shape, F32, kind="ExternalInput")

    xt = din("xt", [D, TT])
    xpt = din("xpt", [D, TT])
    flagd = din("flagd", [128, 1])
    gnwd = din("gnwd", [128, 1])
    bqt = din("bqt", [128, 16])
    bkt = din("bkt", [128, 8])
    bvt = din("bvt", [128, 8])
    bg2t = din("bg2t", [128, 8])
    onesd = din("onesd", [1, 128])
    identd = din("identd", [128, 128])
    amaskd = din("amaskd", [128, 128])
    wq = din("wq", [D, D])
    wk = din("wk", [D, 1024])
    wv = din("wv", [D, 1024])
    wg1d = din("wg1d", [128, DCH, GATE])
    wg2d = din("wg2d", [GATE, 1024])
    wo = din("wo", [D, D])
    wrd = din("wrd", [128, DCH, NE])
    brd = din("brd", [1, NE])
    seld = din("seld", [NE, NE * 128])
    wgl = din("wgl", [NE, D, FF])
    wul = din("wul", [NE, D, FF])
    wdl = din("wdl", [NE, FF, D])
    yout = nc.dram_tensor("y", [TT, D], F32, kind="ExternalOutput")
    dbg_t = {}

    def dbg_dram(name, shape):
        if name not in dbg_t:
            dbg_t[name] = nc.dram_tensor(name, shape, F32,
                                         kind="ExternalOutput")
        return dbg_t[name]

    xt_r = xt.rearrange("(d p) t -> p d t", p=128)
    xpt_r = xpt.rearrange("(d p) t -> p d t", p=128)
    wq_c = wq.rearrange("(d p) m -> p d m", p=128)
    wk_c = wk.rearrange("(d p) m -> p d m", p=128)
    wv_c = wv.rearrange("(d p) m -> p d m", p=128)
    wo_r = wo.rearrange("(h p) m -> p h m", p=128)
    wgl_c = [wgl[e].rearrange("(d p) m -> p d m", p=128) for e in range(NE)]
    wul_c = [wul[e].rearrange("(d p) m -> p d m", p=128) for e in range(NE)]
    wdl_r = [wdl[e].rearrange("(c p) m -> p c m", p=128) for e in range(NE)]

    def sb(name, shape, dt):
        return st.enter_context(nc.sbuf_tensor(name, shape, dt))

    ones = sb("ones", [1, 128], F32)
    onescol = sb("onescol", [128, 1], F32)
    identb = sb("identb", [128, 128], BF16)
    identf = sb("identf", [128, 128], F32)
    amask = sb("amask", [128, 128], F32)
    zeros = sb("zeros", [128, 128], F32)
    bq_s = sb("bq_s", [128, 16], F32)
    bk_s = sb("bk_s", [128, 8], F32)
    bv_s = sb("bv_s", [128, 8], F32)
    bg2p = sb("bg2p", [128, 8], F32)
    bg2n = sb("bg2n", [128, 8], F32)
    gnws = sb("gnws", [128, 1], F32)
    flags = sb("flags", [128, 1], F32)
    epsc = sb("epsc", [128, 1], F32)
    wr_s = sb("wr_s", [128, DCH, NE], F32)
    br_s = sb("br_s", [1, NE], F32)
    sel4 = sb("sel4", [NE, NE * 128], F32)
    wg1_s = sb("wg1_s", [128, DCH, GATE], F32R)
    small_s = sb("small_s", [GATE, 2048], F32)
    xg1s = small_s[:, 0:1024]
    wg2_s = small_s[:, 1024:2048]
    rows3 = small_s[0:1, 0:1024]
    rows5 = small_s[0:1, 0:1024]
    mxs = sb("mxs", [128, 112], F32)
    maskfm = small_s[0:4, 1024:2048]

    ar1 = sb("ar1", [128, DCH, TT], F32)
    ar2 = sb("ar2", [128, 16, TT], BF16)
    ar3 = sb("ar3", [128, 8, TT], BF16)
    ar4 = sb("ar4", [128, 8, TT], BF16)
    ar5 = sb("ar5", [128, 8, TT], BF16)
    ar6 = sb("ar6", [128, 4096], F32)
    ar7 = sb("ar7", [128, 4096], F32R)
    ar8 = sb("ar8", [128, 2816], F32)
    s_t = sb("s_t", [128, KVH, HD], F32)
    s_bf = sb("s_bf", [128, KVH, HD], BF16)

    xv = ar1
    qv = ar2
    y1bf = ar2
    kv = ar3
    hbuf = ar3
    vv = ar4
    wdtile = ar4.rearrange("p h t -> p (h t)").rearrange(
        "p (c m) -> p c m", c=FBLK)
    vfm = ar5
    gkv = ar5
    wo4 = ar5.rearrange("p h t -> p (h t)").rearrange(
        "p (s m) -> p s m", s=4)
    ar6b = ar6.bitcast(BF16)
    wgtile = ar6b[:, 0:4096].rearrange("p (b d m) -> p b d m", b=2, d=DCH)
    wutile = ar6b[:, 4096:8192].rearrange("p (b d m) -> p b d m", b=2, d=DCH)
    wstrr = ar7.rearrange("p (b d m) -> p b d m", b=2, d=DCH)
    xvr = ar1.bitcast(F32R)

    def onv(qh):
        return qv[:, qh, :]

    GST = 1344

    def gsc(par):
        b0 = par * GST
        w = ar8
        wb = ar8.bitcast(BF16)
        return dict(
            b=w[:, b0:b0 + 128],
            expb=w[:, b0 + 128:b0 + 256],
            expnb=w[:, b0 + 256:b0 + 384],
            ebl=w[:, b0 + 384:b0 + 385],
            of0=w[:, b0 + 400:b0 + 528],
            of1=w[:, b0 + 528:b0 + 656],
            sq0=w[:, b0 + 656:b0 + 784],
            sq1=w[:, b0 + 784:b0 + 912],
            ks=wb[:, 2 * b0 + 1824:2 * b0 + 1952],
            ksT=wb[:, 2 * b0 + 1952:2 * b0 + 2080],
            am0=wb[:, 2 * b0 + 2080:2 * b0 + 2208],
            am1=wb[:, 2 * b0 + 2208:2 * b0 + 2336],
            qs0=wb[:, 2 * b0 + 2336:2 * b0 + 2464],
            qs1=wb[:, 2 * b0 + 2464:2 * b0 + 2592],
            qs01=wb[:, 2 * b0 + 2336:2 * b0 + 2592],
            am01=wb[:, 2 * b0 + 2080:2 * b0 + 2336],
            of01=w[:, b0 + 400:b0 + 656],
            sq01=w[:, b0 + 656:b0 + 912],
        )

    gkch = [ar8[:, 0:1024], ar8[:, 1024:2048]]
    y1scr = ar8[:, 0:512]
    sqscr = [ar8[:, 1024:1536], ar8[:, 1536:2048]]
    mask_rep = ar8[:, 0:1024]
    sgscr = ar8.bitcast(BF16)[:, 2048:4096]
    ytm = ar8[:, 0:2048]

    pb = [st.enter_context(nc.psum_tensor(f"pb{i}", [128, HFW], F32))
          for i in range(8)]

    p = P(nc, st)

    for (dst, src) in [(ones, onesd), (identf, identd), (amask, amaskd),
                       (bq_s, bqt), (bk_s, bkt), (bv_s, bvt), (bg2p, bg2t),
                       (gnws, gnwd), (flags, flagd), (wr_s, wrd),
                       (br_s, brd), (sel4, seld),
                       (wg2_s, wg2d)]:
        nd = len(dst.shape)
        p.dma("sync", "s_c", dst[(slice(None),) * nd], src[(slice(None),) * nd])
    n_c = p.c("s_c")
    p.dma("gpsimd", "s_cb", identb[:, :], identd[:, :])
    p.dma("gpsimd", "s_cb", wg1_s[:, :, :], wg1d[:, :, :])
    for d in range(DCH):
        p.dma("gpsimd", "s_x", xvr[:, d, :], xpt_r[:, d, :])
    n_xp = p.c("s_x")
    p.wait("vector", "s_c", n_c)
    p.op("vector", "s_c0", lambda e: e.memset(zeros[:, :], 0.0))
    p.op("vector", "s_c0", lambda e: e.memset(onescol[:, :], 1.0))
    p.op("vector", "s_c0", lambda e: e.memset(epsc[:, :], EPS))
    p.op("vector", "s_c0",
         lambda e: e.memset(s_t.rearrange("p h t -> p (h t)"), 0.0))
    p.op("vector", "s_c0",
         lambda e: e.tensor_scalar_mul(bq_s[:, :], bq_s[:, :], QSCALE))
    p.op("vector", "s_c0",
         lambda e: e.tensor_scalar_mul(bg2n[:, :], bg2p[:, :], -1.0))
    n_c0 = p.c("s_c0")
    for eng in ("tensor", "scalar", "gpsimd"):
        p.wait(eng, "s_c0", n_c0)

    wu = {"u": 0, "mm": {}}

    def proj_matmuls(w_c, j, xload, dst, func, bias, scale):
        u = wu["u"]
        wu["u"] += 1
        buf = u % 2
        if u - 2 in wu["mm"]:
            p.wait("gpsimd", "s_pm", wu["mm"][u - 2])
        ld = p.dma("gpsimd", "s_wa", wstrr[:, buf, :, 0:128],
                   w_c[:, :, j * 128:(j + 1) * 128])
        p.wait("tensor", "s_wa", ld)
        p.wait("tensor", "s_x", xload)
        bis = [0 + (u % 2) * 2, 1 + (u % 2) * 2]
        for bi in bis:
            p.guard("tensor", ("bank", bi))
        for half, bi in ((0, bis[0]), (1, bis[1])):
            bank = pb[bi]
            for d in range(DCH):
                last = d == DCH - 1
                p.op("tensor", "s_pm" if last else None,
                     lambda e, bank=bank, half=half, buf=buf, d=d,
                     s=(d == 0), z=last:
                     e.matmul(bank[:, 0:HFW], wstrr[:, buf, d, :],
                              xvr[:, d, half * HFW:(half + 1) * HFW],
                              start=s, stop=z))
        wu["mm"][u] = p.c("s_pm")
        for half, bi in ((0, bis[0]), (1, bis[1])):
            bank = pb[bi]
            p.wcur("scalar", "s_pm")
            ev = p.op("scalar", "s_pe",
                      lambda e, a=bank[:, 0:HFW],
                      o=dst[:, j, half * HFW:(half + 1) * HFW],
                      f=func, sc=scale, bi_=bias[:, j:j + 1]:
                      e.activation(o, a, f, scale=sc, bias=bi_))
            p.set_guard(("bank", bi), "s_pe", ev)

    def proj_pass(xload, do_q):
        for j in range(KVH):
            proj_matmuls(wk_c, j, xload, kv, AF.Relu, bk_s, 1.0)
        for j in range(KVH):
            proj_matmuls(wv_c, j, xload, vfm, AF.Identity, bv_s, 1.0)
        p.wcur("tensor", "s_pe")
        p.wait("tensor", "s_cb", 16)
        for tc in range(NCH):
            bi = 4 + tc % 2
            bank = pb[bi]
            p.guard("tensor", ("bank", bi))
            bankb = bank.bitcast(BF16)
            for h in range(KVH):
                p.op("tensor", "s_vt" if h == KVH - 1 else None,
                     lambda e, o=bankb[:, h * 128:(h + 1) * 128],
                     i=vfm[:, h, tc * 128:(tc + 1) * 128]:
                     e.transpose(o, i, identb[:, :]))
            p.wcur("vector", "s_vt")
            ev = p.op("vector", "s_vte",
                      lambda e, o=vv[:, tc, :], a=bankb[:, 0:1024]:
                      e.tensor_copy(o, a))
            p.set_guard(("bank", bi), "s_vte", ev)
        p.wcur("tensor", "s_vte")
        for half in range(2):
            p.guard("tensor", ("bank", 6))
            for d in range(DCH):
                last = d == DCH - 1
                p.op("tensor", "s_pm" if last else None,
                     lambda e, d=d, half=half, s=(d == 0), z=last:
                     e.matmul(pb[6][0:GATE, 0:HFW], wg1_s[:, d, :],
                              xvr[:, d, half * HFW:(half + 1) * HFW],
                              start=s, stop=z))
            p.wcur("vector", "s_pm")
            p.guard("vector", ("xg1s", half))
            ev = p.op("vector", "s_g1",
                      lambda e, half=half:
                      e.tensor_copy(xg1s[0:GATE, half * HFW:(half + 1) * HFW],
                                    pb[6][0:GATE, 0:HFW]))
            p.set_guard(("bank", 6), "s_g1", ev)
        for j in range(KVH):
            for half in range(2):
                par = (j * 2 + half) % 2
                bi = 5 + 2 * par
                bank = pb[bi]
                p.guard("tensor", ("bank", bi))
                p.wcur("tensor", "s_g1")
                gm = p.op("tensor", "s_gm",
                          lambda e, j=j, half=half, bank=bank:
                          e.matmul(bank[:, 0:HFW],
                                   wg2_s[0:GATE, j * 128:(j + 1) * 128],
                                   xg1s[0:GATE,
                                        half * HFW:(half + 1) * HFW],
                                   start=True, stop=True))
                p.set_guard(("xg1s", half), "s_gm", gm)
                sa = gkch[par][:, 0:512]
                sbb = gkch[par][:, 512:1024]
                p.wait("scalar", "s_gm", gm)
                p.guard("scalar", ("gkch", par))
                p.op("scalar", None,
                     lambda e, a=bank[:, 0:HFW], o=sa, j=j:
                     e.activation(o, a, AF.Relu, scale=-1.0,
                                  bias=bg2n[:, j:j + 1]))
                ab = p.op("scalar", "s_gk",
                          lambda e, a=bank[:, 0:HFW], o=sbb, j=j:
                          e.activation(o, a, AF.Abs, bias=bg2p[:, j:j + 1]))
                p.set_guard(("bank", bi), "s_gk", ab)
                p.op("scalar", None,
                     lambda e, o=sbb: e.activation(o, o, AF.Exp, scale=-1.0))
                gk2 = p.op("scalar", "s_gk",
                           lambda e, o=sbb:
                           e.activation(o, o, AF.Ln, bias=1.0))
                p.wait("vector", "s_gk", gk2)
                p.op("vector", None,
                     lambda e, a=sa, b_=sbb: e.tensor_add(a, a, b_))
                gv_ = p.op("vector", "s_gv",
                           lambda e, a=sa,
                           o=gkv[:, j, half * HFW:(half + 1) * HFW]:
                           e.tensor_scalar_mul(o, a, -1.0 / 16.0))
                p.set_guard(("gkch", par), "s_gv", gv_)
        if do_q:
            for j in range(QHN):
                proj_matmuls(wq_c, j, xload, qv, AF.Relu, bq_s, QSCALE)

    def scan_chunk(c, h, it, full):
        par = it % 2
        g = gsc(par)
        p.guard("vector", ("gscr", par))
        p.guard("scalar", ("gscr", par))
        sc = p.op("vector", "s_sc",
                  lambda e, o=g["b"], i=gkv[:, h, c * CH:(c + 1) * CH]:
                  e.tensor_tensor_scan(o, i, zeros[:, :], 0.0,
                                       ALU.add, ALU.add))
        p.wait("scalar", "s_sc", sc)
        if full:
            p.op("scalar", None,
                 lambda e, o=g["expb"], i=g["b"]: e.activation(o, i, AF.Exp))
        p.op("scalar", None,
             lambda e, o=g["expnb"], i=g["b"]:
             e.activation(o, i, AF.Exp, scale=-1.0))
        ex = p.op("scalar", "s_ex",
                  lambda e, o=g["ebl"], i=g["b"][:, CH - 1:CH]:
                  e.activation(o, i, AF.Exp))
        p.wait("vector", "s_ex", ex)
        qk = p.op("vector", "s_qk",
                  lambda e, o=g["ks"], a=kv[:, h, c * CH:(c + 1) * CH],
                  b_=g["expnb"]: e.tensor_mul(o, a, b_))
        if full:
            qk = p.op("vector", "s_qk",
                      lambda e,
                      o=g["qs01"].rearrange("p (a c) -> p a c", a=2),
                      a=qv[:, 2 * h:2 * h + 2, c * CH:(c + 1) * CH],
                      b_=g["expb"][:, :].unsqueeze(1).broadcast_to(
                          [128, 2, 128]):
                      e.tensor_mul(o, a, b_))
        biT = 2 + par
        pT = pb[biT]
        p.wait("tensor", "s_qk", qk)
        p.guard("tensor", ("bank", biT))
        kt = p.op("tensor", "s_kt",
                  lambda e, o=pT.bitcast(BF16)[:, 0:128], i=g["ks"]:
                  e.transpose(o, i, identb[:, :]))
        p.wait("vector", "s_kt", kt)
        kte = p.op("vector", "s_kte",
                   lambda e, o=g["ksT"], a=pT.bitcast(BF16)[:, 0:128]:
                   e.tensor_copy(o, a))
        om = None
        if full:
            biA = 0 + par
            pA = pb[biA]
            p.guard("tensor", ("bank", biA))
            am = p.op("tensor", "s_am",
                      lambda e, o=pA[:, 0:256], lh=g["ks"], rh=g["qs01"]:
                      e.matmul(o, lh, rh, start=True, stop=True))
            p.wait("vector", "s_am", am)
            ame = p.op("vector", "s_ame",
                       lambda e,
                       o=g["am01"].rearrange("p (a c) -> p a c", a=2),
                       a=pA[:, 0:256].rearrange("p (a c) -> p a c", a=2),
                       m=amask[:, :].unsqueeze(1).broadcast_to([128, 2, 128]):
                       e.tensor_mul(o, a, m))
            p.set_guard(("bank", biA), "s_ame", ame)
            biO = 4 + par
            pO = pb[biO]
            p.wait("tensor", "s_ame", ame)
            p.wait("tensor", "s_sbf",
                   1 if c == 0 else 1 + (c - 1) * KVH + h + 1)
            p.guard("tensor", ("bank", biO))
            p.op("tensor", None,
                 lambda e, o=pO[:, 0:256],
                 lh=vv[:, c, h * 128:(h + 1) * 128], rh=g["am01"]:
                 e.matmul(o, lh, rh, start=True, stop=False))
            om = p.op("tensor", "s_om",
                      lambda e, o=pO[:, 0:256], lh=s_bf[:, h, :],
                      rh=g["qs01"]:
                      e.matmul(o, lh, rh, start=False, stop=True))
        p.wait("tensor", "s_kte", kte)
        ds = p.op("tensor", "s_ds",
                  lambda e, o=pT[:, 64:192], lh=g["ksT"],
                  rh=vv[:, c, h * 128:(h + 1) * 128]:
                  e.matmul(o, lh, rh, start=True, stop=True))
        p.wait("vector", "s_ds", ds)
        p.op("vector", None,
             lambda e, o=s_t[:, h, :], a=pT[:, 64:192]: e.tensor_add(o, o, a))
        su = p.op("vector", "s_su",
                  lambda e, o=s_t[:, h, :], sc_=g["ebl"]:
                  e.tensor_scalar_mul(o, o, sc_))
        scr_guards = [("s_su", su)]
        p.guards[("bank", biT)] = [("s_kte", kte), ("s_su", su)]
        if full:
            scr_guards.append(("s_om", om))
            p.wait("gpsimd", "s_su", su)
            p.wait("gpsimd", "s_om", om)
            p.op("gpsimd", "s_sbf",
                 lambda e, o=s_bf[:, h, :], i=s_t[:, h, :]:
                 e.tensor_copy(o, i))
            biO = 4 + par
            pO = pb[biO]
            p.wait("scalar", "s_om", om)
            p.op("scalar", None,
                 lambda e, o=g["of01"], a=pO[:, 0:256]:
                 e.activation(o, a, AF.Copy))
            oev = p.op("scalar", "s_oev",
                       lambda e, o=g["sq01"], a=pO[:, 0:256]:
                       e.activation(o, a, AF.Square))
            p.set_guard(("bank", biO), "s_oev", oev)
            pR = pb[6]
            p.wait("tensor", "s_oev", oev)
            p.guard("tensor", ("rowb",))
            rc = p.op("tensor", "s_rc",
                      lambda e, o=pR[0:1, par * 256:par * 256 + 256],
                      lh=onescol[:, :], rh=g["sq01"]:
                      e.matmul(o, lh, rh, start=True, stop=True))
            p.wait("scalar", "s_rc", rc)
            p.op("scalar", None,
                 lambda e, o=rows3[0:1, par * 256:par * 256 + 256],
                 a=pR[0:1, par * 256:par * 256 + 256]:
                 e.activation(o, a, AF.Ln, scale=1.0 / HD,
                              bias=epsc[0:1, 0:1]))
            rw = p.op("scalar", "s_rw",
                      lambda e, o=rows3[0:1, 512 + par * 256:768 + par * 256],
                      i=rows3[0:1, par * 256:par * 256 + 256]:
                      e.activation(o, i, AF.Exp, scale=-0.5))
            p.set_guard(("rowb",), "s_rw", rw)
            pP = pb[7]
            p.wait("tensor", "s_rw", rw)
            p.guard("tensor", ("repb",))
            rp = p.op("tensor", "s_rp",
                      lambda e, o=pP[:, par * 256:par * 256 + 256],
                      lh=ones[0:1, 0:128],
                      rh=rows3[0:1, 512 + par * 256:768 + par * 256]:
                      e.matmul(o, lh, rh, start=True, stop=True))
            p.wait("vector", "s_rp", rp)
            on = p.op("vector", "s_on",
                      lambda e,
                      o=qv[:, 2 * h:2 * h + 2, c * CH:(c + 1) * CH],
                      a=g["of01"].rearrange("p (a c) -> p a c", a=2),
                      r=pP[:, par * 256:par * 256 + 256].rearrange(
                          "p (a c) -> p a c", a=2):
                      e.tensor_mul(o, a, r))
            p.set_guard(("repb",), "s_on", on)
            scr_guards.append(("s_on", on))
        p.guards[("gscr", par)] = scr_guards

    def dump_fm(name, src_ap, nd, width):
        if not debug:
            return
        t = dbg_dram(name, [nd * 128, width])
        tr = t.rearrange("(d p) t -> p d t", p=128)
        for d in range(nd):
            p.dma("scalar", "s_dbg", tr[:, d, :], src_ap[:, d, :])

    def dump_bf(name, src_ap, nd, width):
        if not debug:
            return
        t = dbg_dram(name, [nd * 128, width])
        tr = t.rearrange("(d p) t -> p d t", p=128)
        for d in range(nd):
            p.dma("gpsimd", "s_dbg", tr[:, d, :], src_ap[:, d, :])

    p.mark("P1prefix")
    proj_pass(n_xp, do_q=False)
    it = 0
    for c in range(NCH):
        for h in range(KVH):
            scan_chunk(c, h, it, full=False)
            it += 1
    p.wcur("vector", "s_su")
    fl = p.op("vector", "s_su",
              lambda e: e.tensor_scalar_mul(
                  s_t.rearrange("p h t -> p (h t)"),
                  s_t.rearrange("p h t -> p (h t)"), flags[:, :]))
    p.wait("gpsimd", "s_su", fl)
    p.op("gpsimd", "s_sbf",
         lambda e: e.tensor_copy(s_bf.rearrange("p h t -> p (h t)"),
                                 s_t.rearrange("p h t -> p (h t)")))

    p.mark("P2proj")
    p.wcur("gpsimd", "s_pm")
    p.wcur("gpsimd", "s_gm")
    for d in range(DCH):
        p.dma("gpsimd", "s_x", xvr[:, d, :], xt_r[:, d, :])
    n_x = p.c("s_x")
    p.wcur("scalar", "s_su")
    p.wcur("scalar", "s_kte")
    proj_pass(n_x, do_q=True)

    if debug:
        p.wcur("scalar", "s_pe")
        p.wcur("scalar", "s_gv")
        p.wcur("gpsimd", "s_pe")
        p.wcur("gpsimd", "s_gv")
        p.wcur("gpsimd", "s_su")
        dump_bf("d_q", qv, 16, TT)
        dump_bf("d_k", kv, 8, TT)
        dump_bf("d_v", vv, 8, TT)
        dump_bf("d_gk", gkv, 8, TT)
        dump_fm("d_s", s_t, 8, HD)
        p.wcur("vector", "s_dbg")
        p.wcur("gpsimd", "s_dbg")

    p.mark("P3gla")
    p.wcur("vector", "s_pe")
    p.wcur("vector", "s_gv")
    p.wcur("tensor", "s_vte")
    it = 0
    for c in range(NCH):
        for h in range(KVH):
            scan_chunk(c, h, it, full=True)
            it += 1

    p.mark("P4wo")
    n_on_all = p.c("s_on")
    p.wait("tensor", "s_on", n_on_all)
    p.wcur("gpsimd", "s_sc")
    p.wcur("gpsimd", "s_qk")
    for hp in range(8):
        sc_ = None
        for k2 in range(2):
            h = 2 * hp + k2
            slot = (hp % 2) * 2 + k2
            p.guard("gpsimd", ("wo", slot))
            ld = p.dma("gpsimd", "s_wo", wo4[:, slot, :], wo_r[:, h, :])
            p.wait("gpsimd", "s_wo", ld)
            sc_ = p.op("gpsimd", "s_wos",
                       lambda e, t=wo4[:, slot, :]:
                       e.tensor_scalar_mul(t, t, gnws[:, :]))
        p.wait("tensor", "s_wos", sc_)
        s0 = (hp % 2) * 2
        for d in range(DCH):
            for half in range(2):
                bi = (d * 2 + half) % 4
                bank = pb[bi]
                p.guard("tensor", ("bank", bi))
                p.op("tensor", None,
                     lambda e, o=bank[:, 0:HFW],
                     lh=wo4[:, s0, d * 128:(d + 1) * 128],
                     rh=onv(2 * hp)[:, half * HFW:(half + 1) * HFW]:
                     e.matmul(o, lh, rh, start=True, stop=False))
                wm = p.op("tensor", "s_wm",
                          lambda e, o=bank[:, 0:HFW],
                          lh=wo4[:, s0 + 1, d * 128:(d + 1) * 128],
                          rh=onv(2 * hp + 1)[:, half * HFW:(half + 1) * HFW]:
                          e.matmul(o, lh, rh, start=False, stop=True))
                p.wait("vector", "s_wm", wm)
                pr = p.op("vector", "s_pr",
                          lambda e, o=xvr[:, d, half * HFW:(half + 1) * HFW],
                          i=xv[:, d, half * HFW:(half + 1) * HFW],
                          a=bank[:, 0:HFW]:
                          e.tensor_add(o, i, a))
                p.set_guard(("bank", bi), "s_pr", pr)
        p.set_guard(("wo", s0), "s_wm", p.c("s_wm"))
        p.set_guard(("wo", s0 + 1), "s_wm", p.c("s_wm"))

    def ln_pass(is_ln1):
        tagm = "s_l1m" if is_ln1 else "s_l2m"
        tagq = "s_l1q" if is_ln1 else "s_l2q"
        tagr = "s_l1r" if is_ln1 else "s_l2r"
        tagp = "s_l1p" if is_ln1 else "s_l2p"
        tagt = "s_l1t" if is_ln1 else "s_l2t"
        for half in range(2):
            p.guard("tensor", ("bank", 4))
            p.guard("tensor", ("bank", 5))
            for d in range(DCH):
                last = d == DCH - 1
                p.guard("scalar", ("sqscr", d % 2))
                sq = p.op("scalar", tagq,
                          lambda e, o=sqscr[d % 2],
                          i=xv[:, d, half * HFW:(half + 1) * HFW]:
                          e.activation(o, i, AF.Square))
                p.op("tensor", None,
                     lambda e, o=pb[4][0:1, 0:HFW], lh=onescol[:, :],
                     rh=xv[:, d, half * HFW:(half + 1) * HFW],
                     s=(d == 0), z=last:
                     e.matmul(o, lh, rh, start=s, stop=z))
                p.wait("tensor", tagq, sq)
                mm = p.op("tensor", tagm,
                          lambda e, o=pb[5][0:1, 0:HFW], lh=onescol[:, :],
                          rh=sqscr[d % 2], s=(d == 0), z=last:
                          e.matmul(o, lh, rh, start=s, stop=z))
                p.set_guard(("sqscr", d % 2), tagm, mm)
            nm = p.c(tagm)
            p.wait("scalar", tagm, nm)
            p.wcur("scalar", tagp)
            p.op("scalar", None,
                 lambda e, o=rows5[0:1, 0:512], a=pb[4][0:1, 0:HFW]:
                 e.activation(o, a, AF.Copy, scale=1.0 / 2048.0))
            mu2 = p.op("scalar", tagr,
                       lambda e, o=rows5[0:1, 512:1024], i=rows5[0:1, 0:512]:
                       e.activation(o, i, AF.Square))
            p.wait("vector", tagr, mu2)
            vr = p.op("vector", tagr,
                      lambda e, o=rows5[0:1, 512:1024], a=pb[5][0:1, 0:HFW],
                      m2=rows5[0:1, 512:1024]:
                      e.scalar_tensor_tensor(o, a, 1.0 / 2048.0, m2,
                                             ALU.mult, ALU.subtract))
            p.wait("scalar", tagr, vr)
            p.op("scalar", None,
                 lambda e, o=rows5[0:1, 512:1024], i=rows5[0:1, 512:1024]:
                 e.activation(o, i, AF.Ln, bias=epsc[0:1, 0:1]))
            rs = p.op("scalar", tagr,
                      lambda e, o=rows5[0:1, 512:1024],
                      i=rows5[0:1, 512:1024]:
                      e.activation(o, i, AF.Exp, scale=-0.5))
            p.wait("tensor", tagr, rs)
            p.guard("tensor", ("bank", 6))
            p.guard("tensor", ("bank", 7))
            p.op("tensor", None,
                 lambda e, o=pb[6][:, 0:HFW], lh=ones[0:1, 0:128],
                 rh=rows5[0:1, 0:512]:
                 e.matmul(o, lh, rh, start=True, stop=True))
            rp = p.op("tensor", tagp,
                      lambda e, o=pb[7][:, 0:HFW], lh=ones[0:1, 0:128],
                      rh=rows5[0:1, 512:1024]:
                      e.matmul(o, lh, rh, start=True, stop=True))
            p.wait("vector", tagp, rp)
            for d in range(DCH):
                p.op("vector", None,
                     lambda e, o=y1scr,
                     i=xv[:, d, half * HFW:(half + 1) * HFW],
                     m=pb[6][:, 0:HFW]:
                     e.tensor_sub(o, i, m))
                tr = p.op("vector", tagt,
                          lambda e, o=xvr[:, d, half * HFW:(half + 1) * HFW],
                          i=y1scr, r=pb[7][:, 0:HFW]:
                          e.tensor_mul(o, i, r))
                if is_ln1:
                    p.wait("scalar", tagt, tr)
                    p.op("scalar", "s_l1b",
                         lambda e, o=y1bf[:, d, half * HFW:(half + 1) * HFW],
                         i=xv[:, d, half * HFW:(half + 1) * HFW]:
                         e.activation(o, i, AF.Copy))
            if is_ln1:
                p.wait("tensor", tagt, p.c(tagt))
                for ti in range(4):
                    col = (half * 4 + ti) * NE
                    for d in range(DCH):
                        p.op("tensor", None,
                             lambda e, o=pb[2][:, col:col + NE],
                             lh=xv[:, d, half * HFW + ti * 128:
                                   half * HFW + (ti + 1) * 128],
                             rh=wr_s[:, d, :], s=(d == 0), z=False:
                             e.matmul(o, lh, rh, start=s, stop=z))
                    p.op("tensor", "s_rt",
                         lambda e, o=pb[2][:, col:col + NE],
                         lh=ones[0:1, 0:128], rh=br_s[0:1, :]:
                         e.matmul(o, lh, rh, start=False, stop=True))
            p.set_guard(("bank", 6), tagt, p.c(tagt))
            p.set_guard(("bank", 7), tagt, p.c(tagt))
            p.set_guard(("bank", 4), tagr, p.c(tagr))
            p.set_guard(("bank", 5), tagr, p.c(tagr))

    p.mark("P5ln1")
    n_pr = p.c("s_pr")
    for eng in ("tensor", "scalar", "vector"):
        p.wait(eng, "s_pr", n_pr)
    if debug:
        p.wait("scalar", "s_pr", n_pr)
        dump_fm("d_pre", xv, DCH, TT)
        p.wcur("gpsimd", "s_wm")
        dump_bf("d_on", qv, 16, TT)
        p.wcur("vector", "s_dbg")
    p.wcur("tensor", "s_rw")
    p.guard("tensor", ("bank", 2))
    ln_pass(is_ln1=True)
    n_rt = p.c("s_rt")
    p.wait("vector", "s_rt", n_rt)
    mxv_ = mxs[:, 0:8]
    esl = mxs[:, 8:40]
    sms = mxs[:, 40:48]
    inv = mxs[:, 48:56]
    mxe = mxs[:, 56:64]
    tpw = mxs[:, 64:72]
    msc = mxs[:, 72:104]
    for ti in range(8):
        p.op("vector", None,
             lambda e, o=mxv_[:, ti:ti + 1],
             i=pb[2][:, ti * NE:(ti + 1) * NE]:
             e.reduce_max(o, i, axis=AX.X))
    for ti in range(8):
        p.op("vector", "s_sm" if ti == 7 else None,
             lambda e, o=esl[:, ti * 4:(ti + 1) * 4],
             i=pb[2][:, ti * NE:(ti + 1) * NE], m=mxv_[:, ti:ti + 1]:
             e.tensor_scalar_sub(o, i, m))
    sm0 = p.c("s_sm")
    p.set_guard(("bank", 2), "s_sm", sm0)
    p.wait("scalar", "s_sm", sm0)
    se = p.op("scalar", "s_sme",
              lambda e, o=esl[:, :], i=esl[:, :]: e.activation(o, i, AF.Exp))
    p.wait("vector", "s_sme", se)
    for ti in range(8):
        p.op("vector", None,
             lambda e, o=sms[:, ti:ti + 1], i=esl[:, ti * 4:(ti + 1) * 4]:
             e.reduce_sum(o, i, axis=AX.X))
    for ti in range(8):
        p.op("vector", None,
             lambda e, o=mxe[:, ti:ti + 1], i=esl[:, ti * 4:(ti + 1) * 4]:
             e.reduce_max(o, i, axis=AX.X))
    for ti in range(8):
        p.op("vector", None,
             lambda e, o=inv[:, ti:ti + 1], i=sms[:, ti:ti + 1]:
             e.reciprocal(o, i))
    for ti in range(8):
        p.op("vector", None,
             lambda e, o=tpw[:, ti:ti + 1], a=mxe[:, ti:ti + 1],
             b_=inv[:, ti:ti + 1]: e.tensor_mul(o, a, b_))
    for ti in range(8):
        p.op("vector", None,
             lambda e, o=msc[:, ti * 4:(ti + 1) * 4],
             i=esl[:, ti * 4:(ti + 1) * 4], m=mxe[:, ti:ti + 1]:
             e.tensor_single_scalar(o, i, m, ALU.is_ge))
    for ti in range(8):
        p.op("vector", "s_sm" if ti == 7 else None,
             lambda e, o=msc[:, ti * 4:(ti + 1) * 4],
             i=msc[:, ti * 4:(ti + 1) * 4], m=tpw[:, ti:ti + 1]:
             e.tensor_scalar_mul(o, i, m))
    smn = p.c("s_sm")
    p.wait("tensor", "s_sm", smn)
    p.guard("tensor", ("bank", 3))
    p.guard("tensor", ("bank", 4))
    for ti in range(8):
        bi = 3 if ti < 4 else 4
        p.op("tensor", "s_mt" if ti % 4 == 3 else None,
             lambda e, o=pb[bi][0:4, (ti % 4) * 128:(ti % 4 + 1) * 128],
             i=msc[:, ti * 4:(ti + 1) * 4]:
             e.transpose(o, i, identf[:, :]))
    nmt = p.c("s_mt")
    p.wait("vector", "s_mt", nmt)
    p.op("vector", None,
         lambda e, o=maskfm[0:4, 0:512], a=pb[3][0:4, 0:512]:
         e.tensor_copy(o, a))
    mf = p.op("vector", "s_mf",
              lambda e, o=maskfm[0:4, 512:1024], a=pb[4][0:4, 0:512]:
              e.tensor_copy(o, a))
    p.set_guard(("bank", 3), "s_mf", mf)
    p.set_guard(("bank", 4), "s_mf", mf)

    if debug:
        p.wcur("scalar", "s_l1t")
        dump_fm("d_y1", xv, DCH, TT)
        p.wcur("scalar", "s_sm")
        p.dma("scalar", "s_dbg", dbg_dram("d_msc", [128, 112])[:, :],
              mxs[:, :]) if debug else None
        p.wcur("vector", "s_dbg")

    p.mark("P6ffn")
    n_l1b = p.c("s_l1b")
    p.wait("tensor", "s_l1b", n_l1b)
    p.wait("vector", "s_mf", p.c("s_mf"))
    p.wcur("gpsimd", "s_wm")
    p.wcur("gpsimd", "s_pm")
    p.wcur("gpsimd", "s_om")
    p.wcur("gpsimd", "s_ds")
    p.wcur("gpsimd", "s_vte")
    ldwg = {}
    ldwu = {}
    for e_ in range(NE):
        for half in range(2):
            p.guard("tensor", ("bank", 7))
            p.wcur("tensor", "s_mf")
            mr = p.op("tensor", "s_mr",
                      lambda e, o=pb[7][:, 0:HFW],
                      lh=sel4[0:NE, e_ * 128:(e_ + 1) * 128],
                      rh=maskfm[0:NE, half * HFW:(half + 1) * HFW]:
                      e.matmul(o, lh, rh, start=True, stop=True))
            p.wait("vector", "s_mr", mr)
            mre = p.op("vector", "s_mre",
                       lambda e, o=mask_rep[:, half * HFW:(half + 1) * HFW],
                       a=pb[7][:, 0:HFW]:
                       e.tensor_copy(o, a))
            p.set_guard(("bank", 7), "s_mre", mre)
        for blk in range(NBLK):
            bpar = blk % 2
            for f2 in range(FBLK):
                fi2 = blk * FBLK + f2
                gi = e_ * 64 + fi2
                tpar = fi2 % 2
                p.wait("gpsimd", "s_gu", 2 * (gi - 1))
                ldwg[fi2] = p.dma(
                    "gpsimd", "s_fw", wgtile[:, tpar, :, :],
                    wgl_c[e_][:, :, fi2 * 128:(fi2 + 1) * 128])
                ldwu[fi2] = p.dma(
                    "gpsimd", "s_fw", wutile[:, tpar, :, :],
                    wul_c[e_][:, :, fi2 * 128:(fi2 + 1) * 128])
            p.guard("gpsimd", ("wd",))
            ldwd = p.dma("gpsimd", "s_fw", wdtile[:, :, :],
                         wdl_r[e_][:, blk * FBLK:(blk + 1) * FBLK, :])
            p.guard("vector", ("h", bpar))
            for f in range(FBLK):
                fi = blk * FBLK + f
                tpar = fi % 2
                fo = 0
                p.wait("tensor", "s_fw", ldwu[fi])
                for half in range(2):
                    bgi, bui = half, 2 + half
                    p.guard("tensor", ("bank", bgi))
                    p.guard("tensor", ("bank", bui))
                    for d in range(DCH):
                        last = d == DCH - 1
                        p.op("tensor", None,
                             lambda e, o=pb[bgi][:, 0:HFW],
                             lh=wgtile[:, tpar, d, fo:fo + 128],
                             rh=y1bf[:, d, half * HFW:(half + 1) * HFW],
                             s=(d == 0), z=last:
                             e.matmul(o, lh, rh, start=s, stop=z))
                    for d in range(DCH):
                        last = d == DCH - 1
                        p.op("tensor", "s_gu" if last else None,
                             lambda e, o=pb[bui][:, 0:HFW],
                             lh=wutile[:, tpar, d, fo:fo + 128],
                             rh=y1bf[:, d, half * HFW:(half + 1) * HFW],
                             s=(d == 0), z=last:
                             e.matmul(o, lh, rh, start=s, stop=z))
                    gu = p.c("s_gu")
                    sgsl = sgscr[:, ((f % 2) * 2 + half) * 512:
                                 ((f % 2) * 2 + half + 1) * 512]
                    p.wait("scalar", "s_gu", gu)
                    p.guard("scalar", ("sg", (f % 2) * 2 + half))
                    hh = p.op("scalar", "s_hh",
                              lambda e, o=sgsl, a=pb[bgi][:, 0:HFW]:
                              e.activation(o, a, AF.Silu))
                    p.set_guard(("bank", bgi), "s_hh", hh)
                    p.wait("vector", "s_hh", hh)
                    p.op("vector", None,
                         lambda e, o=sgsl, i=sgsl,
                         m=mask_rep[:, half * HFW:(half + 1) * HFW]:
                         e.tensor_mul(o, i, m))
                    hv = p.op("vector", "s_hv",
                              lambda e,
                              o=hbuf[:, bpar * FBLK + f,
                                     half * HFW:(half + 1) * HFW],
                              i=sgsl, a=pb[bui][:, 0:HFW]:
                              e.tensor_mul(o, i, a))
                    p.set_guard(("bank", bui), "s_hv", hv)
                    p.set_guard(("sg", (f % 2) * 2 + half), "s_hv", hv)
            nhv = p.c("s_hv")
            p.wait("tensor", "s_hv", nhv)
            p.wait("tensor", "s_fw", ldwd)
            for d in range(DCH):
                for half in range(2):
                    bi = 4 + (d * 2 + half) % 2
                    bank = pb[bi]
                    p.guard("tensor", ("bank", bi))
                    for f in range(FBLK):
                        last = f == FBLK - 1
                        p.op("tensor", "s_dn" if last else None,
                             lambda e, o=bank[:, 0:HFW],
                             lh=wdtile[:, f, d * 128:(d + 1) * 128],
                             rh=hbuf[:, bpar * FBLK + f,
                                     half * HFW:(half + 1) * HFW],
                             s=(f == 0), z=last:
                             e.matmul(o, lh, rh, start=s, stop=z))
                    dn = p.c("s_dn")
                    p.wait("vector", "s_dn", dn)
                    da = p.op("vector", "s_da",
                              lambda e,
                              o=xvr[:, d, half * HFW:(half + 1) * HFW],
                              i=xv[:, d, half * HFW:(half + 1) * HFW],
                              a=bank[:, 0:HFW]: e.tensor_add(o, i, a))
                    p.set_guard(("bank", bi), "s_da", da)
            p.set_guard(("wd",), "s_dn", p.c("s_dn"))
            p.set_guard(("h", bpar), "s_dn", p.c("s_dn"))

    p.mark("P7ln2")
    n_da = p.c("s_da")
    for eng in ("tensor", "scalar", "vector"):
        p.wait(eng, "s_da", n_da)
    if debug:
        p.wait("scalar", "s_da", n_da)
        dump_fm("d_y2a", xv, DCH, TT)
        p.wcur("vector", "s_dbg")
    p.wcur("scalar", "s_hv")
    p.wcur("vector", "s_hv")
    ln_pass(is_ln1=False)

    p.mark("P8out")
    n_l2t = p.c("s_l2t")
    p.wait("tensor", "s_l2t", n_l2t)
    p.wcur("vector", "s_mre")
    for ti in range(8):
        p.guard("vector", ("ytm",))
        for k4 in range(4):
            bi = k4
            p.guard("tensor", ("bank", bi))
            for dd in range(4):
                d = k4 * 4 + dd
                p.op("tensor", "s_tr" if dd == 3 else None,
                     lambda e, o=pb[bi][:, dd * 128:(dd + 1) * 128],
                     i=xv[:, d, ti * 128:(ti + 1) * 128]:
                     e.transpose(o, i, identf[:, :]))
            ntr = p.c("s_tr")
            p.wait("vector", "s_tr", ntr)
            te = p.op("vector", "s_te",
                      lambda e, o=ytm[:, k4 * 512:(k4 + 1) * 512],
                      a=pb[bi][:, 0:HFW]: e.tensor_copy(o, a))
            p.set_guard(("bank", bi), "s_te", te)
        nte = p.c("s_te")
        p.wait("sync", "s_te", nte)
        od = p.dma("sync", "s_out", yout[ti * 128:(ti + 1) * 128, :],
                   ytm[:, :])
        p.set_guard(("ytm",), "s_out", od)
    p.wait("sync", "s_out", p.c("s_out"))
    if debug:
        p.wcur("sync", "s_dbg")
        p.wcur("scalar", "s_dbg")
        p.wcur("gpsimd", "s_dbg")

    with nc.Block() as block:
        @block.sync
        def _(e):
            p.replay("sync", e)

        @block.scalar
        def _(e):
            p.replay("scalar", e)

        @block.vector
        def _(e):
            p.replay("vector", e)

        @block.tensor
        def _(e):
            p.replay("tensor", e)

        @block.gpsimd
        def _(e):
            p.replay("gpsimd", e)
    return nc, st


def kernel(**inputs):
    f32 = np.float32

    def g(name):
        return np.ascontiguousarray(np.asarray(inputs[name], f32))

    x = g("L0_attn_hidden_states")
    shared = {
        "gnwd": g("L0_attn_g_norm_w").reshape(128, 1),
        "bqt": np.ascontiguousarray(g("L0_attn_bq").reshape(16, 128).T),
        "bkt": np.ascontiguousarray(g("L0_attn_bk").reshape(8, 128).T),
        "bvt": np.ascontiguousarray(g("L0_attn_bv").reshape(8, 128).T),
        "bg2t": np.ascontiguousarray(g("L0_attn_bg2").reshape(8, 128).T),
        "onesd": np.ones((1, 128), f32),
        "identd": np.eye(128, dtype=f32),
        "amaskd": np.triu(np.ones((128, 128), f32)),
        "wq": g("L0_attn_Wq"),
        "wk": g("L0_attn_Wk"),
        "wv": g("L0_attn_Wv"),
        "wg1d": np.ascontiguousarray(
            g("L0_attn_Wg1").reshape(DCH, 128, GATE).transpose(1, 0, 2)),
        "wg2d": g("L0_attn_Wg2"),
        "wo": g("L0_attn_Wo"),
        "wrd": np.ascontiguousarray(
            g("L0_ffn_Wr").reshape(DCH, 128, NE).transpose(1, 0, 2)),
        "brd": g("L0_ffn_br").reshape(1, NE),
        "seld": np.repeat(np.eye(NE, dtype=f32), 128, axis=1).reshape(
            NE, NE * 128),
        "wgl": g("L0_ffn_Wg"),
        "wul": g("L0_ffn_Wu"),
        "wdl": g("L0_ffn_Wd"),
    }
    in_maps = []
    for c in range(8):
        b, hf = divmod(c, 2)
        m = dict(shared)
        m["xt"] = np.ascontiguousarray(x[b, hf * TT:(hf + 1) * TT, :].T)
        m["xpt"] = np.ascontiguousarray(x[b, 0:TT, :].T)
        m["flagd"] = np.full((128, 1), float(hf), f32)
        in_maps.append(m)
    debug = bool(inputs.pop("_debug", False))
    nc, st = build_program(debug=debug)
    res = run_bass_kernel_spmd(nc, in_maps, core_ids=list(range(8)))
    if debug:
        kernel._dbg = res.results
    st.close()
    out = np.empty((B, L, D), f32)
    for c in range(8):
        b, hf = divmod(c, 2)
        out[b, hf * TT:(hf + 1) * TT, :] = res.results[c]["y"]
    return out


# version 07
# speedup vs baseline: 1.0414x; 1.0414x over previous
import contextlib
import numpy as np
import concourse.bass as bass
import concourse.mybir as mybir
from concourse.bass_utils import run_bass_kernel_spmd

F32 = mybir.dt.float32
F32R = mybir.dt.float32r
BF16 = mybir.dt.bfloat16
AF = mybir.ActivationFunctionType
ALU = mybir.AluOpType
AX = mybir.AxisListType

B, L, D = 4, 2048, 2048
TT = 1024
HFW = 512
CH = 128
NCH = TT // CH
KVH, QHN, HD = 8, 16, 128
DCH = 16
GATE = 16
FF = 8192
NE = 4
FBLK = 4
NBLK = FF // 128 // FBLK
QSCALE = float(HD) ** -0.5
EPS = 1e-5

ENGS = ("sync", "scalar", "vector", "tensor", "gpsimd")


class P:

    def __init__(self, nc, stack):
        self.nc = nc
        self.ops = {e: [] for e in ENGS}
        self.sems = {}
        self.cnt = {}
        self.stack = stack
        self.guards = {}

    def sem(self, name):
        if name not in self.sems:
            self.sems[name] = self.stack.enter_context(self.nc.semaphore(name))
            self.cnt[name] = 0
        return self.sems[name]

    def c(self, name):
        self.sem(name)
        return self.cnt[name]

    def wait(self, eng, name, val):
        if val is None or val <= 0:
            return
        h = self.sem(name)
        self.ops[eng].append(lambda e, h=h, v=val: e.wait_ge(h, v))

    def wcur(self, eng, name):
        self.wait(eng, name, self.c(name))

    def guard(self, eng, key):
        g = self.guards.get(key)
        if g is None:
            return
        for (nm, v) in (g if isinstance(g, list) else [g]):
            self.wait(eng, nm, v)

    def set_guard(self, key, name, val):
        self.guards[key] = (name, val)

    def dma(self, eng, name, out, in_, **kw):
        h = self.sem(name)
        self.cnt[name] += 16
        self.ops[eng].append(
            lambda e, o=out, i=in_, h=h, kw=kw:
            e.dma_start(o, i, **kw).then_inc(h, 16))
        return self.cnt[name]

    def op(self, eng, name, fn):
        if name is None:
            self.ops[eng].append(fn)
            return None
        h = self.sem(name)
        self.cnt[name] += 1
        self.ops[eng].append(lambda e, fn=fn, h=h: fn(e).then_inc(h, 1))
        return self.cnt[name]

    def mark(self, phase):
        if not hasattr(self, "marks"):
            self.marks = []
        self.marks.append((phase, {e: len(self.ops[e]) for e in ENGS}))

    def replay(self, eng, e):
        bounds = []
        for (ph, idx) in getattr(self, "marks", []):
            bounds.append((idx[eng], ph))
        cur = None
        for i, f in enumerate(self.ops[eng]):
            for (bi, ph) in bounds:
                if bi == i:
                    if cur is not None:
                        self.nc.leave_named_scope(cur[0], cur[1], False)
                    sid, _ = self.nc.enter_named_scope(ph + "_" + eng, False)
                    cur = (ph + "_" + eng, sid)
            f(e)
        if cur is not None:
            self.nc.leave_named_scope(cur[0], cur[1], False)


def r32(ap):
    return ap.bitcast(F32R)


def build_program(debug=False):
    nc = bass.Bass("TRN2", target_bir_lowering=False)
    st = contextlib.ExitStack()

    def din(name, shape):
        return nc.dram_tensor(name, shape, F32, kind="ExternalInput")

    xt = din("xt", [D, TT])
    xpt = din("xpt", [D, TT])
    flagd = din("flagd", [128, 1])
    gnwd = din("gnwd", [128, 1])
    bqt = din("bqt", [128, 16])
    bkt = din("bkt", [128, 8])
    bvt = din("bvt", [128, 8])
    bg2t = din("bg2t", [128, 8])
    onesd = din("onesd", [1, 128])
    identd = din("identd", [128, 128])
    amaskd = din("amaskd", [128, 128])
    wq = din("wq", [D, D])
    wk = din("wk", [D, 1024])
    wv = din("wv", [D, 1024])
    wg1d = din("wg1d", [128, DCH, GATE])
    wg2d = din("wg2d", [GATE, 1024])
    wo = din("wo", [D, D])
    wrd = din("wrd", [128, DCH, NE])
    brd = din("brd", [1, NE])
    seld = din("seld", [NE, NE * 128])
    wgl = din("wgl", [NE, D, FF])
    wul = din("wul", [NE, D, FF])
    wdl = din("wdl", [NE, FF, D])
    yout = nc.dram_tensor("y", [TT, D], F32, kind="ExternalOutput")
    dbg_t = {}

    def dbg_dram(name, shape):
        if name not in dbg_t:
            dbg_t[name] = nc.dram_tensor(name, shape, F32,
                                         kind="ExternalOutput")
        return dbg_t[name]

    xt_r = xt.rearrange("(d p) t -> p d t", p=128)
    xpt_r = xpt.rearrange("(d p) t -> p d t", p=128)
    wq_c = wq.rearrange("(d p) m -> p d m", p=128)
    wk_c = wk.rearrange("(d p) m -> p d m", p=128)
    wv_c = wv.rearrange("(d p) m -> p d m", p=128)
    wo_r = wo.rearrange("(h p) m -> p h m", p=128)
    wgl_c = [wgl[e].rearrange("(d p) m -> p d m", p=128) for e in range(NE)]
    wul_c = [wul[e].rearrange("(d p) m -> p d m", p=128) for e in range(NE)]
    wdl_r = [wdl[e].rearrange("(c p) m -> p c m", p=128) for e in range(NE)]

    def sb(name, shape, dt):
        return st.enter_context(nc.sbuf_tensor(name, shape, dt))

    ones = sb("ones", [1, 128], F32)
    onescol = sb("onescol", [128, 1], F32)
    onescol_b = sb("onescol_b", [128, 1], BF16)
    identf = sb("identf", [128, 128], F32)
    zeros = sb("zeros", [128, 128], F32)
    bq_s = sb("bq_s", [128, 16], F32)
    bk_s = sb("bk_s", [128, 8], F32)
    bv_s = sb("bv_s", [128, 8], F32)
    bg2p = sb("bg2p", [128, 8], F32)
    bg2n = sb("bg2n", [128, 8], F32)
    gnws = sb("gnws", [128, 1], F32)
    flags = sb("flags", [128, 1], F32)
    epsc = sb("epsc", [128, 1], F32)
    wr_s = sb("wr_s", [128, DCH, NE], F32)
    br_s = sb("br_s", [1, NE], F32)
    sel4 = sb("sel4", [NE, NE * 128], F32)
    wg1_s = sb("wg1_s", [128, DCH, GATE], F32R)
    small_s = sb("small_s", [GATE, 2048], F32)
    xg1s = small_s[:, 0:1024]
    wg2_s = small_s[:, 1024:2048]
    rows3 = small_s[0:1, 0:1024]
    rows5 = small_s[0:1, 0:1024]
    mxs = sb("mxs", [128, 112], F32)
    maskfm = small_s[0:4, 1024:2048]

    ar1 = sb("ar1", [128, DCH, TT], F32)
    ar2 = sb("ar2", [128, 16, TT], BF16)
    ar3 = sb("ar3", [128, 8, TT], BF16)
    ar4 = sb("ar4", [128, 8, TT], BF16)
    ar5 = sb("ar5", [128, 8, TT], BF16)
    ar6 = sb("ar6", [128, 4096], F32)
    ar7 = sb("ar7", [128, 4096], F32R)
    ar8 = sb("ar8", [128, 3088], F32)
    s_t = sb("s_t", [128, KVH, HD], F32)
    s_bf = sb("s_bf", [128, KVH, HD], BF16)

    xv = ar1
    qv = ar2
    y1bf = ar2
    kv = ar3
    hbuf = ar3
    vv = ar4
    wdtile = ar4.rearrange("p h t -> p (h t)").rearrange(
        "p (c m) -> p c m", c=FBLK)
    vfm = ar5
    gkv = ar5
    wo4 = ar5.rearrange("p h t -> p (h t)").rearrange(
        "p (s m) -> p s m", s=4)
    ar6b = ar6.bitcast(BF16)
    wgtile = ar6b[:, 0:4096].rearrange("p (b d m) -> p b d m", b=2, d=DCH)
    wutile = ar6b[:, 4096:8192].rearrange("p (b d m) -> p b d m", b=2, d=DCH)
    wstrr = ar7.rearrange("p (b d m) -> p b d m", b=2, d=DCH)
    xvr = ar1.bitcast(F32R)

    def onv(qh):
        return qv[:, qh, :]

    gbw = ar8
    gbb = ar8.bitcast(BF16)
    b8 = gbw[:, 0:1024].rearrange("p (h c) -> p h c", h=8)
    am8 = gbb[:, 0:2048]
    of8 = gbb[:, 0:2048]
    expb8 = gbw[:, 1024:2048].rearrange("p (h c) -> p h c", h=8)
    sq8 = gbb[:, 2048:4096]
    expnb8 = gbw[:, 2048:3072].rearrange("p (h c) -> p h c", h=8)
    qs8 = gbb[:, 4096:6144]
    ebl8 = gbw[:, 3072:3088]
    a6b = ar6.bitcast(BF16)
    ks8 = a6b[:, 0:1024].rearrange("p (h c) -> p h c", h=8)
    ksT8 = a6b[:, 1024:2048].rearrange("p (h c) -> p h c", h=8)
    amask = ar6[:, 2048:2176]
    identb = a6b[:, 4352:4480]

    gkch = [ar8[:, 0:1024], ar8[:, 1024:2048]]
    y1scr = ar8[:, 0:512]
    sqscr = [ar8[:, 1024:1536], ar8[:, 1536:2048]]
    mask_rep = ar8[:, 0:1024]
    sgscr = ar8.bitcast(BF16)[:, 2048:4096]
    ytm = ar8[:, 0:2048]

    pb = [st.enter_context(nc.psum_tensor(f"pb{i}", [128, HFW], F32))
          for i in range(8)]

    p = P(nc, st)

    for (dst, src) in [(ones, onesd), (identf, identd), (amask, amaskd),
                       (bq_s, bqt), (bk_s, bkt), (bv_s, bvt), (bg2p, bg2t),
                       (gnws, gnwd), (flags, flagd), (wr_s, wrd),
                       (br_s, brd), (sel4, seld),
                       (wg2_s, wg2d)]:
        nd = len(dst.shape)
        p.dma("sync", "s_c", dst[(slice(None),) * nd], src[(slice(None),) * nd])
    n_c = p.c("s_c")
    p.dma("gpsimd", "s_cb", identb[:, :], identd[:, :])
    p.dma("gpsimd", "s_cb", wg1_s[:, :, :], wg1d[:, :, :])
    for d in range(DCH):
        p.dma("gpsimd", "s_x", xvr[:, d, :], xpt_r[:, d, :])
    n_xp = p.c("s_x")
    p.wait("vector", "s_c", n_c)
    p.op("vector", "s_c0", lambda e: e.memset(zeros[:, :], 0.0))
    p.op("vector", "s_c0", lambda e: e.memset(onescol[:, :], 1.0))
    p.op("vector", "s_c0", lambda e: e.memset(onescol_b[:, :], 1.0))
    p.op("vector", "s_c0", lambda e: e.memset(epsc[:, :], EPS))
    p.op("vector", "s_c0",
         lambda e: e.memset(s_t.rearrange("p h t -> p (h t)"), 0.0))
    p.op("vector", "s_c0",
         lambda e: e.tensor_scalar_mul(bq_s[:, :], bq_s[:, :], QSCALE))
    p.op("vector", "s_c0",
         lambda e: e.tensor_scalar_mul(bg2n[:, :], bg2p[:, :], -1.0))
    n_c0 = p.c("s_c0")
    for eng in ("tensor", "scalar", "gpsimd"):
        p.wait(eng, "s_c0", n_c0)

    wu = {"u": 0, "mm": {}}

    def proj_matmuls(w_c, j, xload, dst, func, bias, scale):
        u = wu["u"]
        wu["u"] += 1
        buf = u % 2
        if u - 2 in wu["mm"]:
            p.wait("gpsimd", "s_pm", wu["mm"][u - 2])
        ld = p.dma("gpsimd", "s_wa", wstrr[:, buf, :, 0:128],
                   w_c[:, :, j * 128:(j + 1) * 128])
        p.wait("tensor", "s_wa", ld)
        p.wait("tensor", "s_x", xload)
        bis = [0 + (u % 2) * 2, 1 + (u % 2) * 2]
        for bi in bis:
            p.guard("tensor", ("bank", bi))
        for half, bi in ((0, bis[0]), (1, bis[1])):
            bank = pb[bi]
            for d in range(DCH):
                last = d == DCH - 1
                p.op("tensor", "s_pm" if last else None,
                     lambda e, bank=bank, half=half, buf=buf, d=d,
                     s=(d == 0), z=last:
                     e.matmul(bank[:, 0:HFW], wstrr[:, buf, d, :],
                              xvr[:, d, half * HFW:(half + 1) * HFW],
                              start=s, stop=z))
        wu["mm"][u] = p.c("s_pm")
        for half, bi in ((0, bis[0]), (1, bis[1])):
            bank = pb[bi]
            p.wcur("scalar", "s_pm")
            ev = p.op("scalar", "s_pe",
                      lambda e, a=bank[:, 0:HFW],
                      o=dst[:, j, half * HFW:(half + 1) * HFW],
                      f=func, sc=scale, bi_=bias[:, j:j + 1]:
                      e.activation(o, a, f, scale=sc, bias=bi_))
            p.set_guard(("bank", bi), "s_pe", ev)

    def proj_pass(xload, do_q):
        for j in range(KVH):
            proj_matmuls(wk_c, j, xload, kv, AF.Relu, bk_s, 1.0)
        for j in range(KVH):
            proj_matmuls(wv_c, j, xload, vfm, AF.Identity, bv_s, 1.0)
        p.wcur("tensor", "s_pe")
        p.wait("tensor", "s_cb", 16)
        for tc in range(NCH):
            bi = 4 + tc % 2
            bank = pb[bi]
            p.guard("tensor", ("bank", bi))
            bankb = bank.bitcast(BF16)
            for h in range(KVH):
                p.op("tensor", "s_vt" if h == KVH - 1 else None,
                     lambda e, o=bankb[:, h * 128:(h + 1) * 128],
                     i=vfm[:, h, tc * 128:(tc + 1) * 128]:
                     e.transpose(o, i, identb[:, :]))
            p.wcur("vector", "s_vt")
            ev = p.op("vector", "s_vte",
                      lambda e, o=vv[:, tc, :], a=bankb[:, 0:1024]:
                      e.tensor_copy(o, a))
            p.set_guard(("bank", bi), "s_vte", ev)
        p.wcur("tensor", "s_vte")
        for half in range(2):
            p.guard("tensor", ("bank", 6))
            for d in range(DCH):
                last = d == DCH - 1
                p.op("tensor", "s_pm" if last else None,
                     lambda e, d=d, half=half, s=(d == 0), z=last:
                     e.matmul(pb[6][0:GATE, 0:HFW], wg1_s[:, d, :],
                              xvr[:, d, half * HFW:(half + 1) * HFW],
                              start=s, stop=z))
            p.wcur("vector", "s_pm")
            p.guard("vector", ("xg1s", half))
            ev = p.op("vector", "s_g1",
                      lambda e, half=half:
                      e.tensor_copy(xg1s[0:GATE, half * HFW:(half + 1) * HFW],
                                    pb[6][0:GATE, 0:HFW]))
            p.set_guard(("bank", 6), "s_g1", ev)
        for j in range(KVH):
            for half in range(2):
                par = (j * 2 + half) % 2
                bi = 5 + 2 * par
                bank = pb[bi]
                p.guard("tensor", ("bank", bi))
                p.wcur("tensor", "s_g1")
                gm = p.op("tensor", "s_gm",
                          lambda e, j=j, half=half, bank=bank:
                          e.matmul(bank[:, 0:HFW],
                                   wg2_s[0:GATE, j * 128:(j + 1) * 128],
                                   xg1s[0:GATE,
                                        half * HFW:(half + 1) * HFW],
                                   start=True, stop=True))
                p.set_guard(("xg1s", half), "s_gm", gm)
                sa = gkch[par][:, 0:512]
                sbb = gkch[par][:, 512:1024]
                p.wait("scalar", "s_gm", gm)
                p.guard("scalar", ("gkch", par))
                p.op("scalar", None,
                     lambda e, a=bank[:, 0:HFW], o=sa, j=j:
                     e.activation(o, a, AF.Relu, scale=-1.0,
                                  bias=bg2n[:, j:j + 1]))
                ab = p.op("scalar", "s_gk",
                          lambda e, a=bank[:, 0:HFW], o=sbb, j=j:
                          e.activation(o, a, AF.Abs, bias=bg2p[:, j:j + 1]))
                p.set_guard(("bank", bi), "s_gk", ab)
                p.op("scalar", None,
                     lambda e, o=sbb: e.activation(o, o, AF.Exp, scale=-1.0))
                gk2 = p.op("scalar", "s_gk",
                           lambda e, o=sbb:
                           e.activation(o, o, AF.Ln, bias=1.0))
                p.wait("vector", "s_gk", gk2)
                p.op("vector", None,
                     lambda e, a=sa, b_=sbb: e.tensor_add(a, a, b_))
                gv_ = p.op("vector", "s_gv",
                           lambda e, a=sa,
                           o=gkv[:, j, half * HFW:(half + 1) * HFW]:
                           e.tensor_scalar_mul(o, a, -1.0 / 16.0))
                p.set_guard(("gkch", par), "s_gv", gv_)
        if do_q:
            for j in range(QHN):
                proj_matmuls(wq_c, j, xload, qv, AF.Relu, bq_s, QSCALE)

    def scan_chunk_b(c, full):
        cs0, cs1 = c * CH, (c + 1) * CH
        p.guard("vector", ("gR1w",))
        for h in range(KVH):
            p.op("vector", "s_sc" if h == KVH - 1 else None,
                 lambda e, o=b8[:, h, :], i=gkv[:, h, cs0:cs1]:
                 e.tensor_tensor_scan(o, i, zeros[:, :], 0.0,
                                      ALU.add, ALU.add))
        nsc = p.c("s_sc")
        p.wait("scalar", "s_sc", nsc)
        p.guard("scalar", ("gR2w",))
        p.guard("scalar", ("gR3w",))
        p.guard("scalar", ("gEblw",))
        if full:
            p.op("scalar", None,
                 lambda e, o=expb8.rearrange("p h c -> p (h c)"),
                 i=b8.rearrange("p h c -> p (h c)"):
                 e.activation(o, i, AF.Exp))
        p.op("scalar", None,
             lambda e, o=expnb8.rearrange("p h c -> p (h c)"),
             i=b8.rearrange("p h c -> p (h c)"):
             e.activation(o, i, AF.Exp, scale=-1.0))
        ex = p.op("scalar", "s_ex",
                  lambda e, o=ebl8[:, 0:8], i=b8[:, :, CH - 1]:
                  e.activation(o, i, AF.Exp))
        p.guards[("gR1w",)] = [("s_ex", ex)]
        p.wait("vector", "s_ex", ex)
        p.guard("vector", ("gKSw",))
        p.op("vector", "s_qk",
             lambda e, o=ks8[:, :, :], a=kv[:, :, cs0:cs1],
             b_=expnb8[:, :, :]: e.tensor_mul(o, a, b_))
        if full:
            p.op("vector", "s_qk",
                 lambda e,
                 o=qs8.rearrange("p (h q c) -> p h q c", h=8, q=2),
                 a=qv[:, :, cs0:cs1].rearrange("p (h q) c -> p h q c", q=2),
                 b_=expb8[:, :, :].unsqueeze(2).broadcast_to(
                     [128, 8, 2, 128]):
                 e.tensor_mul(o, a, b_))
        nqk = p.c("s_qk")
        p.wait("tensor", "s_qk", nqk)
        p.guard("tensor", ("bank", 0))
        for h in range(KVH):
            p.op("tensor", "s_kt" if h == KVH - 1 else None,
                 lambda e, o=pb[0].bitcast(BF16)[:, h * 128:(h + 1) * 128],
                 i=ks8[:, h, :]: e.transpose(o, i, identb[:, :]))
        nkt = p.c("s_kt")
        p.wait("vector", "s_kt", nkt)
        p.guard("vector", ("gKTw",))
        kte = p.op("vector", "s_kte",
                   lambda e, o=ksT8.rearrange("p h c -> p (h c)"),
                   a=pb[0].bitcast(BF16)[:, 0:1024]: e.tensor_copy(o, a))
        p.set_guard(("bank", 0), "s_kte", kte)
        if full:
            p.wait("tensor", "s_sbf", 1 + c if c >= 1 else 1)
            for pp in range(4):
                biA = 1 + pp % 2
                p.guard("tensor", ("bank", biA))
                for hh in range(2):
                    h = pp * 2 + hh
                    p.op("tensor", "s_am" if hh == 1 else None,
                         lambda e, o=pb[biA][:, hh * 256:(hh + 1) * 256],
                         lh=ks8[:, h, :],
                         rh=qs8[:, h * 256:(h + 1) * 256]:
                         e.matmul(o, lh, rh, start=True, stop=True))
                nam = p.c("s_am")
                p.wait("vector", "s_am", nam)
                ame = p.op("vector", "s_ame",
                           lambda e,
                           o=am8[:, pp * 512:(pp + 1) * 512].rearrange(
                               "p (a c) -> p a c", a=4),
                           a=pb[biA][:, 0:512].rearrange(
                               "p (a c) -> p a c", a=4),
                           m=amask[:, :].unsqueeze(1).broadcast_to(
                               [128, 4, 128]):
                           e.tensor_mul(o, a, m))
                biO = 3 + pp % 2
                p.guard("tensor", ("bank", biO))
                p.wait("tensor", "s_ame", ame)
                for hh in range(2):
                    h = pp * 2 + hh
                    p.op("tensor", None,
                         lambda e, o=pb[biO][:, hh * 256:(hh + 1) * 256],
                         lh=vv[:, c, h * 128:(h + 1) * 128],
                         rh=am8[:, h * 256:(h + 1) * 256]:
                         e.matmul(o, lh, rh, start=True, stop=False))
                    p.op("tensor", "s_om" if hh == 1 else None,
                         lambda e, o=pb[biO][:, hh * 256:(hh + 1) * 256],
                         lh=s_bf[:, h, :],
                         rh=qs8[:, h * 256:(h + 1) * 256]:
                         e.matmul(o, lh, rh, start=False, stop=True))
                nom = p.c("s_om")
                p.wait("scalar", "s_om", nom)
                p.op("scalar", None,
                     lambda e, o=of8[:, pp * 512:(pp + 1) * 512],
                     a=pb[biO][:, 0:512]: e.activation(o, a, AF.Copy))
                oev = p.op("scalar", "s_oev",
                           lambda e, o=sq8[:, pp * 512:(pp + 1) * 512],
                           a=pb[biO][:, 0:512]:
                           e.activation(o, a, AF.Square))
                p.set_guard(("bank", biO), "s_oev", oev)
                p.wait("tensor", "s_oev", oev)
                p.guard("tensor", ("rowb",))
                rc = p.op("tensor", "s_rc",
                          lambda e, o=pb[7][0:1, 0:512],
                          lh=onescol_b[:, :],
                          rh=sq8[:, pp * 512:(pp + 1) * 512]:
                          e.matmul(o, lh, rh, start=True, stop=True))
                p.wait("scalar", "s_rc", rc)
                p.op("scalar", None,
                     lambda e,
                     o=rows3[0:1, (pp % 2) * 512:(pp % 2) * 512 + 512],
                     a=pb[7][0:1, 0:512]:
                     e.activation(o, a, AF.Ln, scale=1.0 / HD,
                                  bias=epsc[0:1, 0:1]))
                rw = p.op("scalar", "s_rw",
                          lambda e,
                          o=rows3[0:1, (pp % 2) * 512:(pp % 2) * 512 + 512],
                          i=rows3[0:1, (pp % 2) * 512:(pp % 2) * 512 + 512]:
                          e.activation(o, i, AF.Exp, scale=-0.5))
                p.set_guard(("rowb",), "s_rw", rw)
                p.wait("tensor", "s_rw", rw)
                rp = p.op("tensor", "s_rp",
                          lambda e, o=pb[biA][:, 0:512],
                          lh=ones[0:1, 0:128],
                          rh=rows3[0:1,
                                   (pp % 2) * 512:(pp % 2) * 512 + 512]:
                          e.matmul(o, lh, rh, start=True, stop=True))
                p.wait("vector", "s_rp", rp)
                on = p.op("vector", "s_on",
                          lambda e,
                          o=qv[:, 4 * pp:4 * pp + 4, cs0:cs1],
                          a=of8[:, pp * 512:(pp + 1) * 512].rearrange(
                              "p (a c) -> p a c", a=4),
                          r=pb[biA][:, 0:512].rearrange(
                              "p (a c) -> p a c", a=4):
                          e.tensor_mul(o, a, r))
                p.set_guard(("bank", biA), "s_on", on)
            p.guards[("gR1w",)] = [("s_oev", p.c("s_oev")),
                                   ("s_om", p.c("s_om"))]
            p.set_guard(("gR2w",), "s_rc", p.c("s_rc"))
            p.set_guard(("gR3w",), "s_om", p.c("s_om"))
            p.set_guard(("gKSw",), "s_am", p.c("s_am"))
        else:
            p.set_guard(("gKSw",), "s_kt", nkt)
        p.wait("tensor", "s_kte", kte)
        for h4 in range(2):
            bi = 5 + h4
            p.guard("tensor", ("bank", bi))
            for hh in range(4):
                h = h4 * 4 + hh
                p.op("tensor", "s_ds" if hh == 3 else None,
                     lambda e, o=pb[bi][:, hh * 128:(hh + 1) * 128],
                     lh=ksT8[:, h, :], rh=vv[:, c, h * 128:(h + 1) * 128]:
                     e.matmul(o, lh, rh, start=True, stop=True))
        nds = p.c("s_ds")
        p.set_guard(("gKTw",), "s_ds", nds)
        p.wait("vector", "s_ds", nds)
        for h4 in range(2):
            p.op("vector", None,
                 lambda e, o=s_t[:, 4 * h4:4 * h4 + 4, :],
                 a=pb[5 + h4][:, 0:512].rearrange("p (a c) -> p a c", a=4):
                 e.tensor_add(o, o, a))
        su = p.op("vector", "s_su",
                  lambda e, o=s_t[:, :, :],
                  sc_=ebl8[:, 0:8].unsqueeze(2).broadcast_to([128, 8, 128]):
                  e.tensor_mul(o, o, sc_))
        p.set_guard(("bank", 5), "s_su", su)
        p.set_guard(("bank", 6), "s_su", su)
        p.set_guard(("gEblw",), "s_su", su)
        if full:
            p.wait("gpsimd", "s_su", su)
            p.wait("gpsimd", "s_om", p.c("s_om"))
            p.op("gpsimd", "s_sbf",
                 lambda e: e.tensor_copy(s_bf.rearrange("p h t -> p (h t)"),
                                         s_t.rearrange("p h t -> p (h t)")))

    p.mark("P1prefix")
    proj_pass(n_xp, do_q=False)
    for c in range(NCH):
        scan_chunk_b(c, full=False)
    p.wcur("vector", "s_su")
    fl = p.op("vector", "s_su",
              lambda e: e.tensor_scalar_mul(
                  s_t.rearrange("p h t -> p (h t)"),
                  s_t.rearrange("p h t -> p (h t)"), flags[:, :]))
    p.wait("gpsimd", "s_su", fl)
    p.op("gpsimd", "s_sbf",
         lambda e: e.tensor_copy(s_bf.rearrange("p h t -> p (h t)"),
                                 s_t.rearrange("p h t -> p (h t)")))

    p.mark("P2proj")
    p.wcur("gpsimd", "s_pm")
    p.wcur("gpsimd", "s_gm")
    for d in range(DCH):
        p.dma("gpsimd", "s_x", xvr[:, d, :], xt_r[:, d, :])
    n_x = p.c("s_x")
    p.wcur("scalar", "s_su")
    p.wcur("scalar", "s_kte")
    proj_pass(n_x, do_q=True)

    if debug:
        p.wcur("scalar", "s_pe")
        p.wcur("scalar", "s_gv")
        p.wcur("gpsimd", "s_pe")
        p.wcur("gpsimd", "s_gv")
        p.wcur("gpsimd", "s_su")
        dump_bf("d_q", qv, 16, TT)
        dump_bf("d_k", kv, 8, TT)
        dump_bf("d_v", vv, 8, TT)
        dump_bf("d_gk", gkv, 8, TT)
        dump_fm("d_s", s_t, 8, HD)
        p.wcur("vector", "s_dbg")
        p.wcur("gpsimd", "s_dbg")

    p.mark("P3gla")
    p.wcur("vector", "s_pe")
    p.wcur("vector", "s_gv")
    p.wcur("tensor", "s_vte")
    for c in range(NCH):
        scan_chunk_b(c, full=True)

    p.mark("P4wo")
    n_on_all = p.c("s_on")
    p.wait("tensor", "s_on", n_on_all)
    p.wcur("gpsimd", "s_sc")
    p.wcur("gpsimd", "s_qk")
    for hp in range(8):
        sc_ = None
        for k2 in range(2):
            h = 2 * hp + k2
            slot = (hp % 2) * 2 + k2
            p.guard("gpsimd", ("wo", slot))
            ld = p.dma("gpsimd", "s_wo", wo4[:, slot, :], wo_r[:, h, :])
            p.wait("gpsimd", "s_wo", ld)
            sc_ = p.op("gpsimd", "s_wos",
                       lambda e, t=wo4[:, slot, :]:
                       e.tensor_scalar_mul(t, t, gnws[:, :]))
        p.wait("tensor", "s_wos", sc_)
        s0 = (hp % 2) * 2
        for d in range(DCH):
            for half in range(2):
                bi = (d * 2 + half) % 4
                bank = pb[bi]
                p.guard("tensor", ("bank", bi))
                p.op("tensor", None,
                     lambda e, o=bank[:, 0:HFW],
                     lh=wo4[:, s0, d * 128:(d + 1) * 128],
                     rh=onv(2 * hp)[:, half * HFW:(half + 1) * HFW]:
                     e.matmul(o, lh, rh, start=True, stop=False))
                wm = p.op("tensor", "s_wm",
                          lambda e, o=bank[:, 0:HFW],
                          lh=wo4[:, s0 + 1, d * 128:(d + 1) * 128],
                          rh=onv(2 * hp + 1)[:, half * HFW:(half + 1) * HFW]:
                          e.matmul(o, lh, rh, start=False, stop=True))
                p.wait("vector", "s_wm", wm)
                pr = p.op("vector", "s_pr",
                          lambda e, o=xvr[:, d, half * HFW:(half + 1) * HFW],
                          i=xv[:, d, half * HFW:(half + 1) * HFW],
                          a=bank[:, 0:HFW]:
                          e.tensor_add(o, i, a))
                p.set_guard(("bank", bi), "s_pr", pr)
        p.set_guard(("wo", s0), "s_wm", p.c("s_wm"))
        p.set_guard(("wo", s0 + 1), "s_wm", p.c("s_wm"))

    def ln_pass(is_ln1):
        tagm = "s_l1m" if is_ln1 else "s_l2m"
        tagq = "s_l1q" if is_ln1 else "s_l2q"
        tagr = "s_l1r" if is_ln1 else "s_l2r"
        tagp = "s_l1p" if is_ln1 else "s_l2p"
        tagt = "s_l1t" if is_ln1 else "s_l2t"
        for half in range(2):
            p.guard("tensor", ("bank", 4))
            p.guard("tensor", ("bank", 5))
            for d in range(DCH):
                last = d == DCH - 1
                p.guard("scalar", ("sqscr", d % 2))
                sq = p.op("scalar", tagq,
                          lambda e, o=sqscr[d % 2],
                          i=xv[:, d, half * HFW:(half + 1) * HFW]:
                          e.activation(o, i, AF.Square))
                p.op("tensor", None,
                     lambda e, o=pb[4][0:1, 0:HFW], lh=onescol[:, :],
                     rh=xv[:, d, half * HFW:(half + 1) * HFW],
                     s=(d == 0), z=last:
                     e.matmul(o, lh, rh, start=s, stop=z))
                p.wait("tensor", tagq, sq)
                mm = p.op("tensor", tagm,
                          lambda e, o=pb[5][0:1, 0:HFW], lh=onescol[:, :],
                          rh=sqscr[d % 2], s=(d == 0), z=last:
                          e.matmul(o, lh, rh, start=s, stop=z))
                p.set_guard(("sqscr", d % 2), tagm, mm)
            nm = p.c(tagm)
            p.wait("scalar", tagm, nm)
            p.wcur("scalar", tagp)
            p.op("scalar", None,
                 lambda e, o=rows5[0:1, 0:512], a=pb[4][0:1, 0:HFW]:
                 e.activation(o, a, AF.Copy, scale=1.0 / 2048.0))
            mu2 = p.op("scalar", tagr,
                       lambda e, o=rows5[0:1, 512:1024], i=rows5[0:1, 0:512]:
                       e.activation(o, i, AF.Square))
            p.wait("vector", tagr, mu2)
            vr = p.op("vector", tagr,
                      lambda e, o=rows5[0:1, 512:1024], a=pb[5][0:1, 0:HFW],
                      m2=rows5[0:1, 512:1024]:
                      e.scalar_tensor_tensor(o, a, 1.0 / 2048.0, m2,
                                             ALU.mult, ALU.subtract))
            p.wait("scalar", tagr, vr)
            p.op("scalar", None,
                 lambda e, o=rows5[0:1, 512:1024], i=rows5[0:1, 512:1024]:
                 e.activation(o, i, AF.Ln, bias=epsc[0:1, 0:1]))
            rs = p.op("scalar", tagr,
                      lambda e, o=rows5[0:1, 512:1024],
                      i=rows5[0:1, 512:1024]:
                      e.activation(o, i, AF.Exp, scale=-0.5))
            p.wait("tensor", tagr, rs)
            p.guard("tensor", ("bank", 6))
            p.guard("tensor", ("bank", 7))
            p.op("tensor", None,
                 lambda e, o=pb[6][:, 0:HFW], lh=ones[0:1, 0:128],
                 rh=rows5[0:1, 0:512]:
                 e.matmul(o, lh, rh, start=True, stop=True))
            rp = p.op("tensor", tagp,
                      lambda e, o=pb[7][:, 0:HFW], lh=ones[0:1, 0:128],
                      rh=rows5[0:1, 512:1024]:
                      e.matmul(o, lh, rh, start=True, stop=True))
            p.wait("vector", tagp, rp)
            for d in range(DCH):
                p.op("vector", None,
                     lambda e, o=y1scr,
                     i=xv[:, d, half * HFW:(half + 1) * HFW],
                     m=pb[6][:, 0:HFW]:
                     e.tensor_sub(o, i, m))
                tr = p.op("vector", tagt,
                          lambda e, o=xvr[:, d, half * HFW:(half + 1) * HFW],
                          i=y1scr, r=pb[7][:, 0:HFW]:
                          e.tensor_mul(o, i, r))
                if is_ln1:
                    p.wait("scalar", tagt, tr)
                    p.op("scalar", "s_l1b",
                         lambda e, o=y1bf[:, d, half * HFW:(half + 1) * HFW],
                         i=xv[:, d, half * HFW:(half + 1) * HFW]:
                         e.activation(o, i, AF.Copy))
            if is_ln1:
                p.wait("tensor", tagt, p.c(tagt))
                for ti in range(4):
                    col = (half * 4 + ti) * NE
                    for d in range(DCH):
                        p.op("tensor", None,
                             lambda e, o=pb[2][:, col:col + NE],
                             lh=xv[:, d, half * HFW + ti * 128:
                                   half * HFW + (ti + 1) * 128],
                             rh=wr_s[:, d, :], s=(d == 0), z=False:
                             e.matmul(o, lh, rh, start=s, stop=z))
                    p.op("tensor", "s_rt",
                         lambda e, o=pb[2][:, col:col + NE],
                         lh=ones[0:1, 0:128], rh=br_s[0:1, :]:
                         e.matmul(o, lh, rh, start=False, stop=True))
            p.set_guard(("bank", 6), tagt, p.c(tagt))
            p.set_guard(("bank", 7), tagt, p.c(tagt))
            p.set_guard(("bank", 4), tagr, p.c(tagr))
            p.set_guard(("bank", 5), tagr, p.c(tagr))

    p.mark("P5ln1")
    n_pr = p.c("s_pr")
    for eng in ("tensor", "scalar", "vector"):
        p.wait(eng, "s_pr", n_pr)
    if debug:
        p.wait("scalar", "s_pr", n_pr)
        dump_fm("d_pre", xv, DCH, TT)
        p.wcur("gpsimd", "s_wm")
        dump_bf("d_on", qv, 16, TT)
        p.wcur("vector", "s_dbg")
    p.wcur("tensor", "s_rw")
    p.guard("tensor", ("bank", 2))
    ln_pass(is_ln1=True)
    n_rt = p.c("s_rt")
    p.wait("vector", "s_rt", n_rt)
    mxv_ = mxs[:, 0:8]
    esl = mxs[:, 8:40]
    sms = mxs[:, 40:48]
    inv = mxs[:, 48:56]
    mxe = mxs[:, 56:64]
    tpw = mxs[:, 64:72]
    msc = mxs[:, 72:104]
    for ti in range(8):
        p.op("vector", None,
             lambda e, o=mxv_[:, ti:ti + 1],
             i=pb[2][:, ti * NE:(ti + 1) * NE]:
             e.reduce_max(o, i, axis=AX.X))
    for ti in range(8):
        p.op("vector", "s_sm" if ti == 7 else None,
             lambda e, o=esl[:, ti * 4:(ti + 1) * 4],
             i=pb[2][:, ti * NE:(ti + 1) * NE], m=mxv_[:, ti:ti + 1]:
             e.tensor_scalar_sub(o, i, m))
    sm0 = p.c("s_sm")
    p.set_guard(("bank", 2), "s_sm", sm0)
    p.wait("scalar", "s_sm", sm0)
    se = p.op("scalar", "s_sme",
              lambda e, o=esl[:, :], i=esl[:, :]: e.activation(o, i, AF.Exp))
    p.wait("vector", "s_sme", se)
    for ti in range(8):
        p.op("vector", None,
             lambda e, o=sms[:, ti:ti + 1], i=esl[:, ti * 4:(ti + 1) * 4]:
             e.reduce_sum(o, i, axis=AX.X))
    for ti in range(8):
        p.op("vector", None,
             lambda e, o=mxe[:, ti:ti + 1], i=esl[:, ti * 4:(ti + 1) * 4]:
             e.reduce_max(o, i, axis=AX.X))
    for ti in range(8):
        p.op("vector", None,
             lambda e, o=inv[:, ti:ti + 1], i=sms[:, ti:ti + 1]:
             e.reciprocal(o, i))
    for ti in range(8):
        p.op("vector", None,
             lambda e, o=tpw[:, ti:ti + 1], a=mxe[:, ti:ti + 1],
             b_=inv[:, ti:ti + 1]: e.tensor_mul(o, a, b_))
    for ti in range(8):
        p.op("vector", None,
             lambda e, o=msc[:, ti * 4:(ti + 1) * 4],
             i=esl[:, ti * 4:(ti + 1) * 4], m=mxe[:, ti:ti + 1]:
             e.tensor_single_scalar(o, i, m, ALU.is_ge))
    for ti in range(8):
        p.op("vector", "s_sm" if ti == 7 else None,
             lambda e, o=msc[:, ti * 4:(ti + 1) * 4],
             i=msc[:, ti * 4:(ti + 1) * 4], m=tpw[:, ti:ti + 1]:
             e.tensor_scalar_mul(o, i, m))
    smn = p.c("s_sm")
    p.wait("tensor", "s_sm", smn)
    p.guard("tensor", ("bank", 3))
    p.guard("tensor", ("bank", 4))
    for ti in range(8):
        bi = 3 if ti < 4 else 4
        p.op("tensor", "s_mt" if ti % 4 == 3 else None,
             lambda e, o=pb[bi][0:4, (ti % 4) * 128:(ti % 4 + 1) * 128],
             i=msc[:, ti * 4:(ti + 1) * 4]:
             e.transpose(o, i, identf[:, :]))
    nmt = p.c("s_mt")
    p.wait("vector", "s_mt", nmt)
    p.op("vector", None,
         lambda e, o=maskfm[0:4, 0:512], a=pb[3][0:4, 0:512]:
         e.tensor_copy(o, a))
    mf = p.op("vector", "s_mf",
              lambda e, o=maskfm[0:4, 512:1024], a=pb[4][0:4, 0:512]:
              e.tensor_copy(o, a))
    p.set_guard(("bank", 3), "s_mf", mf)
    p.set_guard(("bank", 4), "s_mf", mf)

    if debug:
        p.wcur("scalar", "s_l1t")
        dump_fm("d_y1", xv, DCH, TT)
        p.wcur("scalar", "s_sm")
        p.dma("scalar", "s_dbg", dbg_dram("d_msc", [128, 112])[:, :],
              mxs[:, :]) if debug else None
        p.wcur("vector", "s_dbg")

    p.mark("P6ffn")
    n_l1b = p.c("s_l1b")
    p.wait("tensor", "s_l1b", n_l1b)
    p.wait("vector", "s_mf", p.c("s_mf"))
    p.wcur("gpsimd", "s_wm")
    p.wcur("gpsimd", "s_pm")
    p.wcur("gpsimd", "s_om")
    p.wcur("gpsimd", "s_ds")
    p.wcur("gpsimd", "s_vte")
    p.wcur("gpsimd", "s_pr")
    ldwg = {}
    ldwu = {}
    for e_ in range(NE):
        for half in range(2):
            p.guard("tensor", ("bank", 7))
            p.wcur("tensor", "s_mf")
            mr = p.op("tensor", "s_mr",
                      lambda e, o=pb[7][:, 0:HFW],
                      lh=sel4[0:NE, e_ * 128:(e_ + 1) * 128],
                      rh=maskfm[0:NE, half * HFW:(half + 1) * HFW]:
                      e.matmul(o, lh, rh, start=True, stop=True))
            p.wait("vector", "s_mr", mr)
            mre = p.op("vector", "s_mre",
                       lambda e, o=mask_rep[:, half * HFW:(half + 1) * HFW],
                       a=pb[7][:, 0:HFW]:
                       e.tensor_copy(o, a))
            p.set_guard(("bank", 7), "s_mre", mre)
        for blk in range(NBLK):
            bpar = blk % 2
            for f2 in range(FBLK):
                fi2 = blk * FBLK + f2
                gi = e_ * 64 + fi2
                tpar = fi2 % 2
                p.wait("gpsimd", "s_gu", 2 * (gi - 1))
                ldwg[fi2] = p.dma(
                    "gpsimd", "s_fw", wgtile[:, tpar, :, :],
                    wgl_c[e_][:, :, fi2 * 128:(fi2 + 1) * 128])
                ldwu[fi2] = p.dma(
                    "gpsimd", "s_fw", wutile[:, tpar, :, :],
                    wul_c[e_][:, :, fi2 * 128:(fi2 + 1) * 128])
            p.guard("gpsimd", ("wd",))
            ldwd = p.dma("gpsimd", "s_fw", wdtile[:, :, :],
                         wdl_r[e_][:, blk * FBLK:(blk + 1) * FBLK, :])
            p.guard("vector", ("h", bpar))
            for f in range(FBLK):
                fi = blk * FBLK + f
                tpar = fi % 2
                fo = 0
                p.wait("tensor", "s_fw", ldwu[fi])
                for half in range(2):
                    bgi, bui = half, 2 + half
                    p.guard("tensor", ("bank", bgi))
                    p.guard("tensor", ("bank", bui))
                    for d in range(DCH):
                        last = d == DCH - 1
                        p.op("tensor", None,
                             lambda e, o=pb[bgi][:, 0:HFW],
                             lh=wgtile[:, tpar, d, fo:fo + 128],
                             rh=y1bf[:, d, half * HFW:(half + 1) * HFW],
                             s=(d == 0), z=last:
                             e.matmul(o, lh, rh, start=s, stop=z))
                    for d in range(DCH):
                        last = d == DCH - 1
                        p.op("tensor", "s_gu" if last else None,
                             lambda e, o=pb[bui][:, 0:HFW],
                             lh=wutile[:, tpar, d, fo:fo + 128],
                             rh=y1bf[:, d, half * HFW:(half + 1) * HFW],
                             s=(d == 0), z=last:
                             e.matmul(o, lh, rh, start=s, stop=z))
                    gu = p.c("s_gu")
                    sgsl = sgscr[:, ((f % 2) * 2 + half) * 512:
                                 ((f % 2) * 2 + half + 1) * 512]
                    p.wait("scalar", "s_gu", gu)
                    p.guard("scalar", ("sg", (f % 2) * 2 + half))
                    hh = p.op("scalar", "s_hh",
                              lambda e, o=sgsl, a=pb[bgi][:, 0:HFW]:
                              e.activation(o, a, AF.Silu))
                    p.set_guard(("bank", bgi), "s_hh", hh)
                    p.wait("vector", "s_hh", hh)
                    p.op("vector", None,
                         lambda e, o=sgsl, i=sgsl,
                         m=mask_rep[:, half * HFW:(half + 1) * HFW]:
                         e.tensor_mul(o, i, m))
                    hv = p.op("vector", "s_hv",
                              lambda e,
                              o=hbuf[:, bpar * FBLK + f,
                                     half * HFW:(half + 1) * HFW],
                              i=sgsl, a=pb[bui][:, 0:HFW]:
                              e.tensor_mul(o, i, a))
                    p.set_guard(("bank", bui), "s_hv", hv)
                    p.set_guard(("sg", (f % 2) * 2 + half), "s_hv", hv)
            nhv = p.c("s_hv")
            p.wait("tensor", "s_hv", nhv)
            p.wait("tensor", "s_fw", ldwd)
            for d in range(DCH):
                for half in range(2):
                    bi = 4 + (d * 2 + half) % 2
                    bank = pb[bi]
                    p.guard("tensor", ("bank", bi))
                    for f in range(FBLK):
                        last = f == FBLK - 1
                        p.op("tensor", "s_dn" if last else None,
                             lambda e, o=bank[:, 0:HFW],
                             lh=wdtile[:, f, d * 128:(d + 1) * 128],
                             rh=hbuf[:, bpar * FBLK + f,
                                     half * HFW:(half + 1) * HFW],
                             s=(f == 0), z=last:
                             e.matmul(o, lh, rh, start=s, stop=z))
                    dn = p.c("s_dn")
                    p.wait("vector", "s_dn", dn)
                    da = p.op("vector", "s_da",
                              lambda e,
                              o=xvr[:, d, half * HFW:(half + 1) * HFW],
                              i=xv[:, d, half * HFW:(half + 1) * HFW],
                              a=bank[:, 0:HFW]: e.tensor_add(o, i, a))
                    p.set_guard(("bank", bi), "s_da", da)
            p.set_guard(("wd",), "s_dn", p.c("s_dn"))
            p.set_guard(("h", bpar), "s_dn", p.c("s_dn"))

    p.mark("P7ln2")
    n_da = p.c("s_da")
    for eng in ("tensor", "scalar", "vector"):
        p.wait(eng, "s_da", n_da)
    if debug:
        p.wait("scalar", "s_da", n_da)
        dump_fm("d_y2a", xv, DCH, TT)
        p.wcur("vector", "s_dbg")
    p.wcur("scalar", "s_hv")
    p.wcur("vector", "s_hv")
    ln_pass(is_ln1=False)

    p.mark("P8out")
    n_l2t = p.c("s_l2t")
    p.wait("tensor", "s_l2t", n_l2t)
    p.wcur("vector", "s_mre")
    for ti in range(8):
        p.guard("vector", ("ytm",))
        for k4 in range(4):
            bi = k4
            p.guard("tensor", ("bank", bi))
            for dd in range(4):
                d = k4 * 4 + dd
                p.op("tensor", "s_tr" if dd == 3 else None,
                     lambda e, o=pb[bi][:, dd * 128:(dd + 1) * 128],
                     i=xv[:, d, ti * 128:(ti + 1) * 128]:
                     e.transpose(o, i, identf[:, :]))
            ntr = p.c("s_tr")
            p.wait("vector", "s_tr", ntr)
            te = p.op("vector", "s_te",
                      lambda e, o=ytm[:, k4 * 512:(k4 + 1) * 512],
                      a=pb[bi][:, 0:HFW]: e.tensor_copy(o, a))
            p.set_guard(("bank", bi), "s_te", te)
        nte = p.c("s_te")
        p.wait("sync", "s_te", nte)
        od = p.dma("sync", "s_out", yout[ti * 128:(ti + 1) * 128, :],
                   ytm[:, :])
        p.set_guard(("ytm",), "s_out", od)
    p.wait("sync", "s_out", p.c("s_out"))
    if debug:
        p.wcur("sync", "s_dbg")
        p.wcur("scalar", "s_dbg")
        p.wcur("gpsimd", "s_dbg")

    with nc.Block() as block:
        @block.sync
        def _(e):
            p.replay("sync", e)

        @block.scalar
        def _(e):
            p.replay("scalar", e)

        @block.vector
        def _(e):
            p.replay("vector", e)

        @block.tensor
        def _(e):
            p.replay("tensor", e)

        @block.gpsimd
        def _(e):
            p.replay("gpsimd", e)
    return nc, st


def kernel(**inputs):
    f32 = np.float32

    def g(name):
        return np.ascontiguousarray(np.asarray(inputs[name], f32))

    x = g("L0_attn_hidden_states")
    shared = {
        "gnwd": g("L0_attn_g_norm_w").reshape(128, 1),
        "bqt": np.ascontiguousarray(g("L0_attn_bq").reshape(16, 128).T),
        "bkt": np.ascontiguousarray(g("L0_attn_bk").reshape(8, 128).T),
        "bvt": np.ascontiguousarray(g("L0_attn_bv").reshape(8, 128).T),
        "bg2t": np.ascontiguousarray(g("L0_attn_bg2").reshape(8, 128).T),
        "onesd": np.ones((1, 128), f32),
        "identd": np.eye(128, dtype=f32),
        "amaskd": np.triu(np.ones((128, 128), f32)),
        "wq": g("L0_attn_Wq"),
        "wk": g("L0_attn_Wk"),
        "wv": g("L0_attn_Wv"),
        "wg1d": np.ascontiguousarray(
            g("L0_attn_Wg1").reshape(DCH, 128, GATE).transpose(1, 0, 2)),
        "wg2d": g("L0_attn_Wg2"),
        "wo": g("L0_attn_Wo"),
        "wrd": np.ascontiguousarray(
            g("L0_ffn_Wr").reshape(DCH, 128, NE).transpose(1, 0, 2)),
        "brd": g("L0_ffn_br").reshape(1, NE),
        "seld": np.repeat(np.eye(NE, dtype=f32), 128, axis=1).reshape(
            NE, NE * 128),
        "wgl": g("L0_ffn_Wg"),
        "wul": g("L0_ffn_Wu"),
        "wdl": g("L0_ffn_Wd"),
    }
    in_maps = []
    for c in range(8):
        b, hf = divmod(c, 2)
        m = dict(shared)
        m["xt"] = np.ascontiguousarray(x[b, hf * TT:(hf + 1) * TT, :].T)
        m["xpt"] = np.ascontiguousarray(x[b, 0:TT, :].T)
        m["flagd"] = np.full((128, 1), float(hf), f32)
        in_maps.append(m)
    debug = bool(inputs.pop("_debug", False))
    nc, st = build_program(debug=debug)
    res = run_bass_kernel_spmd(nc, in_maps, core_ids=list(range(8)))
    if debug:
        kernel._dbg = res.results
    st.close()
    out = np.empty((B, L, D), f32)
    for c in range(8):
        b, hf = divmod(c, 2)
        out[b, hf * TT:(hf + 1) * TT, :] = res.results[c]["y"]
    return out


# version 09
# speedup vs baseline: 1.0597x; 1.0175x over previous
import contextlib
import numpy as np
import concourse.bass as bass
import concourse.mybir as mybir
from concourse.bass_utils import run_bass_kernel_spmd

F32 = mybir.dt.float32
F32R = mybir.dt.float32r
BF16 = mybir.dt.bfloat16
AF = mybir.ActivationFunctionType
ALU = mybir.AluOpType
AX = mybir.AxisListType

B, L, D = 4, 2048, 2048
TT = 1024
HFW = 512
CH = 128
NCH = TT // CH
KVH, QHN, HD = 8, 16, 128
DCH = 16
GATE = 16
FF = 8192
NE = 4
FBLK = 4
NBLK = FF // 128 // FBLK
QSCALE = float(HD) ** -0.5
EPS = 1e-5

ENGS = ("sync", "scalar", "vector", "tensor", "gpsimd")


class P:

    def __init__(self, nc, stack):
        self.nc = nc
        self.ops = {e: [] for e in ENGS}
        self.sems = {}
        self.cnt = {}
        self.stack = stack
        self.guards = {}

    def sem(self, name):
        if name not in self.sems:
            self.sems[name] = self.stack.enter_context(self.nc.semaphore(name))
            self.cnt[name] = 0
        return self.sems[name]

    def c(self, name):
        self.sem(name)
        return self.cnt[name]

    def wait(self, eng, name, val):
        if val is None or val <= 0:
            return
        h = self.sem(name)
        self.ops[eng].append(lambda e, h=h, v=val: e.wait_ge(h, v))

    def wcur(self, eng, name):
        self.wait(eng, name, self.c(name))

    def guard(self, eng, key):
        g = self.guards.get(key)
        if g is None:
            return
        for (nm, v) in (g if isinstance(g, list) else [g]):
            self.wait(eng, nm, v)

    def set_guard(self, key, name, val):
        self.guards[key] = (name, val)

    def dma(self, eng, name, out, in_, **kw):
        h = self.sem(name)
        self.cnt[name] += 16
        self.ops[eng].append(
            lambda e, o=out, i=in_, h=h, kw=kw:
            e.dma_start(o, i, **kw).then_inc(h, 16))
        return self.cnt[name]

    def op(self, eng, name, fn):
        if name is None:
            self.ops[eng].append(fn)
            return None
        h = self.sem(name)
        self.cnt[name] += 1
        self.ops[eng].append(lambda e, fn=fn, h=h: fn(e).then_inc(h, 1))
        return self.cnt[name]

    def mark(self, phase):
        if not hasattr(self, "marks"):
            self.marks = []
        self.marks.append((phase, {e: len(self.ops[e]) for e in ENGS}))

    def replay(self, eng, e):
        bounds = []
        for (ph, idx) in getattr(self, "marks", []):
            bounds.append((idx[eng], ph))
        cur = None
        for i, f in enumerate(self.ops[eng]):
            for (bi, ph) in bounds:
                if bi == i:
                    if cur is not None:
                        self.nc.leave_named_scope(cur[0], cur[1], False)
                    sid, _ = self.nc.enter_named_scope(ph + "_" + eng, False)
                    cur = (ph + "_" + eng, sid)
            f(e)
        if cur is not None:
            self.nc.leave_named_scope(cur[0], cur[1], False)


def r32(ap):
    return ap.bitcast(F32R)


def build_program(debug=False):
    nc = bass.Bass("TRN2", target_bir_lowering=False)
    st = contextlib.ExitStack()

    def din(name, shape):
        return nc.dram_tensor(name, shape, F32, kind="ExternalInput")

    xt = din("xt", [D, TT])
    xpt = din("xpt", [D, TT])
    flagd = din("flagd", [128, 1])
    gnwd = din("gnwd", [128, 1])
    bqt = din("bqt", [128, 16])
    bkt = din("bkt", [128, 8])
    bvt = din("bvt", [128, 8])
    bg2t = din("bg2t", [128, 8])
    onesd = din("onesd", [1, 128])
    identd = din("identd", [128, 128])
    amaskd = din("amaskd", [128, 128])
    wq = din("wq", [D, D])
    wk = din("wk", [D, 1024])
    wv = din("wv", [D, 1024])
    wg1d = din("wg1d", [128, DCH, GATE])
    wg2d = din("wg2d", [GATE, 1024])
    wo = din("wo", [D, D])
    wrd = din("wrd", [128, DCH, NE])
    brd = din("brd", [1, NE])
    seld = din("seld", [NE, NE * 128])
    wgl = din("wgl", [NE, D, FF])
    wul = din("wul", [NE, D, FF])
    wdl = din("wdl", [NE, FF, D])
    yout = nc.dram_tensor("y", [TT, D], F32, kind="ExternalOutput")
    dbg_t = {}

    def dbg_dram(name, shape):
        if name not in dbg_t:
            dbg_t[name] = nc.dram_tensor(name, shape, F32,
                                         kind="ExternalOutput")
        return dbg_t[name]

    xt_r = xt.rearrange("(d p) t -> p d t", p=128)
    xpt_r = xpt.rearrange("(d p) t -> p d t", p=128)
    wq_c = wq.rearrange("(d p) m -> p d m", p=128)
    wk_c = wk.rearrange("(d p) m -> p d m", p=128)
    wv_c = wv.rearrange("(d p) m -> p d m", p=128)
    wo_r = wo.rearrange("(h p) m -> p h m", p=128)
    wgl_c = [wgl[e].rearrange("(d p) m -> p d m", p=128) for e in range(NE)]
    wul_c = [wul[e].rearrange("(d p) m -> p d m", p=128) for e in range(NE)]
    wdl_r = [wdl[e].rearrange("(c p) m -> p c m", p=128) for e in range(NE)]

    def sb(name, shape, dt):
        return st.enter_context(nc.sbuf_tensor(name, shape, dt))

    ones = sb("ones", [1, 128], F32)
    onescol = sb("onescol", [128, 1], F32)
    onescol_b = sb("onescol_b", [128, 1], BF16)
    identf = sb("identf", [128, 128], F32)
    zeros = sb("zeros", [128, 128], F32)
    bq_s = sb("bq_s", [128, 16], F32)
    bk_s = sb("bk_s", [128, 8], F32)
    bv_s = sb("bv_s", [128, 8], F32)
    bg2p = sb("bg2p", [128, 8], F32)
    bg2n = sb("bg2n", [128, 8], F32)
    gnws = sb("gnws", [128, 1], F32)
    flags = sb("flags", [128, 1], F32)
    epsc = sb("epsc", [128, 1], F32)
    wr_s = sb("wr_s", [128, DCH, NE], F32)
    br_s = sb("br_s", [1, NE], F32)
    sel4 = sb("sel4", [NE, NE * 128], F32)
    wg1_s = sb("wg1_s", [128, DCH, GATE], F32R)
    small_s = sb("small_s", [GATE, 2048], F32)
    xg1s = small_s[:, 0:1024]
    wg2_s = small_s[:, 1024:2048]
    rows3 = small_s[0:1, 0:1024]
    rows5 = small_s[0:1, 0:1024]
    mxs = sb("mxs", [128, 112], F32)
    maskfm = small_s[0:4, 1024:2048]

    ar1 = sb("ar1", [128, DCH, TT], F32)
    ar2 = sb("ar2", [128, 16, TT], BF16)
    ar3 = sb("ar3", [128, 8, TT], BF16)
    ar4 = sb("ar4", [128, 8, TT], BF16)
    ar5 = sb("ar5", [128, 8, TT], BF16)
    ar6 = sb("ar6", [128, 4096], F32)
    ar7 = sb("ar7", [128, 4096], F32R)
    ar8 = sb("ar8", [128, 3088], F32)
    s_t = sb("s_t", [128, KVH, HD], F32)
    s_bf = sb("s_bf", [128, KVH, HD], BF16)

    xv = ar1
    qv = ar2
    y1bf = ar2
    kv = ar3
    hbuf = ar3
    vv = ar4
    wdtile = ar4.rearrange("p h t -> p (h t)").rearrange(
        "p (c m) -> p c m", c=FBLK)
    vfm = ar5
    gkv = ar5
    wo4 = ar5.rearrange("p h t -> p (h t)").rearrange(
        "p (s m) -> p s m", s=4)
    ar6b = ar6.bitcast(BF16)
    wgtile = ar6b[:, 0:4096].rearrange("p (b d m) -> p b d m", b=2, d=DCH)
    wutile = ar6b[:, 4096:8192].rearrange("p (b d m) -> p b d m", b=2, d=DCH)
    wstrr = ar7.rearrange("p (b d m) -> p b d m", b=2, d=DCH)
    xvr = ar1.bitcast(F32R)

    def onv(qh):
        return qv[:, qh, :]

    gbw = ar8
    gbb = ar8.bitcast(BF16)
    b8 = gbw[:, 0:1024].rearrange("p (h c) -> p h c", h=8)
    am8 = gbb[:, 0:2048]
    of8 = gbb[:, 0:2048]
    expb8 = gbw[:, 1024:2048].rearrange("p (h c) -> p h c", h=8)
    sq8 = gbb[:, 2048:4096]
    expnb8 = gbw[:, 2048:3072].rearrange("p (h c) -> p h c", h=8)
    qs8 = gbb[:, 4096:6144]
    ebl8 = gbw[:, 3072:3088]
    a6b = ar6.bitcast(BF16)
    ks8 = a6b[:, 0:1024].rearrange("p (h c) -> p h c", h=8)
    ksT8 = a6b[:, 1024:2048].rearrange("p (h c) -> p h c", h=8)
    amask = ar6[:, 2048:2176]
    identb = a6b[:, 4352:4480]

    gkch = [ar8[:, 0:1024], ar8[:, 1024:2048]]
    y1scr = ar8[:, 0:512]
    sqscr = [ar8[:, 1024:1536], ar8[:, 1536:2048]]
    mask_rep = ar8[:, 0:1024]
    sgscr = ar8.bitcast(BF16)[:, 2048:4096]
    ytm = ar8[:, 0:2048]

    pb = [st.enter_context(nc.psum_tensor(f"pb{i}", [128, HFW], F32))
          for i in range(8)]

    p = P(nc, st)

    for (dst, src) in [(ones, onesd), (identf, identd), (amask, amaskd),
                       (bq_s, bqt), (bk_s, bkt), (bv_s, bvt), (bg2p, bg2t),
                       (gnws, gnwd), (flags, flagd), (wr_s, wrd),
                       (br_s, brd), (sel4, seld),
                       (wg2_s, wg2d)]:
        nd = len(dst.shape)
        p.dma("sync", "s_c", dst[(slice(None),) * nd], src[(slice(None),) * nd])
    n_c = p.c("s_c")
    p.dma("gpsimd", "s_cb", identb[:, :], identd[:, :])
    p.dma("gpsimd", "s_cb", wg1_s[:, :, :], wg1d[:, :, :])
    for d in range(DCH):
        p.dma("gpsimd", "s_x", xvr[:, d, :], xpt_r[:, d, :])
    n_xp = p.c("s_x")
    p.wait("vector", "s_c", n_c)
    p.op("vector", "s_c0", lambda e: e.memset(zeros[:, :], 0.0))
    p.op("vector", "s_c0", lambda e: e.memset(onescol[:, :], 1.0))
    p.op("vector", "s_c0", lambda e: e.memset(onescol_b[:, :], 1.0))
    p.op("vector", "s_c0", lambda e: e.memset(epsc[:, :], EPS))
    p.op("vector", "s_c0",
         lambda e: e.memset(s_t.rearrange("p h t -> p (h t)"), 0.0))
    p.op("vector", "s_c0",
         lambda e: e.tensor_scalar_mul(bq_s[:, :], bq_s[:, :], QSCALE))
    p.op("vector", "s_c0",
         lambda e: e.tensor_scalar_mul(bg2n[:, :], bg2p[:, :], -1.0))
    n_c0 = p.c("s_c0")
    for eng in ("tensor", "scalar", "gpsimd"):
        p.wait(eng, "s_c0", n_c0)

    wu = {"u": 0, "mm": {}}

    def proj_matmuls(w_c, j, xload, dst, func, bias, scale):
        u = wu["u"]
        wu["u"] += 1
        buf = u % 2
        if u - 2 in wu["mm"]:
            p.wait("gpsimd", "s_pm", wu["mm"][u - 2])
        ld = p.dma("gpsimd", "s_wa", wstrr[:, buf, :, 0:128],
                   w_c[:, :, j * 128:(j + 1) * 128])
        p.wait("tensor", "s_wa", ld)
        p.wait("tensor", "s_x", xload)
        bis = [0 + (u % 2) * 2, 1 + (u % 2) * 2]
        for bi in bis:
            p.guard("tensor", ("bank", bi))
        for half, bi in ((0, bis[0]), (1, bis[1])):
            bank = pb[bi]
            for d in range(DCH):
                last = d == DCH - 1
                p.op("tensor", "s_pm" if last else None,
                     lambda e, bank=bank, half=half, buf=buf, d=d,
                     s=(d == 0), z=last:
                     e.matmul(bank[:, 0:HFW], wstrr[:, buf, d, :],
                              xvr[:, d, half * HFW:(half + 1) * HFW],
                              start=s, stop=z))
        wu["mm"][u] = p.c("s_pm")
        for half, bi in ((0, bis[0]), (1, bis[1])):
            bank = pb[bi]
            p.wcur("scalar", "s_pm")
            ev = p.op("scalar", "s_pe",
                      lambda e, a=bank[:, 0:HFW],
                      o=dst[:, j, half * HFW:(half + 1) * HFW],
                      f=func, sc=scale, bi_=bias[:, j:j + 1]:
                      e.activation(o, a, f, scale=sc, bias=bi_))
            p.set_guard(("bank", bi), "s_pe", ev)

    def proj_pass(xload, do_q):
        for j in range(KVH):
            proj_matmuls(wk_c, j, xload, kv, AF.Relu, bk_s, 1.0)
        for j in range(KVH):
            proj_matmuls(wv_c, j, xload, vfm, AF.Identity, bv_s, 1.0)
        p.wcur("tensor", "s_pe")
        p.wait("tensor", "s_cb", 16)
        for tc in range(NCH):
            bi = 4 + tc % 2
            bank = pb[bi]
            p.guard("tensor", ("bank", bi))
            bankb = bank.bitcast(BF16)
            for h in range(KVH):
                p.op("tensor", "s_vt" if h == KVH - 1 else None,
                     lambda e, o=bankb[:, h * 128:(h + 1) * 128],
                     i=vfm[:, h, tc * 128:(tc + 1) * 128]:
                     e.transpose(o, i, identb[:, :]))
            p.wcur("vector", "s_vt")
            ev = p.op("vector", "s_vte",
                      lambda e, o=vv[:, tc, :], a=bankb[:, 0:1024]:
                      e.tensor_copy(o, a))
            p.set_guard(("bank", bi), "s_vte", ev)
        p.wcur("tensor", "s_vte")
        for half in range(2):
            p.guard("tensor", ("bank", 6))
            for d in range(DCH):
                last = d == DCH - 1
                p.op("tensor", "s_pm" if last else None,
                     lambda e, d=d, half=half, s=(d == 0), z=last:
                     e.matmul(pb[6][0:GATE, 0:HFW], wg1_s[:, d, :],
                              xvr[:, d, half * HFW:(half + 1) * HFW],
                              start=s, stop=z))
            p.wcur("vector", "s_pm")
            p.guard("vector", ("xg1s", half))
            ev = p.op("vector", "s_g1",
                      lambda e, half=half:
                      e.tensor_copy(xg1s[0:GATE, half * HFW:(half + 1) * HFW],
                                    pb[6][0:GATE, 0:HFW]))
            p.set_guard(("bank", 6), "s_g1", ev)
        for j in range(KVH):
            for half in range(2):
                par = (j * 2 + half) % 2
                bi = 5 + 2 * par
                bank = pb[bi]
                p.guard("tensor", ("bank", bi))
                p.wcur("tensor", "s_g1")
                gm = p.op("tensor", "s_gm",
                          lambda e, j=j, half=half, bank=bank:
                          e.matmul(bank[:, 0:HFW],
                                   wg2_s[0:GATE, j * 128:(j + 1) * 128],
                                   xg1s[0:GATE,
                                        half * HFW:(half + 1) * HFW],
                                   start=True, stop=True))
                p.set_guard(("xg1s", half), "s_gm", gm)
                sa = gkch[par][:, 0:512]
                sbb = gkch[par][:, 512:1024]
                p.wait("scalar", "s_gm", gm)
                p.guard("scalar", ("gkch", par))
                p.op("scalar", None,
                     lambda e, a=bank[:, 0:HFW], o=sa, j=j:
                     e.activation(o, a, AF.Relu, scale=-1.0,
                                  bias=bg2n[:, j:j + 1]))
                ab = p.op("scalar", "s_gk",
                          lambda e, a=bank[:, 0:HFW], o=sbb, j=j:
                          e.activation(o, a, AF.Abs, bias=bg2p[:, j:j + 1]))
                p.set_guard(("bank", bi), "s_gk", ab)
                p.op("scalar", None,
                     lambda e, o=sbb: e.activation(o, o, AF.Exp, scale=-1.0))
                gk2 = p.op("scalar", "s_gk",
                           lambda e, o=sbb:
                           e.activation(o, o, AF.Ln, bias=1.0))
                p.wait("vector", "s_gk", gk2)
                p.op("vector", None,
                     lambda e, a=sa, b_=sbb: e.tensor_add(a, a, b_))
                gv_ = p.op("vector", "s_gv",
                           lambda e, a=sa,
                           o=gkv[:, j, half * HFW:(half + 1) * HFW]:
                           e.tensor_scalar_mul(o, a, -1.0 / 16.0))
                p.set_guard(("gkch", par), "s_gv", gv_)
        if do_q:
            for j in range(QHN):
                proj_matmuls(wq_c, j, xload, qv, AF.Relu, bq_s, QSCALE)

    def scan_chunk_b(c, full):
        cs0, cs1 = c * CH, (c + 1) * CH
        p.guard("vector", ("gR1w",))
        for h in range(KVH):
            p.op("vector", "s_sc" if h == KVH - 1 else None,
                 lambda e, o=b8[:, h, :], i=gkv[:, h, cs0:cs1]:
                 e.tensor_tensor_scan(o, i, zeros[:, :], 0.0,
                                      ALU.add, ALU.add))
        nsc = p.c("s_sc")
        p.wait("scalar", "s_sc", nsc)
        p.guard("scalar", ("gR2w",))
        p.guard("scalar", ("gR3w",))
        p.guard("scalar", ("gEblw",))
        if full:
            p.op("scalar", None,
                 lambda e, o=expb8.rearrange("p h c -> p (h c)"),
                 i=b8.rearrange("p h c -> p (h c)"):
                 e.activation(o, i, AF.Exp))
        p.op("scalar", None,
             lambda e, o=expnb8.rearrange("p h c -> p (h c)"),
             i=b8.rearrange("p h c -> p (h c)"):
             e.activation(o, i, AF.Exp, scale=-1.0))
        ex = p.op("scalar", "s_ex",
                  lambda e, o=ebl8[:, 0:8], i=b8[:, :, CH - 1]:
                  e.activation(o, i, AF.Exp))
        p.guards[("gR1w",)] = [("s_ex", ex)]
        p.wait("vector", "s_ex", ex)
        p.guard("vector", ("gKSw",))
        p.op("vector", "s_qk",
             lambda e, o=ks8[:, :, :], a=kv[:, :, cs0:cs1],
             b_=expnb8[:, :, :]: e.tensor_mul(o, a, b_))
        if full:
            p.op("vector", "s_qk",
                 lambda e,
                 o=qs8.rearrange("p (h q c) -> p h q c", h=8, q=2),
                 a=qv[:, :, cs0:cs1].rearrange("p (h q) c -> p h q c", q=2),
                 b_=expb8[:, :, :].unsqueeze(2).broadcast_to(
                     [128, 8, 2, 128]):
                 e.tensor_mul(o, a, b_))
        nqk = p.c("s_qk")
        p.wait("tensor", "s_qk", nqk)
        p.guard("tensor", ("bank", 0))
        for h in range(KVH):
            p.op("tensor", "s_kt" if h == KVH - 1 else None,
                 lambda e, o=pb[0].bitcast(BF16)[:, h * 128:(h + 1) * 128],
                 i=ks8[:, h, :]: e.transpose(o, i, identb[:, :]))
        nkt = p.c("s_kt")
        p.wait("vector", "s_kt", nkt)
        p.guard("vector", ("gKTw",))
        kte = p.op("vector", "s_kte",
                   lambda e, o=ksT8.rearrange("p h c -> p (h c)"),
                   a=pb[0].bitcast(BF16)[:, 0:1024]: e.tensor_copy(o, a))
        p.set_guard(("bank", 0), "s_kte", kte)
        if full:
            p.wait("tensor", "s_sbf", 1 + c if c >= 1 else 1)
            for pp in range(4):
                biA = 1 + pp % 2
                p.guard("tensor", ("bank", biA))
                for hh in range(2):
                    h = pp * 2 + hh
                    p.op("tensor", "s_am" if hh == 1 else None,
                         lambda e, o=pb[biA][:, hh * 256:(hh + 1) * 256],
                         lh=ks8[:, h, :],
                         rh=qs8[:, h * 256:(h + 1) * 256]:
                         e.matmul(o, lh, rh, start=True, stop=True))
                nam = p.c("s_am")
                p.wait("vector", "s_am", nam)
                ame = p.op("vector", "s_ame",
                           lambda e,
                           o=am8[:, pp * 512:(pp + 1) * 512].rearrange(
                               "p (a c) -> p a c", a=4),
                           a=pb[biA][:, 0:512].rearrange(
                               "p (a c) -> p a c", a=4),
                           m=amask[:, :].unsqueeze(1).broadcast_to(
                               [128, 4, 128]):
                           e.tensor_mul(o, a, m))
                biO = 3 + pp % 2
                p.guard("tensor", ("bank", biO))
                p.wait("tensor", "s_ame", ame)
                for hh in range(2):
                    h = pp * 2 + hh
                    p.op("tensor", None,
                         lambda e, o=pb[biO][:, hh * 256:(hh + 1) * 256],
                         lh=vv[:, c, h * 128:(h + 1) * 128],
                         rh=am8[:, h * 256:(h + 1) * 256]:
                         e.matmul(o, lh, rh, start=True, stop=False))
                    p.op("tensor", "s_om" if hh == 1 else None,
                         lambda e, o=pb[biO][:, hh * 256:(hh + 1) * 256],
                         lh=s_bf[:, h, :],
                         rh=qs8[:, h * 256:(h + 1) * 256]:
                         e.matmul(o, lh, rh, start=False, stop=True))
                nom = p.c("s_om")
                p.wait("scalar", "s_om", nom)
                p.op("scalar", None,
                     lambda e, o=of8[:, pp * 512:(pp + 1) * 512],
                     a=pb[biO][:, 0:512]: e.activation(o, a, AF.Copy))
                oev = p.op("scalar", "s_oev",
                           lambda e, o=sq8[:, pp * 512:(pp + 1) * 512],
                           a=pb[biO][:, 0:512]:
                           e.activation(o, a, AF.Square))
                p.set_guard(("bank", biO), "s_oev", oev)
                p.wait("tensor", "s_oev", oev)
                p.guard("tensor", ("rowb",))
                rc = p.op("tensor", "s_rc",
                          lambda e, o=pb[7][0:1, 0:512],
                          lh=onescol_b[:, :],
                          rh=sq8[:, pp * 512:(pp + 1) * 512]:
                          e.matmul(o, lh, rh, start=True, stop=True))
                p.wait("scalar", "s_rc", rc)
                p.op("scalar", None,
                     lambda e,
                     o=rows3[0:1, (pp % 2) * 512:(pp % 2) * 512 + 512],
                     a=pb[7][0:1, 0:512]:
                     e.activation(o, a, AF.Ln, scale=1.0 / HD,
                                  bias=epsc[0:1, 0:1]))
                rw = p.op("scalar", "s_rw",
                          lambda e,
                          o=rows3[0:1, (pp % 2) * 512:(pp % 2) * 512 + 512],
                          i=rows3[0:1, (pp % 2) * 512:(pp % 2) * 512 + 512]:
                          e.activation(o, i, AF.Exp, scale=-0.5))
                p.set_guard(("rowb",), "s_rw", rw)
                p.wait("tensor", "s_rw", rw)
                rp = p.op("tensor", "s_rp",
                          lambda e, o=pb[biA][:, 0:512],
                          lh=ones[0:1, 0:128],
                          rh=rows3[0:1,
                                   (pp % 2) * 512:(pp % 2) * 512 + 512]:
                          e.matmul(o, lh, rh, start=True, stop=True))
                p.wait("vector", "s_rp", rp)
                on = p.op("vector", "s_on",
                          lambda e,
                          o=qv[:, 4 * pp:4 * pp + 4, cs0:cs1],
                          a=of8[:, pp * 512:(pp + 1) * 512].rearrange(
                              "p (a c) -> p a c", a=4),
                          r=pb[biA][:, 0:512].rearrange(
                              "p (a c) -> p a c", a=4):
                          e.tensor_mul(o, a, r))
                p.set_guard(("bank", biA), "s_on", on)
            p.guards[("gR1w",)] = [("s_oev", p.c("s_oev")),
                                   ("s_om", p.c("s_om"))]
            p.set_guard(("gR2w",), "s_rc", p.c("s_rc"))
            p.set_guard(("gR3w",), "s_om", p.c("s_om"))
            p.set_guard(("gKSw",), "s_am", p.c("s_am"))
        else:
            p.set_guard(("gKSw",), "s_kt", nkt)
        p.wait("tensor", "s_kte", kte)
        for h4 in range(2):
            bi = 5 + h4
            p.guard("tensor", ("bank", bi))
            for hh in range(4):
                h = h4 * 4 + hh
                p.op("tensor", "s_ds" if hh == 3 else None,
                     lambda e, o=pb[bi][:, hh * 128:(hh + 1) * 128],
                     lh=ksT8[:, h, :], rh=vv[:, c, h * 128:(h + 1) * 128]:
                     e.matmul(o, lh, rh, start=True, stop=True))
        nds = p.c("s_ds")
        p.set_guard(("gKTw",), "s_ds", nds)
        p.wait("vector", "s_ds", nds)
        for h4 in range(2):
            p.op("vector", None,
                 lambda e, o=s_t[:, 4 * h4:4 * h4 + 4, :],
                 a=pb[5 + h4][:, 0:512].rearrange("p (a c) -> p a c", a=4):
                 e.tensor_add(o, o, a))
        su = p.op("vector", "s_su",
                  lambda e, o=s_t[:, :, :],
                  sc_=ebl8[:, 0:8].unsqueeze(2).broadcast_to([128, 8, 128]):
                  e.tensor_mul(o, o, sc_))
        p.set_guard(("bank", 5), "s_su", su)
        p.set_guard(("bank", 6), "s_su", su)
        p.set_guard(("gEblw",), "s_su", su)
        if full:
            p.wait("gpsimd", "s_su", su)
            p.wait("gpsimd", "s_om", p.c("s_om"))
            p.op("gpsimd", "s_sbf",
                 lambda e: e.tensor_copy(s_bf.rearrange("p h t -> p (h t)"),
                                         s_t.rearrange("p h t -> p (h t)")))

    p.mark("P1prefix")
    proj_pass(n_xp, do_q=False)
    for c in range(NCH):
        scan_chunk_b(c, full=False)
    p.wcur("vector", "s_su")
    fl = p.op("vector", "s_su",
              lambda e: e.tensor_scalar_mul(
                  s_t.rearrange("p h t -> p (h t)"),
                  s_t.rearrange("p h t -> p (h t)"), flags[:, :]))
    p.wait("gpsimd", "s_su", fl)
    p.op("gpsimd", "s_sbf",
         lambda e: e.tensor_copy(s_bf.rearrange("p h t -> p (h t)"),
                                 s_t.rearrange("p h t -> p (h t)")))

    p.mark("P2proj")
    p.wcur("gpsimd", "s_pm")
    p.wcur("gpsimd", "s_gm")
    for d in range(DCH):
        p.dma("gpsimd", "s_x", xvr[:, d, :], xt_r[:, d, :])
    n_x = p.c("s_x")
    p.wcur("scalar", "s_su")
    p.wcur("scalar", "s_kte")
    proj_pass(n_x, do_q=True)

    if debug:
        p.wcur("scalar", "s_pe")
        p.wcur("scalar", "s_gv")
        p.wcur("gpsimd", "s_pe")
        p.wcur("gpsimd", "s_gv")
        p.wcur("gpsimd", "s_su")
        dump_bf("d_q", qv, 16, TT)
        dump_bf("d_k", kv, 8, TT)
        dump_bf("d_v", vv, 8, TT)
        dump_bf("d_gk", gkv, 8, TT)
        dump_fm("d_s", s_t, 8, HD)
        p.wcur("vector", "s_dbg")
        p.wcur("gpsimd", "s_dbg")

    p.mark("P3gla")
    p.wcur("vector", "s_pe")
    p.wcur("vector", "s_gv")
    p.wcur("tensor", "s_vte")
    for c in range(NCH):
        scan_chunk_b(c, full=True)

    p.mark("P4wo")
    n_on_all = p.c("s_on")
    p.wcur("gpsimd", "s_sc")
    p.wcur("gpsimd", "s_qk")
    for hp in range(8):
        sc_ = None
        for k2 in range(2):
            h = 2 * hp + k2
            slot = (hp % 2) * 2 + k2
            p.guard("gpsimd", ("wo", slot))
            ld = p.dma("gpsimd", "s_wo", wo4[:, slot, :], wo_r[:, h, :])
            p.wait("gpsimd", "s_wo", ld)
            sc_ = p.op("gpsimd", "s_wos",
                       lambda e, t=wo4[:, slot, :]:
                       e.tensor_scalar_mul(t, t, gnws[:, :]))
        p.wait("tensor", "s_wos", sc_)
        p.wait("tensor", "s_on", n_on_all - 4 + hp // 2 + 1)
        s0 = (hp % 2) * 2
        for d in range(DCH):
            for half in range(2):
                bi = (d * 2 + half) % 4
                bank = pb[bi]
                p.guard("tensor", ("bank", bi))
                p.op("tensor", None,
                     lambda e, o=bank[:, 0:HFW],
                     lh=wo4[:, s0, d * 128:(d + 1) * 128],
                     rh=onv(2 * hp)[:, half * HFW:(half + 1) * HFW]:
                     e.matmul(o, lh, rh, start=True, stop=False))
                wm = p.op("tensor", "s_wm",
                          lambda e, o=bank[:, 0:HFW],
                          lh=wo4[:, s0 + 1, d * 128:(d + 1) * 128],
                          rh=onv(2 * hp + 1)[:, half * HFW:(half + 1) * HFW]:
                          e.matmul(o, lh, rh, start=False, stop=True))
                p.wait("vector", "s_wm", wm)
                pr = p.op("vector", "s_pr",
                          lambda e, o=xvr[:, d, half * HFW:(half + 1) * HFW],
                          i=xv[:, d, half * HFW:(half + 1) * HFW],
                          a=bank[:, 0:HFW]:
                          e.tensor_add(o, i, a))
                p.set_guard(("bank", bi), "s_pr", pr)
        p.set_guard(("wo", s0), "s_wm", p.c("s_wm"))
        p.set_guard(("wo", s0 + 1), "s_wm", p.c("s_wm"))

    def ln_pass(is_ln1):
        tagm = "s_l1m" if is_ln1 else "s_l2m"
        tagq = "s_l1q" if is_ln1 else "s_l2q"
        tagr = "s_l1r" if is_ln1 else "s_l2r"
        tagp = "s_l1p" if is_ln1 else "s_l2p"
        tagt = "s_l1t" if is_ln1 else "s_l2t"
        for half in range(2):
            p.guard("tensor", ("bank", 4))
            p.guard("tensor", ("bank", 5))
            for d in range(DCH):
                last = d == DCH - 1
                p.guard("scalar", ("sqscr", d % 2))
                sq = p.op("scalar", tagq,
                          lambda e, o=sqscr[d % 2],
                          i=xv[:, d, half * HFW:(half + 1) * HFW]:
                          e.activation(o, i, AF.Square))
                p.op("tensor", None,
                     lambda e, o=pb[4][0:1, 0:HFW], lh=onescol[:, :],
                     rh=xv[:, d, half * HFW:(half + 1) * HFW],
                     s=(d == 0), z=last:
                     e.matmul(o, lh, rh, start=s, stop=z))
                p.wait("tensor", tagq, sq)
                mm = p.op("tensor", tagm,
                          lambda e, o=pb[5][0:1, 0:HFW], lh=onescol[:, :],
                          rh=sqscr[d % 2], s=(d == 0), z=last:
                          e.matmul(o, lh, rh, start=s, stop=z))
                p.set_guard(("sqscr", d % 2), tagm, mm)
            nm = p.c(tagm)
            p.wait("scalar", tagm, nm)
            p.wcur("scalar", tagp)
            p.op("scalar", None,
                 lambda e, o=rows5[0:1, 0:512], a=pb[4][0:1, 0:HFW]:
                 e.activation(o, a, AF.Copy, scale=1.0 / 2048.0))
            mu2 = p.op("scalar", tagr,
                       lambda e, o=rows5[0:1, 512:1024], i=rows5[0:1, 0:512]:
                       e.activation(o, i, AF.Square))
            p.wait("vector", tagr, mu2)
            vr = p.op("vector", tagr,
                      lambda e, o=rows5[0:1, 512:1024], a=pb[5][0:1, 0:HFW],
                      m2=rows5[0:1, 512:1024]:
                      e.scalar_tensor_tensor(o, a, 1.0 / 2048.0, m2,
                                             ALU.mult, ALU.subtract))
            p.wait("scalar", tagr, vr)
            p.op("scalar", None,
                 lambda e, o=rows5[0:1, 512:1024], i=rows5[0:1, 512:1024]:
                 e.activation(o, i, AF.Ln, bias=epsc[0:1, 0:1]))
            rs = p.op("scalar", tagr,
                      lambda e, o=rows5[0:1, 512:1024],
                      i=rows5[0:1, 512:1024]:
                      e.activation(o, i, AF.Exp, scale=-0.5))
            p.wait("tensor", tagr, rs)
            p.guard("tensor", ("bank", 6))
            p.guard("tensor", ("bank", 7))
            p.op("tensor", None,
                 lambda e, o=pb[6][:, 0:HFW], lh=ones[0:1, 0:128],
                 rh=rows5[0:1, 0:512]:
                 e.matmul(o, lh, rh, start=True, stop=True))
            rp = p.op("tensor", tagp,
                      lambda e, o=pb[7][:, 0:HFW], lh=ones[0:1, 0:128],
                      rh=rows5[0:1, 512:1024]:
                      e.matmul(o, lh, rh, start=True, stop=True))
            p.wait("vector", tagp, rp)
            for d in range(DCH):
                p.op("vector", None,
                     lambda e, o=y1scr,
                     i=xv[:, d, half * HFW:(half + 1) * HFW],
                     m=pb[6][:, 0:HFW]:
                     e.tensor_sub(o, i, m))
                tr = p.op("vector", tagt,
                          lambda e, o=xvr[:, d, half * HFW:(half + 1) * HFW],
                          i=y1scr, r=pb[7][:, 0:HFW]:
                          e.tensor_mul(o, i, r))
                if is_ln1:
                    p.wait("scalar", tagt, tr)
                    p.op("scalar", "s_l1b",
                         lambda e, o=y1bf[:, d, half * HFW:(half + 1) * HFW],
                         i=xv[:, d, half * HFW:(half + 1) * HFW]:
                         e.activation(o, i, AF.Copy))
            if is_ln1:
                p.wait("tensor", tagt, p.c(tagt))
                for ti in range(4):
                    col = (half * 4 + ti) * NE
                    for d in range(DCH):
                        p.op("tensor", None,
                             lambda e, o=pb[2][:, col:col + NE],
                             lh=xv[:, d, half * HFW + ti * 128:
                                   half * HFW + (ti + 1) * 128],
                             rh=wr_s[:, d, :], s=(d == 0), z=False:
                             e.matmul(o, lh, rh, start=s, stop=z))
                    p.op("tensor", "s_rt",
                         lambda e, o=pb[2][:, col:col + NE],
                         lh=ones[0:1, 0:128], rh=br_s[0:1, :]:
                         e.matmul(o, lh, rh, start=False, stop=True))
            p.set_guard(("bank", 6), tagt, p.c(tagt))
            p.set_guard(("bank", 7), tagt, p.c(tagt))
            p.set_guard(("bank", 4), tagr, p.c(tagr))
            p.set_guard(("bank", 5), tagr, p.c(tagr))

    p.mark("P5ln1")
    n_pr = p.c("s_pr")
    for eng in ("tensor", "scalar", "vector"):
        p.wait(eng, "s_pr", n_pr)
    if debug:
        p.wait("scalar", "s_pr", n_pr)
        dump_fm("d_pre", xv, DCH, TT)
        p.wcur("gpsimd", "s_wm")
        dump_bf("d_on", qv, 16, TT)
        p.wcur("vector", "s_dbg")
    p.wcur("tensor", "s_rw")
    p.guard("tensor", ("bank", 2))
    ln_pass(is_ln1=True)
    n_rt = p.c("s_rt")
    p.wait("vector", "s_rt", n_rt)
    mxv_ = mxs[:, 0:8]
    esl = mxs[:, 8:40]
    sms = mxs[:, 40:48]
    inv = mxs[:, 48:56]
    mxe = mxs[:, 56:64]
    tpw = mxs[:, 64:72]
    msc = mxs[:, 72:104]
    for ti in range(8):
        p.op("vector", None,
             lambda e, o=mxv_[:, ti:ti + 1],
             i=pb[2][:, ti * NE:(ti + 1) * NE]:
             e.reduce_max(o, i, axis=AX.X))
    for ti in range(8):
        p.op("vector", "s_sm" if ti == 7 else None,
             lambda e, o=esl[:, ti * 4:(ti + 1) * 4],
             i=pb[2][:, ti * NE:(ti + 1) * NE], m=mxv_[:, ti:ti + 1]:
             e.tensor_scalar_sub(o, i, m))
    sm0 = p.c("s_sm")
    p.set_guard(("bank", 2), "s_sm", sm0)
    p.wait("scalar", "s_sm", sm0)
    se = p.op("scalar", "s_sme",
              lambda e, o=esl[:, :], i=esl[:, :]: e.activation(o, i, AF.Exp))
    p.wait("vector", "s_sme", se)
    for ti in range(8):
        p.op("vector", None,
             lambda e, o=sms[:, ti:ti + 1], i=esl[:, ti * 4:(ti + 1) * 4]:
             e.reduce_sum(o, i, axis=AX.X))
    for ti in range(8):
        p.op("vector", None,
             lambda e, o=mxe[:, ti:ti + 1], i=esl[:, ti * 4:(ti + 1) * 4]:
             e.reduce_max(o, i, axis=AX.X))
    for ti in range(8):
        p.op("vector", None,
             lambda e, o=inv[:, ti:ti + 1], i=sms[:, ti:ti + 1]:
             e.reciprocal(o, i))
    for ti in range(8):
        p.op("vector", None,
             lambda e, o=tpw[:, ti:ti + 1], a=mxe[:, ti:ti + 1],
             b_=inv[:, ti:ti + 1]: e.tensor_mul(o, a, b_))
    for ti in range(8):
        p.op("vector", None,
             lambda e, o=msc[:, ti * 4:(ti + 1) * 4],
             i=esl[:, ti * 4:(ti + 1) * 4], m=mxe[:, ti:ti + 1]:
             e.tensor_single_scalar(o, i, m, ALU.is_ge))
    for ti in range(8):
        p.op("vector", "s_sm" if ti == 7 else None,
             lambda e, o=msc[:, ti * 4:(ti + 1) * 4],
             i=msc[:, ti * 4:(ti + 1) * 4], m=tpw[:, ti:ti + 1]:
             e.tensor_scalar_mul(o, i, m))
    smn = p.c("s_sm")
    p.wait("tensor", "s_sm", smn)
    p.guard("tensor", ("bank", 3))
    p.guard("tensor", ("bank", 4))
    for ti in range(8):
        bi = 3 if ti < 4 else 4
        p.op("tensor", "s_mt" if ti % 4 == 3 else None,
             lambda e, o=pb[bi][0:4, (ti % 4) * 128:(ti % 4 + 1) * 128],
             i=msc[:, ti * 4:(ti + 1) * 4]:
             e.transpose(o, i, identf[:, :]))
    nmt = p.c("s_mt")
    p.wait("vector", "s_mt", nmt)
    p.op("vector", None,
         lambda e, o=maskfm[0:4, 0:512], a=pb[3][0:4, 0:512]:
         e.tensor_copy(o, a))
    mf = p.op("vector", "s_mf",
              lambda e, o=maskfm[0:4, 512:1024], a=pb[4][0:4, 0:512]:
              e.tensor_copy(o, a))
    p.set_guard(("bank", 3), "s_mf", mf)
    p.set_guard(("bank", 4), "s_mf", mf)

    if debug:
        p.wcur("scalar", "s_l1t")
        dump_fm("d_y1", xv, DCH, TT)
        p.wcur("scalar", "s_sm")
        p.dma("scalar", "s_dbg", dbg_dram("d_msc", [128, 112])[:, :],
              mxs[:, :]) if debug else None
        p.wcur("vector", "s_dbg")

    p.mark("P6ffn")
    n_l1b = p.c("s_l1b")
    p.wait("tensor", "s_l1b", n_l1b)
    p.wait("vector", "s_mf", p.c("s_mf"))
    p.wcur("gpsimd", "s_wm")
    p.wcur("gpsimd", "s_pm")
    p.wcur("gpsimd", "s_om")
    p.wcur("gpsimd", "s_ds")
    p.wcur("gpsimd", "s_vte")
    p.wcur("gpsimd", "s_pr")
    ldwg = {}
    ldwu = {}
    for e_ in range(NE):
        for half in range(2):
            p.guard("tensor", ("bank", 7))
            p.wcur("tensor", "s_mf")
            mr = p.op("tensor", "s_mr",
                      lambda e, o=pb[7][:, 0:HFW],
                      lh=sel4[0:NE, e_ * 128:(e_ + 1) * 128],
                      rh=maskfm[0:NE, half * HFW:(half + 1) * HFW]:
                      e.matmul(o, lh, rh, start=True, stop=True))
            p.wait("vector", "s_mr", mr)
            mre = p.op("vector", "s_mre",
                       lambda e, o=mask_rep[:, half * HFW:(half + 1) * HFW],
                       a=pb[7][:, 0:HFW]:
                       e.tensor_copy(o, a))
            p.set_guard(("bank", 7), "s_mre", mre)
        for blk in range(NBLK):
            bpar = blk % 2
            for f2 in range(FBLK):
                fi2 = blk * FBLK + f2
                gi = e_ * 64 + fi2
                tpar = fi2 % 2
                p.wait("gpsimd", "s_gu", 2 * (gi - 1))
                ldwg[fi2] = p.dma(
                    "gpsimd", "s_fw", wgtile[:, tpar, :, :],
                    wgl_c[e_][:, :, fi2 * 128:(fi2 + 1) * 128])
                ldwu[fi2] = p.dma(
                    "gpsimd", "s_fw", wutile[:, tpar, :, :],
                    wul_c[e_][:, :, fi2 * 128:(fi2 + 1) * 128])
            p.guard("gpsimd", ("wd",))
            ldwd = p.dma("gpsimd", "s_fw", wdtile[:, :, :],
                         wdl_r[e_][:, blk * FBLK:(blk + 1) * FBLK, :])
            p.guard("vector", ("h", bpar))
            for f in range(FBLK):
                fi = blk * FBLK + f
                tpar = fi % 2
                fo = 0
                p.wait("tensor", "s_fw", ldwu[fi])
                for half in range(2):
                    bgi, bui = half, 2 + half
                    p.guard("tensor", ("bank", bgi))
                    p.guard("tensor", ("bank", bui))
                    for d in range(DCH):
                        last = d == DCH - 1
                        p.op("tensor", None,
                             lambda e, o=pb[bgi][:, 0:HFW],
                             lh=wgtile[:, tpar, d, fo:fo + 128],
                             rh=y1bf[:, d, half * HFW:(half + 1) * HFW],
                             s=(d == 0), z=last:
                             e.matmul(o, lh, rh, start=s, stop=z))
                    for d in range(DCH):
                        last = d == DCH - 1
                        p.op("tensor", "s_gu" if last else None,
                             lambda e, o=pb[bui][:, 0:HFW],
                             lh=wutile[:, tpar, d, fo:fo + 128],
                             rh=y1bf[:, d, half * HFW:(half + 1) * HFW],
                             s=(d == 0), z=last:
                             e.matmul(o, lh, rh, start=s, stop=z))
                    gu = p.c("s_gu")
                    sgsl = sgscr[:, ((f % 2) * 2 + half) * 512:
                                 ((f % 2) * 2 + half + 1) * 512]
                    p.wait("scalar", "s_gu", gu)
                    p.guard("scalar", ("sg", (f % 2) * 2 + half))
                    hh = p.op("scalar", "s_hh",
                              lambda e, o=sgsl, a=pb[bgi][:, 0:HFW]:
                              e.activation(o, a, AF.Silu))
                    p.set_guard(("bank", bgi), "s_hh", hh)
                    p.wait("vector", "s_hh", hh)
                    p.op("vector", None,
                         lambda e, o=sgsl, i=sgsl,
                         m=mask_rep[:, half * HFW:(half + 1) * HFW]:
                         e.tensor_mul(o, i, m))
                    hv = p.op("vector", "s_hv",
                              lambda e,
                              o=hbuf[:, bpar * FBLK + f,
                                     half * HFW:(half + 1) * HFW],
                              i=sgsl, a=pb[bui][:, 0:HFW]:
                              e.tensor_mul(o, i, a))
                    p.set_guard(("bank", bui), "s_hv", hv)
                    p.set_guard(("sg", (f % 2) * 2 + half), "s_hv", hv)
            nhv = p.c("s_hv")
            p.wait("tensor", "s_hv", nhv)
            p.wait("tensor", "s_fw", ldwd)
            for d in range(DCH):
                for half in range(2):
                    bi = 4 + (d * 2 + half) % 3
                    bank = pb[bi]
                    p.guard("tensor", ("bank", bi))
                    for f in range(FBLK):
                        last = f == FBLK - 1
                        p.op("tensor", "s_dn" if last else None,
                             lambda e, o=bank[:, 0:HFW],
                             lh=wdtile[:, f, d * 128:(d + 1) * 128],
                             rh=hbuf[:, bpar * FBLK + f,
                                     half * HFW:(half + 1) * HFW],
                             s=(f == 0), z=last:
                             e.matmul(o, lh, rh, start=s, stop=z))
                    dn = p.c("s_dn")
                    p.wait("vector", "s_dn", dn)
                    da = p.op("vector", "s_da",
                              lambda e,
                              o=xvr[:, d, half * HFW:(half + 1) * HFW],
                              i=xv[:, d, half * HFW:(half + 1) * HFW],
                              a=bank[:, 0:HFW]: e.tensor_add(o, i, a))
                    p.set_guard(("bank", bi), "s_da", da)
            p.set_guard(("wd",), "s_dn", p.c("s_dn"))
            p.set_guard(("h", bpar), "s_dn", p.c("s_dn"))

    p.mark("P7ln2")
    n_da = p.c("s_da")
    for eng in ("tensor", "scalar", "vector"):
        p.wait(eng, "s_da", n_da)
    if debug:
        p.wait("scalar", "s_da", n_da)
        dump_fm("d_y2a", xv, DCH, TT)
        p.wcur("vector", "s_dbg")
    p.wcur("scalar", "s_hv")
    p.wcur("vector", "s_hv")
    ln_pass(is_ln1=False)

    p.mark("P8out")
    n_l2t = p.c("s_l2t")
    p.wait("tensor", "s_l2t", n_l2t)
    p.wcur("vector", "s_mre")
    for ti in range(8):
        p.guard("vector", ("ytm",))
        for k4 in range(4):
            bi = k4
            p.guard("tensor", ("bank", bi))
            for dd in range(4):
                d = k4 * 4 + dd
                p.op("tensor", "s_tr" if dd == 3 else None,
                     lambda e, o=pb[bi][:, dd * 128:(dd + 1) * 128],
                     i=xv[:, d, ti * 128:(ti + 1) * 128]:
                     e.transpose(o, i, identf[:, :]))
            ntr = p.c("s_tr")
            p.wait("vector", "s_tr", ntr)
            te = p.op("vector", "s_te",
                      lambda e, o=ytm[:, k4 * 512:(k4 + 1) * 512],
                      a=pb[bi][:, 0:HFW]: e.tensor_copy(o, a))
            p.set_guard(("bank", bi), "s_te", te)
        nte = p.c("s_te")
        p.wait("sync", "s_te", nte)
        od = p.dma("sync", "s_out", yout[ti * 128:(ti + 1) * 128, :],
                   ytm[:, :])
        p.set_guard(("ytm",), "s_out", od)
    p.wait("sync", "s_out", p.c("s_out"))
    if debug:
        p.wcur("sync", "s_dbg")
        p.wcur("scalar", "s_dbg")
        p.wcur("gpsimd", "s_dbg")

    with nc.Block() as block:
        @block.sync
        def _(e):
            p.replay("sync", e)

        @block.scalar
        def _(e):
            p.replay("scalar", e)

        @block.vector
        def _(e):
            p.replay("vector", e)

        @block.tensor
        def _(e):
            p.replay("tensor", e)

        @block.gpsimd
        def _(e):
            p.replay("gpsimd", e)
    return nc, st


def kernel(**inputs):
    f32 = np.float32

    def g(name):
        return np.ascontiguousarray(np.asarray(inputs[name], f32))

    x = g("L0_attn_hidden_states")
    shared = {
        "gnwd": g("L0_attn_g_norm_w").reshape(128, 1),
        "bqt": np.ascontiguousarray(g("L0_attn_bq").reshape(16, 128).T),
        "bkt": np.ascontiguousarray(g("L0_attn_bk").reshape(8, 128).T),
        "bvt": np.ascontiguousarray(g("L0_attn_bv").reshape(8, 128).T),
        "bg2t": np.ascontiguousarray(g("L0_attn_bg2").reshape(8, 128).T),
        "onesd": np.ones((1, 128), f32),
        "identd": np.eye(128, dtype=f32),
        "amaskd": np.triu(np.ones((128, 128), f32)),
        "wq": g("L0_attn_Wq"),
        "wk": g("L0_attn_Wk"),
        "wv": g("L0_attn_Wv"),
        "wg1d": np.ascontiguousarray(
            g("L0_attn_Wg1").reshape(DCH, 128, GATE).transpose(1, 0, 2)),
        "wg2d": g("L0_attn_Wg2"),
        "wo": g("L0_attn_Wo"),
        "wrd": np.ascontiguousarray(
            g("L0_ffn_Wr").reshape(DCH, 128, NE).transpose(1, 0, 2)),
        "brd": g("L0_ffn_br").reshape(1, NE),
        "seld": np.repeat(np.eye(NE, dtype=f32), 128, axis=1).reshape(
            NE, NE * 128),
        "wgl": g("L0_ffn_Wg"),
        "wul": g("L0_ffn_Wu"),
        "wdl": g("L0_ffn_Wd"),
    }
    in_maps = []
    for c in range(8):
        b, hf = divmod(c, 2)
        m = dict(shared)
        m["xt"] = np.ascontiguousarray(x[b, hf * TT:(hf + 1) * TT, :].T)
        m["xpt"] = np.ascontiguousarray(x[b, 0:TT, :].T)
        m["flagd"] = np.full((128, 1), float(hf), f32)
        in_maps.append(m)
    debug = bool(inputs.pop("_debug", False))
    nc, st = build_program(debug=debug)
    res = run_bass_kernel_spmd(nc, in_maps, core_ids=list(range(8)))
    if debug:
        kernel._dbg = res.results
    st.close()
    out = np.empty((B, L, D), f32)
    for c in range(8):
        b, hf = divmod(c, 2)
        out[b, hf * TT:(hf + 1) * TT, :] = res.results[c]["y"]
    return out


# version 10
# speedup vs baseline: 1.0769x; 1.0162x over previous
import contextlib
import numpy as np
import concourse.bass as bass
import concourse.mybir as mybir
from concourse.bass_utils import run_bass_kernel_spmd

F32 = mybir.dt.float32
F32R = mybir.dt.float32r
BF16 = mybir.dt.bfloat16
AF = mybir.ActivationFunctionType
ALU = mybir.AluOpType
AX = mybir.AxisListType

B, L, D = 4, 2048, 2048
TT = 1024
HFW = 512
CH = 128
NCH = TT // CH
KVH, QHN, HD = 8, 16, 128
DCH = 16
GATE = 16
FF = 8192
NE = 4
FBLK = 4
NBLK = FF // 128 // FBLK
QSCALE = float(HD) ** -0.5
EPS = 1e-5

ENGS = ("sync", "scalar", "vector", "tensor", "gpsimd")


class P:

    def __init__(self, nc, stack):
        self.nc = nc
        self.ops = {e: [] for e in ENGS}
        self.sems = {}
        self.cnt = {}
        self.stack = stack
        self.guards = {}

    def sem(self, name):
        if name not in self.sems:
            self.sems[name] = self.stack.enter_context(self.nc.semaphore(name))
            self.cnt[name] = 0
        return self.sems[name]

    def c(self, name):
        self.sem(name)
        return self.cnt[name]

    def wait(self, eng, name, val):
        if val is None or val <= 0:
            return
        h = self.sem(name)
        self.ops[eng].append(lambda e, h=h, v=val: e.wait_ge(h, v))

    def wcur(self, eng, name):
        self.wait(eng, name, self.c(name))

    def guard(self, eng, key):
        g = self.guards.get(key)
        if g is None:
            return
        for (nm, v) in (g if isinstance(g, list) else [g]):
            self.wait(eng, nm, v)

    def set_guard(self, key, name, val):
        self.guards[key] = (name, val)

    def dma(self, eng, name, out, in_, **kw):
        h = self.sem(name)
        self.cnt[name] += 16
        self.ops[eng].append(
            lambda e, o=out, i=in_, h=h, kw=kw:
            e.dma_start(o, i, **kw).then_inc(h, 16))
        return self.cnt[name]

    def op(self, eng, name, fn):
        if name is None:
            self.ops[eng].append(fn)
            return None
        h = self.sem(name)
        self.cnt[name] += 1
        self.ops[eng].append(lambda e, fn=fn, h=h: fn(e).then_inc(h, 1))
        return self.cnt[name]

    def mark(self, phase):
        if not hasattr(self, "marks"):
            self.marks = []
        self.marks.append((phase, {e: len(self.ops[e]) for e in ENGS}))

    def replay(self, eng, e):
        bounds = []
        for (ph, idx) in getattr(self, "marks", []):
            bounds.append((idx[eng], ph))
        cur = None
        for i, f in enumerate(self.ops[eng]):
            for (bi, ph) in bounds:
                if bi == i:
                    if cur is not None:
                        self.nc.leave_named_scope(cur[0], cur[1], False)
                    sid, _ = self.nc.enter_named_scope(ph + "_" + eng, False)
                    cur = (ph + "_" + eng, sid)
            f(e)
        if cur is not None:
            self.nc.leave_named_scope(cur[0], cur[1], False)


def r32(ap):
    return ap.bitcast(F32R)


def build_program(debug=False):
    nc = bass.Bass("TRN2", target_bir_lowering=False)
    st = contextlib.ExitStack()

    def din(name, shape):
        return nc.dram_tensor(name, shape, F32, kind="ExternalInput")

    xt = din("xt", [D, TT])
    xpt = din("xpt", [D, TT])
    flagd = din("flagd", [128, 1])
    gnwd = din("gnwd", [128, 1])
    bqt = din("bqt", [128, 16])
    bkt = din("bkt", [128, 8])
    bvt = din("bvt", [128, 8])
    bg2t = din("bg2t", [128, 8])
    onesd = din("onesd", [1, 128])
    identd = din("identd", [128, 128])
    amaskd = din("amaskd", [128, 128])
    wq = din("wq", [D, D])
    wk = din("wk", [D, 1024])
    wv = din("wv", [D, 1024])
    wg1d = din("wg1d", [128, DCH, GATE])
    wg2d = din("wg2d", [GATE, 1024])
    wo = din("wo", [D, D])
    wrd = din("wrd", [128, DCH, NE])
    brd = din("brd", [1, NE])
    seld = din("seld", [NE, NE * 128])
    wgl = din("wgl", [NE, D, FF])
    wul = din("wul", [NE, D, FF])
    wdl = din("wdl", [NE, FF, D])
    yout = nc.dram_tensor("y", [TT, D], F32, kind="ExternalOutput")
    dbg_t = {}

    def dbg_dram(name, shape):
        if name not in dbg_t:
            dbg_t[name] = nc.dram_tensor(name, shape, F32,
                                         kind="ExternalOutput")
        return dbg_t[name]

    xt_r = xt.rearrange("(d p) t -> p d t", p=128)
    xpt_r = xpt.rearrange("(d p) t -> p d t", p=128)
    wq_c = wq.rearrange("(d p) m -> p d m", p=128)
    wk_c = wk.rearrange("(d p) m -> p d m", p=128)
    wv_c = wv.rearrange("(d p) m -> p d m", p=128)
    wo_r = wo.rearrange("(h p) m -> p h m", p=128)
    wgl_c = [wgl[e].rearrange("(d p) m -> p d m", p=128) for e in range(NE)]
    wul_c = [wul[e].rearrange("(d p) m -> p d m", p=128) for e in range(NE)]
    wdl_r = [wdl[e].rearrange("(c p) m -> p c m", p=128) for e in range(NE)]

    def sb(name, shape, dt):
        return st.enter_context(nc.sbuf_tensor(name, shape, dt))

    ones = sb("ones", [1, 128], F32)
    onescol = sb("onescol", [128, 1], F32)
    onescol_b = sb("onescol_b", [128, 1], BF16)
    identf = sb("identf", [128, 128], F32)
    zeros = sb("zeros", [128, 128], F32)
    bq_s = sb("bq_s", [128, 16], F32)
    bk_s = sb("bk_s", [128, 8], F32)
    bv_s = sb("bv_s", [128, 8], F32)
    bg2p = sb("bg2p", [128, 8], F32)
    bg2n = sb("bg2n", [128, 8], F32)
    gnws = sb("gnws", [128, 1], F32)
    flags = sb("flags", [128, 1], F32)
    epsc = sb("epsc", [128, 1], F32)
    wr_s = sb("wr_s", [128, DCH, NE], F32)
    br_s = sb("br_s", [1, NE], F32)
    sel4 = sb("sel4", [NE, NE * 128], F32)
    wg1_s = sb("wg1_s", [128, DCH, GATE], F32R)
    small_s = sb("small_s", [GATE, 2048], F32)
    xg1s = small_s[:, 0:1024]
    wg2_s = small_s[:, 1024:2048]
    rows3 = small_s[0:1, 0:1024]
    rows5 = small_s[0:1, 0:1024]
    mxs = sb("mxs", [128, 112], F32)
    maskfm = small_s[0:4, 1024:2048]

    ar1 = sb("ar1", [128, DCH, TT], F32)
    ar2 = sb("ar2", [128, 16, TT], BF16)
    ar3 = sb("ar3", [128, 8, TT], BF16)
    ar4 = sb("ar4", [128, 8, TT], BF16)
    ar5 = sb("ar5", [128, 8, TT], BF16)
    ar6 = sb("ar6", [128, 4096], F32)
    ar7 = sb("ar7", [128, 4096], F32R)
    ar8 = sb("ar8", [128, 3088], F32)
    s_t = sb("s_t", [128, KVH, HD], F32)
    s_bf = sb("s_bf", [128, KVH, HD], BF16)

    xv = ar1
    qv = ar2
    y1bf = ar2
    kv = ar3
    hbuf = ar3
    vv = ar4
    wdtile = ar4.rearrange("p h t -> p (h t)").rearrange(
        "p (c m) -> p c m", c=FBLK)
    vfm = ar5
    gkv = ar5
    wo4 = ar5.rearrange("p h t -> p (h t)").rearrange(
        "p (s m) -> p s m", s=4)
    ar6b = ar6.bitcast(BF16)
    wgtile = ar6b[:, 0:4096].rearrange("p (b d m) -> p b d m", b=2, d=DCH)
    wutile = ar6b[:, 4096:8192].rearrange("p (b d m) -> p b d m", b=2, d=DCH)
    wstrr = ar7.rearrange("p (b d m) -> p b d m", b=2, d=DCH)
    xvr = ar1.bitcast(F32R)

    def onv(qh):
        return qv[:, qh, :]

    gbw = ar8
    gbb = ar8.bitcast(BF16)
    b8 = gbw[:, 0:1024].rearrange("p (h c) -> p h c", h=8)
    am8 = gbb[:, 0:2048]
    of8 = gbb[:, 0:2048]
    expb8 = gbw[:, 1024:2048].rearrange("p (h c) -> p h c", h=8)
    sq8 = gbb[:, 2048:4096]
    expnb8 = gbw[:, 2048:3072].rearrange("p (h c) -> p h c", h=8)
    qs8 = gbb[:, 4096:6144]
    ebl8 = gbw[:, 3072:3088]
    a6b = ar6.bitcast(BF16)
    ks8 = a6b[:, 0:1024].rearrange("p (h c) -> p h c", h=8)
    ksT8 = a6b[:, 1024:2048].rearrange("p (h c) -> p h c", h=8)
    amask = ar6[:, 2048:2176]
    identb = a6b[:, 4352:4480]

    gkch = [ar8[:, 0:1024], ar8[:, 1024:2048]]
    y1scr = ar8[:, 0:512]
    sqscr = [ar8[:, 1024:1536], ar8[:, 1536:2048]]
    mask_rep = ar8[:, 0:1024]
    sgscr = ar8.bitcast(BF16)[:, 2048:4096]
    ytm = ar8[:, 0:2048]

    pb = [st.enter_context(nc.psum_tensor(f"pb{i}", [128, HFW], F32))
          for i in range(8)]

    p = P(nc, st)

    for (dst, src) in [(ones, onesd), (identf, identd), (amask, amaskd),
                       (bq_s, bqt), (bk_s, bkt), (bv_s, bvt), (bg2p, bg2t),
                       (gnws, gnwd), (flags, flagd), (wr_s, wrd),
                       (br_s, brd), (sel4, seld),
                       (wg2_s, wg2d)]:
        nd = len(dst.shape)
        p.dma("sync", "s_c", dst[(slice(None),) * nd], src[(slice(None),) * nd])
    n_c = p.c("s_c")
    p.dma("gpsimd", "s_cb", identb[:, :], identd[:, :])
    p.dma("gpsimd", "s_cb", wg1_s[:, :, :], wg1d[:, :, :])
    for d in range(DCH):
        p.dma("gpsimd", "s_x", xvr[:, d, :], xpt_r[:, d, :])
    n_xp = p.c("s_x")
    p.wait("vector", "s_c", n_c)
    p.op("vector", "s_c0", lambda e: e.memset(zeros[:, :], 0.0))
    p.op("vector", "s_c0", lambda e: e.memset(onescol[:, :], 1.0))
    p.op("vector", "s_c0", lambda e: e.memset(onescol_b[:, :], 1.0))
    p.op("vector", "s_c0", lambda e: e.memset(epsc[:, :], EPS))
    p.op("vector", "s_c0",
         lambda e: e.memset(s_t.rearrange("p h t -> p (h t)"), 0.0))
    p.op("vector", "s_c0",
         lambda e: e.tensor_scalar_mul(bq_s[:, :], bq_s[:, :], QSCALE))
    p.op("vector", "s_c0",
         lambda e: e.tensor_scalar_mul(bg2n[:, :], bg2p[:, :], -1.0))
    n_c0 = p.c("s_c0")
    for eng in ("tensor", "scalar", "gpsimd"):
        p.wait(eng, "s_c0", n_c0)

    wu = {"u": 0, "mm": {}}

    def proj_matmuls(w_c, j, xload, dst, func, bias, scale):
        u = wu["u"]
        wu["u"] += 1
        buf = u % 2
        if u - 2 in wu["mm"]:
            p.wait("gpsimd", "s_pm", wu["mm"][u - 2])
        ld = p.dma("gpsimd", "s_wa", wstrr[:, buf, :, 0:128],
                   w_c[:, :, j * 128:(j + 1) * 128])
        p.wait("tensor", "s_wa", ld)
        p.wait("tensor", "s_x", xload)
        bis = [0 + (u % 2) * 2, 1 + (u % 2) * 2]
        for bi in bis:
            p.guard("tensor", ("bank", bi))
        for half, bi in ((0, bis[0]), (1, bis[1])):
            bank = pb[bi]
            for d in range(DCH):
                last = d == DCH - 1
                p.op("tensor", "s_pm" if last else None,
                     lambda e, bank=bank, half=half, buf=buf, d=d,
                     s=(d == 0), z=last:
                     e.matmul(bank[:, 0:HFW], wstrr[:, buf, d, :],
                              xvr[:, d, half * HFW:(half + 1) * HFW],
                              start=s, stop=z))
        wu["mm"][u] = p.c("s_pm")
        for half, bi in ((0, bis[0]), (1, bis[1])):
            bank = pb[bi]
            p.wcur("scalar", "s_pm")
            ev = p.op("scalar", "s_pe",
                      lambda e, a=bank[:, 0:HFW],
                      o=dst[:, j, half * HFW:(half + 1) * HFW],
                      f=func, sc=scale, bi_=bias[:, j:j + 1]:
                      e.activation(o, a, f, scale=sc, bias=bi_))
            p.set_guard(("bank", bi), "s_pe", ev)

    def proj_pass(xload, do_q):
        for j in range(KVH):
            proj_matmuls(wk_c, j, xload, kv, AF.Relu, bk_s, 1.0)
        for j in range(KVH):
            proj_matmuls(wv_c, j, xload, vfm, AF.Identity, bv_s, 1.0)
        p.wcur("tensor", "s_pe")
        p.wait("tensor", "s_cb", 16)
        for tc in range(NCH):
            bi = 4 + tc % 2
            bank = pb[bi]
            p.guard("tensor", ("bank", bi))
            bankb = bank.bitcast(BF16)
            for h in range(KVH):
                p.op("tensor", "s_vt" if h == KVH - 1 else None,
                     lambda e, o=bankb[:, h * 128:(h + 1) * 128],
                     i=vfm[:, h, tc * 128:(tc + 1) * 128]:
                     e.transpose(o, i, identb[:, :]))
            p.wcur("vector", "s_vt")
            ev = p.op("vector", "s_vte",
                      lambda e, o=vv[:, tc, :], a=bankb[:, 0:1024]:
                      e.tensor_copy(o, a))
            p.set_guard(("bank", bi), "s_vte", ev)
        p.wcur("tensor", "s_vte")
        for half in range(2):
            p.guard("tensor", ("bank", 6))
            for d in range(DCH):
                last = d == DCH - 1
                p.op("tensor", "s_pm" if last else None,
                     lambda e, d=d, half=half, s=(d == 0), z=last:
                     e.matmul(pb[6][0:GATE, 0:HFW], wg1_s[:, d, :],
                              xvr[:, d, half * HFW:(half + 1) * HFW],
                              start=s, stop=z))
            p.wcur("vector", "s_pm")
            p.guard("vector", ("xg1s", half))
            ev = p.op("vector", "s_g1",
                      lambda e, half=half:
                      e.tensor_copy(xg1s[0:GATE, half * HFW:(half + 1) * HFW],
                                    pb[6][0:GATE, 0:HFW]))
            p.set_guard(("bank", 6), "s_g1", ev)
        for j in range(KVH):
            for half in range(2):
                par = (j * 2 + half) % 2
                bi = 5 + 2 * par
                bank = pb[bi]
                p.guard("tensor", ("bank", bi))
                p.wcur("tensor", "s_g1")
                gm = p.op("tensor", "s_gm",
                          lambda e, j=j, half=half, bank=bank:
                          e.matmul(bank[:, 0:HFW],
                                   wg2_s[0:GATE, j * 128:(j + 1) * 128],
                                   xg1s[0:GATE,
                                        half * HFW:(half + 1) * HFW],
                                   start=True, stop=True))
                p.set_guard(("xg1s", half), "s_gm", gm)
                sa = gkch[par][:, 0:512]
                sbb = gkch[par][:, 512:1024]
                p.wait("scalar", "s_gm", gm)
                p.guard("scalar", ("gkch", par))
                p.op("scalar", None,
                     lambda e, a=bank[:, 0:HFW], o=sa, j=j:
                     e.activation(o, a, AF.Relu, scale=-1.0,
                                  bias=bg2n[:, j:j + 1]))
                ab = p.op("scalar", "s_gk",
                          lambda e, a=bank[:, 0:HFW], o=sbb, j=j:
                          e.activation(o, a, AF.Abs, bias=bg2p[:, j:j + 1]))
                p.set_guard(("bank", bi), "s_gk", ab)
                p.op("scalar", None,
                     lambda e, o=sbb: e.activation(o, o, AF.Exp, scale=-1.0))
                gk2 = p.op("scalar", "s_gk",
                           lambda e, o=sbb:
                           e.activation(o, o, AF.Ln, bias=1.0))
                p.wait("vector", "s_gk", gk2)
                p.op("vector", None,
                     lambda e, a=sa, b_=sbb: e.tensor_add(a, a, b_))
                gv_ = p.op("vector", "s_gv",
                           lambda e, a=sa,
                           o=gkv[:, j, half * HFW:(half + 1) * HFW]:
                           e.tensor_scalar_mul(o, a, -1.0 / 16.0))
                p.set_guard(("gkch", par), "s_gv", gv_)
        if do_q:
            for j in range(QHN):
                proj_matmuls(wq_c, j, xload, qv, AF.Relu, bq_s, QSCALE)

    def scan_chunk_b(c, full):
        cs0, cs1 = c * CH, (c + 1) * CH
        p.guard("vector", ("gR1w",))
        for h in range(KVH):
            p.op("vector", "s_sc" if h == KVH - 1 else None,
                 lambda e, o=b8[:, h, :], i=gkv[:, h, cs0:cs1]:
                 e.tensor_tensor_scan(o, i, zeros[:, :], 0.0,
                                      ALU.add, ALU.add))
        nsc = p.c("s_sc")
        p.wait("scalar", "s_sc", nsc)
        p.guard("scalar", ("gR2w",))
        p.guard("scalar", ("gR3w",))
        p.guard("scalar", ("gEblw",))
        if full:
            p.op("scalar", None,
                 lambda e, o=expb8.rearrange("p h c -> p (h c)"),
                 i=b8.rearrange("p h c -> p (h c)"):
                 e.activation(o, i, AF.Exp))
        p.op("scalar", None,
             lambda e, o=expnb8.rearrange("p h c -> p (h c)"),
             i=b8.rearrange("p h c -> p (h c)"):
             e.activation(o, i, AF.Exp, scale=-1.0))
        ex = p.op("scalar", "s_ex",
                  lambda e, o=ebl8[:, 0:8], i=b8[:, :, CH - 1]:
                  e.activation(o, i, AF.Exp))
        p.guards[("gR1w",)] = [("s_ex", ex)]
        p.wait("vector", "s_ex", ex)
        p.guard("vector", ("gKSw",))
        p.op("vector", "s_qk",
             lambda e, o=ks8[:, :, :], a=kv[:, :, cs0:cs1],
             b_=expnb8[:, :, :]: e.tensor_mul(o, a, b_))
        if full:
            p.op("vector", "s_qk",
                 lambda e,
                 o=qs8.rearrange("p (h q c) -> p h q c", h=8, q=2),
                 a=qv[:, :, cs0:cs1].rearrange("p (h q) c -> p h q c", q=2),
                 b_=expb8[:, :, :].unsqueeze(2).broadcast_to(
                     [128, 8, 2, 128]):
                 e.tensor_mul(o, a, b_))
        nqk = p.c("s_qk")
        p.wait("tensor", "s_qk", nqk)
        p.guard("tensor", ("bank", 0))
        for h in range(KVH):
            p.op("tensor", "s_kt" if h == KVH - 1 else None,
                 lambda e, o=pb[0].bitcast(BF16)[:, h * 128:(h + 1) * 128],
                 i=ks8[:, h, :]: e.transpose(o, i, identb[:, :]))
        nkt = p.c("s_kt")
        p.wait("vector", "s_kt", nkt)
        p.guard("vector", ("gKTw",))
        kte = p.op("vector", "s_kte",
                   lambda e, o=ksT8.rearrange("p h c -> p (h c)"),
                   a=pb[0].bitcast(BF16)[:, 0:1024]: e.tensor_copy(o, a))
        p.set_guard(("bank", 0), "s_kte", kte)
        if full:
            p.wait("tensor", "s_sbf", 1 + c if c >= 1 else 1)
            for pp in range(4):
                biA = 1 + pp % 2
                p.guard("tensor", ("bank", biA))
                for hh in range(2):
                    h = pp * 2 + hh
                    p.op("tensor", "s_am" if hh == 1 else None,
                         lambda e, o=pb[biA][:, hh * 256:(hh + 1) * 256],
                         lh=ks8[:, h, :],
                         rh=qs8[:, h * 256:(h + 1) * 256]:
                         e.matmul(o, lh, rh, start=True, stop=True))
                nam = p.c("s_am")
                p.wait("vector", "s_am", nam)
                ame = p.op("vector", "s_ame",
                           lambda e,
                           o=am8[:, pp * 512:(pp + 1) * 512].rearrange(
                               "p (a c) -> p a c", a=4),
                           a=pb[biA][:, 0:512].rearrange(
                               "p (a c) -> p a c", a=4),
                           m=amask[:, :].unsqueeze(1).broadcast_to(
                               [128, 4, 128]):
                           e.tensor_mul(o, a, m))
                biO = 3 + pp % 2
                p.guard("tensor", ("bank", biO))
                p.wait("tensor", "s_ame", ame)
                for hh in range(2):
                    h = pp * 2 + hh
                    p.op("tensor", None,
                         lambda e, o=pb[biO][:, hh * 256:(hh + 1) * 256],
                         lh=vv[:, c, h * 128:(h + 1) * 128],
                         rh=am8[:, h * 256:(h + 1) * 256]:
                         e.matmul(o, lh, rh, start=True, stop=False))
                    p.op("tensor", "s_om" if hh == 1 else None,
                         lambda e, o=pb[biO][:, hh * 256:(hh + 1) * 256],
                         lh=s_bf[:, h, :],
                         rh=qs8[:, h * 256:(h + 1) * 256]:
                         e.matmul(o, lh, rh, start=False, stop=True))
                nom = p.c("s_om")
                p.wait("scalar", "s_om", nom)
                p.op("scalar", None,
                     lambda e, o=of8[:, pp * 512:(pp + 1) * 512],
                     a=pb[biO][:, 0:512]: e.activation(o, a, AF.Copy))
                oev = p.op("scalar", "s_oev",
                           lambda e, o=sq8[:, pp * 512:(pp + 1) * 512],
                           a=pb[biO][:, 0:512]:
                           e.activation(o, a, AF.Square))
                p.set_guard(("bank", biO), "s_oev", oev)
                p.wait("tensor", "s_oev", oev)
                p.guard("tensor", ("rowb",))
                rc = p.op("tensor", "s_rc",
                          lambda e, o=pb[7][0:1, 0:512],
                          lh=onescol_b[:, :],
                          rh=sq8[:, pp * 512:(pp + 1) * 512]:
                          e.matmul(o, lh, rh, start=True, stop=True))
                p.wait("scalar", "s_rc", rc)
                p.op("scalar", None,
                     lambda e,
                     o=rows3[0:1, (pp % 2) * 512:(pp % 2) * 512 + 512],
                     a=pb[7][0:1, 0:512]:
                     e.activation(o, a, AF.Ln, scale=1.0 / HD,
                                  bias=epsc[0:1, 0:1]))
                rw = p.op("scalar", "s_rw",
                          lambda e,
                          o=rows3[0:1, (pp % 2) * 512:(pp % 2) * 512 + 512],
                          i=rows3[0:1, (pp % 2) * 512:(pp % 2) * 512 + 512]:
                          e.activation(o, i, AF.Exp, scale=-0.5))
                p.set_guard(("rowb",), "s_rw", rw)
                p.wait("tensor", "s_rw", rw)
                rp = p.op("tensor", "s_rp",
                          lambda e, o=pb[biA][:, 0:512],
                          lh=ones[0:1, 0:128],
                          rh=rows3[0:1,
                                   (pp % 2) * 512:(pp % 2) * 512 + 512]:
                          e.matmul(o, lh, rh, start=True, stop=True))
                p.wait("vector", "s_rp", rp)
                on = p.op("vector", "s_on",
                          lambda e,
                          o=qv[:, 4 * pp:4 * pp + 4, cs0:cs1],
                          a=of8[:, pp * 512:(pp + 1) * 512].rearrange(
                              "p (a c) -> p a c", a=4),
                          r=pb[biA][:, 0:512].rearrange(
                              "p (a c) -> p a c", a=4):
                          e.tensor_mul(o, a, r))
                p.set_guard(("bank", biA), "s_on", on)
            p.guards[("gR1w",)] = [("s_oev", p.c("s_oev")),
                                   ("s_om", p.c("s_om"))]
            p.set_guard(("gR2w",), "s_rc", p.c("s_rc"))
            p.set_guard(("gR3w",), "s_om", p.c("s_om"))
            p.set_guard(("gKSw",), "s_am", p.c("s_am"))
        else:
            p.set_guard(("gKSw",), "s_kt", nkt)
        p.wait("tensor", "s_kte", kte)
        for h4 in range(2):
            bi = 5 + h4
            p.guard("tensor", ("bank", bi))
            for hh in range(4):
                h = h4 * 4 + hh
                p.op("tensor", "s_ds" if hh == 3 else None,
                     lambda e, o=pb[bi][:, hh * 128:(hh + 1) * 128],
                     lh=ksT8[:, h, :], rh=vv[:, c, h * 128:(h + 1) * 128]:
                     e.matmul(o, lh, rh, start=True, stop=True))
        nds = p.c("s_ds")
        p.set_guard(("gKTw",), "s_ds", nds)
        p.wait("vector", "s_ds", nds)
        for h4 in range(2):
            p.op("vector", None,
                 lambda e, o=s_t[:, 4 * h4:4 * h4 + 4, :],
                 a=pb[5 + h4][:, 0:512].rearrange("p (a c) -> p a c", a=4):
                 e.tensor_add(o, o, a))
        su = p.op("vector", "s_su",
                  lambda e, o=s_t[:, :, :],
                  sc_=ebl8[:, 0:8].unsqueeze(2).broadcast_to([128, 8, 128]):
                  e.tensor_mul(o, o, sc_))
        p.set_guard(("bank", 5), "s_su", su)
        p.set_guard(("bank", 6), "s_su", su)
        p.set_guard(("gEblw",), "s_su", su)
        if full:
            p.wait("gpsimd", "s_su", su)
            p.wait("gpsimd", "s_om", p.c("s_om"))
            p.op("gpsimd", "s_sbf",
                 lambda e: e.tensor_copy(s_bf.rearrange("p h t -> p (h t)"),
                                         s_t.rearrange("p h t -> p (h t)")))

    p.mark("P1prefix")
    proj_pass(n_xp, do_q=False)
    for c in range(NCH):
        scan_chunk_b(c, full=False)
    p.wcur("vector", "s_su")
    fl = p.op("vector", "s_su",
              lambda e: e.tensor_scalar_mul(
                  s_t.rearrange("p h t -> p (h t)"),
                  s_t.rearrange("p h t -> p (h t)"), flags[:, :]))
    p.wait("gpsimd", "s_su", fl)
    p.op("gpsimd", "s_sbf",
         lambda e: e.tensor_copy(s_bf.rearrange("p h t -> p (h t)"),
                                 s_t.rearrange("p h t -> p (h t)")))

    p.mark("P2proj")
    p.wcur("gpsimd", "s_pm")
    p.wcur("gpsimd", "s_gm")
    for d in range(DCH):
        p.dma("gpsimd", "s_x", xvr[:, d, :], xt_r[:, d, :])
    n_x = p.c("s_x")
    p.wcur("scalar", "s_su")
    p.wcur("scalar", "s_kte")
    proj_pass(n_x, do_q=True)

    if debug:
        p.wcur("scalar", "s_pe")
        p.wcur("scalar", "s_gv")
        p.wcur("gpsimd", "s_pe")
        p.wcur("gpsimd", "s_gv")
        p.wcur("gpsimd", "s_su")
        dump_bf("d_q", qv, 16, TT)
        dump_bf("d_k", kv, 8, TT)
        dump_bf("d_v", vv, 8, TT)
        dump_bf("d_gk", gkv, 8, TT)
        dump_fm("d_s", s_t, 8, HD)
        p.wcur("vector", "s_dbg")
        p.wcur("gpsimd", "s_dbg")

    p.mark("P3gla")
    p.wcur("vector", "s_pe")
    p.wcur("vector", "s_gv")
    p.wcur("tensor", "s_vte")
    for c in range(NCH):
        scan_chunk_b(c, full=True)

    p.mark("P4wo")
    n_on_all = p.c("s_on")
    p.wcur("gpsimd", "s_sc")
    p.wcur("gpsimd", "s_qk")
    for hp in range(8):
        sc_ = None
        for k2 in range(2):
            h = 2 * hp + k2
            slot = (hp % 2) * 2 + k2
            p.guard("gpsimd", ("wo", slot))
            ld = p.dma("gpsimd", "s_wo", wo4[:, slot, :], wo_r[:, h, :])
            p.wait("gpsimd", "s_wo", ld)
            sc_ = p.op("gpsimd", "s_wos",
                       lambda e, t=wo4[:, slot, :]:
                       e.tensor_scalar_mul(t, t, gnws[:, :]))
        p.wait("tensor", "s_wos", sc_)
        p.wait("tensor", "s_on", n_on_all - 4 + hp // 2 + 1)
        s0 = (hp % 2) * 2
        for d in range(DCH):
            for half in range(2):
                bi = (d * 2 + half) % 4
                bank = pb[bi]
                p.guard("tensor", ("bank", bi))
                p.op("tensor", None,
                     lambda e, o=bank[:, 0:HFW],
                     lh=wo4[:, s0, d * 128:(d + 1) * 128],
                     rh=onv(2 * hp)[:, half * HFW:(half + 1) * HFW]:
                     e.matmul(o, lh, rh, start=True, stop=False))
                wm = p.op("tensor", "s_wm",
                          lambda e, o=bank[:, 0:HFW],
                          lh=wo4[:, s0 + 1, d * 128:(d + 1) * 128],
                          rh=onv(2 * hp + 1)[:, half * HFW:(half + 1) * HFW]:
                          e.matmul(o, lh, rh, start=False, stop=True))
                p.wait("vector", "s_wm", wm)
                pr = p.op("vector", "s_pr",
                          lambda e, o=xvr[:, d, half * HFW:(half + 1) * HFW],
                          i=xv[:, d, half * HFW:(half + 1) * HFW],
                          a=bank[:, 0:HFW]:
                          e.tensor_add(o, i, a))
                p.set_guard(("bank", bi), "s_pr", pr)
        p.set_guard(("wo", s0), "s_wm", p.c("s_wm"))
        p.set_guard(("wo", s0 + 1), "s_wm", p.c("s_wm"))

    def ln_pass(is_ln1):
        tagm = "s_l1m" if is_ln1 else "s_l2m"
        tagq = "s_l1q" if is_ln1 else "s_l2q"
        tagr = "s_l1r" if is_ln1 else "s_l2r"
        tagp = "s_l1p" if is_ln1 else "s_l2p"
        tagt = "s_l1t" if is_ln1 else "s_l2t"
        for half in range(2):
            p.guard("tensor", ("bank", 4))
            p.guard("tensor", ("bank", 5))
            for d in range(DCH):
                last = d == DCH - 1
                p.guard("scalar", ("sqscr", d % 2))
                sq = p.op("scalar", tagq,
                          lambda e, o=sqscr[d % 2],
                          i=xv[:, d, half * HFW:(half + 1) * HFW]:
                          e.activation(o, i, AF.Square))
                p.op("tensor", None,
                     lambda e, o=pb[4][0:1, 0:HFW], lh=onescol[:, :],
                     rh=xv[:, d, half * HFW:(half + 1) * HFW],
                     s=(d == 0), z=last:
                     e.matmul(o, lh, rh, start=s, stop=z))
                p.wait("tensor", tagq, sq)
                mm = p.op("tensor", tagm,
                          lambda e, o=pb[5][0:1, 0:HFW], lh=onescol[:, :],
                          rh=sqscr[d % 2], s=(d == 0), z=last:
                          e.matmul(o, lh, rh, start=s, stop=z))
                p.set_guard(("sqscr", d % 2), tagm, mm)
            nm = p.c(tagm)
            p.wait("scalar", tagm, nm)
            p.wcur("scalar", tagp)
            p.op("scalar", None,
                 lambda e, o=rows5[0:1, 0:512], a=pb[4][0:1, 0:HFW]:
                 e.activation(o, a, AF.Copy, scale=1.0 / 2048.0))
            mu2 = p.op("scalar", tagr,
                       lambda e, o=rows5[0:1, 512:1024], i=rows5[0:1, 0:512]:
                       e.activation(o, i, AF.Square))
            p.wait("vector", tagr, mu2)
            vr = p.op("vector", tagr,
                      lambda e, o=rows5[0:1, 512:1024], a=pb[5][0:1, 0:HFW],
                      m2=rows5[0:1, 512:1024]:
                      e.scalar_tensor_tensor(o, a, 1.0 / 2048.0, m2,
                                             ALU.mult, ALU.subtract))
            p.wait("scalar", tagr, vr)
            p.op("scalar", None,
                 lambda e, o=rows5[0:1, 512:1024], i=rows5[0:1, 512:1024]:
                 e.activation(o, i, AF.Ln, bias=epsc[0:1, 0:1]))
            rs = p.op("scalar", tagr,
                      lambda e, o=rows5[0:1, 512:1024],
                      i=rows5[0:1, 512:1024]:
                      e.activation(o, i, AF.Exp, scale=-0.5))
            p.wait("tensor", tagr, rs)
            p.guard("tensor", ("bank", 6))
            p.guard("tensor", ("bank", 7))
            p.op("tensor", None,
                 lambda e, o=pb[6][:, 0:HFW], lh=ones[0:1, 0:128],
                 rh=rows5[0:1, 0:512]:
                 e.matmul(o, lh, rh, start=True, stop=True))
            rp = p.op("tensor", tagp,
                      lambda e, o=pb[7][:, 0:HFW], lh=ones[0:1, 0:128],
                      rh=rows5[0:1, 512:1024]:
                      e.matmul(o, lh, rh, start=True, stop=True))
            p.wait("vector", tagp, rp)
            for d in range(DCH):
                p.op("vector", None,
                     lambda e, o=y1scr,
                     i=xv[:, d, half * HFW:(half + 1) * HFW],
                     m=pb[6][:, 0:HFW]:
                     e.tensor_sub(o, i, m))
                tr = p.op("vector", tagt,
                          lambda e, o=xvr[:, d, half * HFW:(half + 1) * HFW],
                          i=y1scr, r=pb[7][:, 0:HFW]:
                          e.tensor_mul(o, i, r))
                if is_ln1:
                    p.wait("scalar", tagt, tr)
                    p.op("scalar", "s_l1b",
                         lambda e, o=y1bf[:, d, half * HFW:(half + 1) * HFW],
                         i=xv[:, d, half * HFW:(half + 1) * HFW]:
                         e.activation(o, i, AF.Copy))
            if is_ln1:
                p.wait("tensor", tagt, p.c(tagt))
                for ti in range(4):
                    col = (half * 4 + ti) * NE
                    for d in range(DCH):
                        p.op("tensor", None,
                             lambda e, o=pb[2][:, col:col + NE],
                             lh=xv[:, d, half * HFW + ti * 128:
                                   half * HFW + (ti + 1) * 128],
                             rh=wr_s[:, d, :], s=(d == 0), z=False:
                             e.matmul(o, lh, rh, start=s, stop=z))
                    p.op("tensor", "s_rt",
                         lambda e, o=pb[2][:, col:col + NE],
                         lh=ones[0:1, 0:128], rh=br_s[0:1, :]:
                         e.matmul(o, lh, rh, start=False, stop=True))
            p.set_guard(("bank", 6), tagt, p.c(tagt))
            p.set_guard(("bank", 7), tagt, p.c(tagt))
            p.set_guard(("bank", 4), tagr, p.c(tagr))
            p.set_guard(("bank", 5), tagr, p.c(tagr))

    p.mark("P5ln1")
    n_pr = p.c("s_pr")
    for eng in ("tensor", "scalar", "vector"):
        p.wait(eng, "s_pr", n_pr)
    if debug:
        p.wait("scalar", "s_pr", n_pr)
        dump_fm("d_pre", xv, DCH, TT)
        p.wcur("gpsimd", "s_wm")
        dump_bf("d_on", qv, 16, TT)
        p.wcur("vector", "s_dbg")
    p.wcur("tensor", "s_rw")
    p.guard("tensor", ("bank", 2))
    ln_pass(is_ln1=True)
    n_rt = p.c("s_rt")
    p.wait("vector", "s_rt", n_rt)
    mxv_ = mxs[:, 0:8]
    esl = mxs[:, 8:40]
    sms = mxs[:, 40:48]
    inv = mxs[:, 48:56]
    mxe = mxs[:, 56:64]
    tpw = mxs[:, 64:72]
    msc = mxs[:, 72:104]
    for ti in range(8):
        p.op("vector", None,
             lambda e, o=mxv_[:, ti:ti + 1],
             i=pb[2][:, ti * NE:(ti + 1) * NE]:
             e.reduce_max(o, i, axis=AX.X))
    for ti in range(8):
        p.op("vector", "s_sm" if ti == 7 else None,
             lambda e, o=esl[:, ti * 4:(ti + 1) * 4],
             i=pb[2][:, ti * NE:(ti + 1) * NE], m=mxv_[:, ti:ti + 1]:
             e.tensor_scalar_sub(o, i, m))
    sm0 = p.c("s_sm")
    p.set_guard(("bank", 2), "s_sm", sm0)
    p.wait("scalar", "s_sm", sm0)
    se = p.op("scalar", "s_sme",
              lambda e, o=esl[:, :], i=esl[:, :]: e.activation(o, i, AF.Exp))
    p.wait("vector", "s_sme", se)
    for ti in range(8):
        p.op("vector", None,
             lambda e, o=sms[:, ti:ti + 1], i=esl[:, ti * 4:(ti + 1) * 4]:
             e.reduce_sum(o, i, axis=AX.X))
    for ti in range(8):
        p.op("vector", None,
             lambda e, o=mxe[:, ti:ti + 1], i=esl[:, ti * 4:(ti + 1) * 4]:
             e.reduce_max(o, i, axis=AX.X))
    for ti in range(8):
        p.op("vector", None,
             lambda e, o=inv[:, ti:ti + 1], i=sms[:, ti:ti + 1]:
             e.reciprocal(o, i))
    for ti in range(8):
        p.op("vector", None,
             lambda e, o=tpw[:, ti:ti + 1], a=mxe[:, ti:ti + 1],
             b_=inv[:, ti:ti + 1]: e.tensor_mul(o, a, b_))
    for ti in range(8):
        p.op("vector", None,
             lambda e, o=msc[:, ti * 4:(ti + 1) * 4],
             i=esl[:, ti * 4:(ti + 1) * 4], m=mxe[:, ti:ti + 1]:
             e.tensor_single_scalar(o, i, m, ALU.is_ge))
    for ti in range(8):
        p.op("vector", "s_sm" if ti == 7 else None,
             lambda e, o=msc[:, ti * 4:(ti + 1) * 4],
             i=msc[:, ti * 4:(ti + 1) * 4], m=tpw[:, ti:ti + 1]:
             e.tensor_scalar_mul(o, i, m))
    smn = p.c("s_sm")
    p.wait("tensor", "s_sm", smn)
    p.guard("tensor", ("bank", 3))
    p.guard("tensor", ("bank", 4))
    for ti in range(8):
        bi = 3 if ti < 4 else 4
        p.op("tensor", "s_mt" if ti % 4 == 3 else None,
             lambda e, o=pb[bi][0:4, (ti % 4) * 128:(ti % 4 + 1) * 128],
             i=msc[:, ti * 4:(ti + 1) * 4]:
             e.transpose(o, i, identf[:, :]))
    nmt = p.c("s_mt")
    p.wait("vector", "s_mt", nmt)
    p.op("vector", None,
         lambda e, o=maskfm[0:4, 0:512], a=pb[3][0:4, 0:512]:
         e.tensor_copy(o, a))
    mf = p.op("vector", "s_mf",
              lambda e, o=maskfm[0:4, 512:1024], a=pb[4][0:4, 0:512]:
              e.tensor_copy(o, a))
    p.set_guard(("bank", 3), "s_mf", mf)
    p.set_guard(("bank", 4), "s_mf", mf)

    if debug:
        p.wcur("scalar", "s_l1t")
        dump_fm("d_y1", xv, DCH, TT)
        p.wcur("scalar", "s_sm")
        p.dma("scalar", "s_dbg", dbg_dram("d_msc", [128, 112])[:, :],
              mxs[:, :]) if debug else None
        p.wcur("vector", "s_dbg")

    p.mark("P6ffn")
    n_l1b = p.c("s_l1b")
    p.wait("tensor", "s_l1b", n_l1b)
    p.wait("vector", "s_mf", p.c("s_mf"))
    p.wcur("gpsimd", "s_wm")
    p.wcur("gpsimd", "s_pm")
    p.wcur("gpsimd", "s_om")
    p.wcur("gpsimd", "s_ds")
    p.wcur("gpsimd", "s_vte")
    p.wcur("gpsimd", "s_pr")
    ldwg = {}
    ldwu = {}
    state6 = {"pend": None, "ldwd": None}

    def emit_maskrep(e_):
        for half in range(2):
            p.guard("tensor", ("bank", 7))
            p.wcur("tensor", "s_mf")
            mr = p.op("tensor", "s_mr",
                      lambda e, o=pb[7][:, 0:HFW],
                      lh=sel4[0:NE, e_ * 128:(e_ + 1) * 128],
                      rh=maskfm[0:NE, half * HFW:(half + 1) * HFW]:
                      e.matmul(o, lh, rh, start=True, stop=True))
            p.wait("vector", "s_mr", mr)
            mre = p.op("vector", "s_mre",
                       lambda e, o=mask_rep[:, half * HFW:(half + 1) * HFW],
                       a=pb[7][:, 0:HFW]:
                       e.tensor_copy(o, a))
            p.set_guard(("bank", 7), "s_mre", mre)

    def emit_down(part):
        (de, dblk, dbpar, dldwd, dnhv) = state6["pend"]
        if part == 0:
            p.wait("tensor", "s_hv", dnhv)
            p.wait("tensor", "s_fw", dldwd)
        rng = range(0, DCH // 2) if part == 0 else range(DCH // 2, DCH)
        for d in rng:
            for half in range(2):
                bi = 4 + (d * 2 + half) % 3
                bank = pb[bi]
                p.guard("tensor", ("bank", bi))
                for f in range(FBLK):
                    last = f == FBLK - 1
                    p.op("tensor", "s_dn" if last else None,
                         lambda e, o=bank[:, 0:HFW],
                         lh=wdtile[:, f, d * 128:(d + 1) * 128],
                         rh=hbuf[:, dbpar * FBLK + f,
                                 half * HFW:(half + 1) * HFW],
                         s=(f == 0), z=last:
                         e.matmul(o, lh, rh, start=s, stop=z))
                dn = p.c("s_dn")
                p.wait("vector", "s_dn", dn)
                da = p.op("vector", "s_da",
                          lambda e,
                          o=xvr[:, d, half * HFW:(half + 1) * HFW],
                          i=xv[:, d, half * HFW:(half + 1) * HFW],
                          a=bank[:, 0:HFW]: e.tensor_add(o, i, a))
                p.set_guard(("bank", bi), "s_da", da)
        if part == 1:
            p.set_guard(("wd",), "s_dn", p.c("s_dn"))
            p.set_guard(("h", dbpar), "s_dn", p.c("s_dn"))
            state6["pend"] = None

    for e_ in range(NE):
        emit_maskrep(e_)
        for blk in range(NBLK):
            bpar = blk % 2
            for f2 in range(FBLK):
                fi2 = blk * FBLK + f2
                gi = e_ * 64 + fi2
                tpar = fi2 % 2
                p.wait("gpsimd", "s_gu", 2 * (gi - 1))
                ldwg[fi2] = p.dma(
                    "gpsimd", "s_fw", wgtile[:, tpar, :, :],
                    wgl_c[e_][:, :, fi2 * 128:(fi2 + 1) * 128])
                ldwu[fi2] = p.dma(
                    "gpsimd", "s_fw", wutile[:, tpar, :, :],
                    wul_c[e_][:, :, fi2 * 128:(fi2 + 1) * 128])
            p.guard("vector", ("h", bpar))
            for f in range(FBLK):
                fi = blk * FBLK + f
                tpar = fi % 2
                fo = 0
                p.wait("tensor", "s_fw", ldwu[fi])
                for half in range(2):
                    bgi, bui = half, 2 + half
                    p.guard("tensor", ("bank", bgi))
                    p.guard("tensor", ("bank", bui))
                    for d in range(DCH):
                        last = d == DCH - 1
                        p.op("tensor", None,
                             lambda e, o=pb[bgi][:, 0:HFW],
                             lh=wgtile[:, tpar, d, fo:fo + 128],
                             rh=y1bf[:, d, half * HFW:(half + 1) * HFW],
                             s=(d == 0), z=last:
                             e.matmul(o, lh, rh, start=s, stop=z))
                    for d in range(DCH):
                        last = d == DCH - 1
                        p.op("tensor", "s_gu" if last else None,
                             lambda e, o=pb[bui][:, 0:HFW],
                             lh=wutile[:, tpar, d, fo:fo + 128],
                             rh=y1bf[:, d, half * HFW:(half + 1) * HFW],
                             s=(d == 0), z=last:
                             e.matmul(o, lh, rh, start=s, stop=z))
                    gu = p.c("s_gu")
                    sgsl = sgscr[:, ((f % 2) * 2 + half) * 512:
                                 ((f % 2) * 2 + half + 1) * 512]
                    p.wait("scalar", "s_gu", gu)
                    p.guard("scalar", ("sg", (f % 2) * 2 + half))
                    hh = p.op("scalar", "s_hh",
                              lambda e, o=sgsl, a=pb[bgi][:, 0:HFW]:
                              e.activation(o, a, AF.Silu))
                    p.set_guard(("bank", bgi), "s_hh", hh)
                    p.wait("vector", "s_hh", hh)
                    p.op("vector", None,
                         lambda e, o=sgsl, i=sgsl,
                         m=mask_rep[:, half * HFW:(half + 1) * HFW]:
                         e.tensor_mul(o, i, m))
                    hv = p.op("vector", "s_hv",
                              lambda e,
                              o=hbuf[:, bpar * FBLK + f,
                                     half * HFW:(half + 1) * HFW],
                              i=sgsl, a=pb[bui][:, 0:HFW]:
                              e.tensor_mul(o, i, a))
                    p.set_guard(("bank", bui), "s_hv", hv)
                    p.set_guard(("sg", (f % 2) * 2 + half), "s_hv", hv)
                if f == 0 and state6["pend"] is not None:
                    emit_down(0)
                if f == 1:
                    if state6["pend"] is not None:
                        emit_down(1)
                    p.guard("gpsimd", ("wd",))
                    state6["ldwd"] = p.dma(
                        "gpsimd", "s_fw", wdtile[:, :, :],
                        wdl_r[e_][:, blk * FBLK:(blk + 1) * FBLK, :])
            state6["pend"] = (e_, blk, bpar, state6["ldwd"], p.c("s_hv"))
    emit_down(0)
    emit_down(1)

    p.mark("P7ln2")
    n_da = p.c("s_da")
    for eng in ("tensor", "scalar", "vector"):
        p.wait(eng, "s_da", n_da)
    if debug:
        p.wait("scalar", "s_da", n_da)
        dump_fm("d_y2a", xv, DCH, TT)
        p.wcur("vector", "s_dbg")
    p.wcur("scalar", "s_hv")
    p.wcur("vector", "s_hv")
    ln_pass(is_ln1=False)

    p.mark("P8out")
    n_l2t = p.c("s_l2t")
    p.wait("tensor", "s_l2t", n_l2t)
    p.wcur("vector", "s_mre")
    for ti in range(8):
        p.guard("vector", ("ytm",))
        for k4 in range(4):
            bi = k4
            p.guard("tensor", ("bank", bi))
            for dd in range(4):
                d = k4 * 4 + dd
                p.op("tensor", "s_tr" if dd == 3 else None,
                     lambda e, o=pb[bi][:, dd * 128:(dd + 1) * 128],
                     i=xv[:, d, ti * 128:(ti + 1) * 128]:
                     e.transpose(o, i, identf[:, :]))
            ntr = p.c("s_tr")
            p.wait("vector", "s_tr", ntr)
            te = p.op("vector", "s_te",
                      lambda e, o=ytm[:, k4 * 512:(k4 + 1) * 512],
                      a=pb[bi][:, 0:HFW]: e.tensor_copy(o, a))
            p.set_guard(("bank", bi), "s_te", te)
        nte = p.c("s_te")
        p.wait("sync", "s_te", nte)
        od = p.dma("sync", "s_out", yout[ti * 128:(ti + 1) * 128, :],
                   ytm[:, :])
        p.set_guard(("ytm",), "s_out", od)
    p.wait("sync", "s_out", p.c("s_out"))
    if debug:
        p.wcur("sync", "s_dbg")
        p.wcur("scalar", "s_dbg")
        p.wcur("gpsimd", "s_dbg")

    with nc.Block() as block:
        @block.sync
        def _(e):
            p.replay("sync", e)

        @block.scalar
        def _(e):
            p.replay("scalar", e)

        @block.vector
        def _(e):
            p.replay("vector", e)

        @block.tensor
        def _(e):
            p.replay("tensor", e)

        @block.gpsimd
        def _(e):
            p.replay("gpsimd", e)
    return nc, st


def kernel(**inputs):
    f32 = np.float32

    def g(name):
        return np.ascontiguousarray(np.asarray(inputs[name], f32))

    x = g("L0_attn_hidden_states")
    shared = {
        "gnwd": g("L0_attn_g_norm_w").reshape(128, 1),
        "bqt": np.ascontiguousarray(g("L0_attn_bq").reshape(16, 128).T),
        "bkt": np.ascontiguousarray(g("L0_attn_bk").reshape(8, 128).T),
        "bvt": np.ascontiguousarray(g("L0_attn_bv").reshape(8, 128).T),
        "bg2t": np.ascontiguousarray(g("L0_attn_bg2").reshape(8, 128).T),
        "onesd": np.ones((1, 128), f32),
        "identd": np.eye(128, dtype=f32),
        "amaskd": np.triu(np.ones((128, 128), f32)),
        "wq": g("L0_attn_Wq"),
        "wk": g("L0_attn_Wk"),
        "wv": g("L0_attn_Wv"),
        "wg1d": np.ascontiguousarray(
            g("L0_attn_Wg1").reshape(DCH, 128, GATE).transpose(1, 0, 2)),
        "wg2d": g("L0_attn_Wg2"),
        "wo": g("L0_attn_Wo"),
        "wrd": np.ascontiguousarray(
            g("L0_ffn_Wr").reshape(DCH, 128, NE).transpose(1, 0, 2)),
        "brd": g("L0_ffn_br").reshape(1, NE),
        "seld": np.repeat(np.eye(NE, dtype=f32), 128, axis=1).reshape(
            NE, NE * 128),
        "wgl": g("L0_ffn_Wg"),
        "wul": g("L0_ffn_Wu"),
        "wdl": g("L0_ffn_Wd"),
    }
    in_maps = []
    for c in range(8):
        b, hf = divmod(c, 2)
        m = dict(shared)
        m["xt"] = np.ascontiguousarray(x[b, hf * TT:(hf + 1) * TT, :].T)
        m["xpt"] = np.ascontiguousarray(x[b, 0:TT, :].T)
        m["flagd"] = np.full((128, 1), float(hf), f32)
        in_maps.append(m)
    debug = bool(inputs.pop("_debug", False))
    nc, st = build_program(debug=debug)
    res = run_bass_kernel_spmd(nc, in_maps, core_ids=list(range(8)))
    if debug:
        kernel._dbg = res.results
    st.close()
    out = np.empty((B, L, D), f32)
    for c in range(8):
        b, hf = divmod(c, 2)
        out[b, hf * TT:(hf + 1) * TT, :] = res.results[c]["y"]
    return out


# version 21
# speedup vs baseline: 1.1357x; 1.0546x over previous
import contextlib
import numpy as np
import concourse.bass as bass
import concourse.mybir as mybir
from concourse.bass_utils import run_bass_kernel_spmd

F32 = mybir.dt.float32
F32R = mybir.dt.float32r
BF16 = mybir.dt.bfloat16
AF = mybir.ActivationFunctionType
ALU = mybir.AluOpType
AX = mybir.AxisListType

B, L, D = 4, 2048, 2048
TT = 1024
HFW = 512
CH = 128
NCH = TT // CH
KVH, QHN, HD = 8, 16, 128
DCH = 16
GATE = 16
FF = 8192
NE = 4
FBLK = 4
NBLK = FF // 128 // FBLK
QSCALE = float(HD) ** -0.5
EPS = 1e-5

ENGS = ("sync", "scalar", "vector", "tensor", "gpsimd")


class P:

    def __init__(self, nc, stack):
        self.nc = nc
        self.ops = {e: [] for e in ENGS}
        self.sems = {}
        self.cnt = {}
        self.stack = stack
        self.guards = {}

    def sem(self, name):
        if name not in self.sems:
            self.sems[name] = self.stack.enter_context(self.nc.semaphore(name))
            self.cnt[name] = 0
        return self.sems[name]

    def c(self, name):
        self.sem(name)
        return self.cnt[name]

    def wait(self, eng, name, val):
        if val is None or val <= 0:
            return
        h = self.sem(name)
        self.ops[eng].append(lambda e, h=h, v=val: e.wait_ge(h, v))

    def wcur(self, eng, name):
        self.wait(eng, name, self.c(name))

    def guard(self, eng, key):
        g = self.guards.get(key)
        if g is None:
            return
        for (nm, v) in (g if isinstance(g, list) else [g]):
            self.wait(eng, nm, v)

    def set_guard(self, key, name, val):
        self.guards[key] = (name, val)

    def dma(self, eng, name, out, in_, **kw):
        h = self.sem(name)
        self.cnt[name] += 16
        self.ops[eng].append(
            lambda e, o=out, i=in_, h=h, kw=kw:
            e.dma_start(o, i, **kw).then_inc(h, 16))
        return self.cnt[name]

    def op(self, eng, name, fn):
        if name is None:
            self.ops[eng].append(fn)
            return None
        h = self.sem(name)
        self.cnt[name] += 1
        self.ops[eng].append(lambda e, fn=fn, h=h: fn(e).then_inc(h, 1))
        return self.cnt[name]

    def mark(self, phase):
        if not hasattr(self, "marks"):
            self.marks = []
        self.marks.append((phase, {e: len(self.ops[e]) for e in ENGS}))

    def replay(self, eng, e):
        bounds = []
        for (ph, idx) in getattr(self, "marks", []):
            bounds.append((idx[eng], ph))
        cur = None
        for i, f in enumerate(self.ops[eng]):
            for (bi, ph) in bounds:
                if bi == i:
                    if cur is not None:
                        self.nc.leave_named_scope(cur[0], cur[1], False)
                    sid, _ = self.nc.enter_named_scope(ph + "_" + eng, False)
                    cur = (ph + "_" + eng, sid)
            f(e)
        if cur is not None:
            self.nc.leave_named_scope(cur[0], cur[1], False)


def r32(ap):
    return ap.bitcast(F32R)


def build_program(debug=False):
    nc = bass.Bass("TRN2", target_bir_lowering=False)
    st = contextlib.ExitStack()

    def din(name, shape):
        return nc.dram_tensor(name, shape, F32, kind="ExternalInput")

    xt = din("xt", [D, TT])
    xpt = din("xpt", [D, TT])
    flagd = din("flagd", [128, 1])
    gnwd = din("gnwd", [128, 1])
    bqt = din("bqt", [128, 16])
    bkt = din("bkt", [128, 8])
    bvt = din("bvt", [128, 8])
    bg2t = din("bg2t", [128, 8])
    onesd = din("onesd", [1, 128])
    identd = din("identd", [128, 128])
    amaskd = din("amaskd", [128, 128])
    wq = din("wq", [D, D])
    wk = din("wk", [D, 1024])
    wv = din("wv", [D, 1024])
    wg1d = din("wg1d", [128, DCH, GATE])
    wg2d = din("wg2d", [GATE, 1024])
    wo = din("wo", [D, D])
    wrd = din("wrd", [128, DCH, NE])
    brd = din("brd", [1, NE])
    seld = din("seld", [NE, NE * 128])
    wgl = din("wgl", [NE, D, FF])
    wul = din("wul", [NE, D, FF])
    wdl = din("wdl", [NE, FF, D])
    yout = nc.dram_tensor("y", [TT, D], F32, kind="ExternalOutput")
    chk_d = nc.dram_tensor("chk_d", [NCH, 128, KVH * HD], BF16,
                           kind="Internal")
    dbg_t = {}

    def dbg_dram(name, shape):
        if name not in dbg_t:
            dbg_t[name] = nc.dram_tensor(name, shape, F32,
                                         kind="ExternalOutput")
        return dbg_t[name]

    xt_r = xt.rearrange("(d p) t -> p d t", p=128)
    xpt_r = xpt.rearrange("(d p) t -> p d t", p=128)
    wq_c = wq.rearrange("(d p) m -> p d m", p=128)
    wk_c = wk.rearrange("(d p) m -> p d m", p=128)
    wv_c = wv.rearrange("(d p) m -> p d m", p=128)
    wo_r = wo.rearrange("(h p) m -> p h m", p=128)
    wgl_c = [wgl[e].rearrange("(d p) m -> p d m", p=128) for e in range(NE)]
    wul_c = [wul[e].rearrange("(d p) m -> p d m", p=128) for e in range(NE)]
    wdl_r = [wdl[e].rearrange("(c p) m -> p c m", p=128) for e in range(NE)]

    def sb(name, shape, dt):
        return st.enter_context(nc.sbuf_tensor(name, shape, dt))

    ones = sb("ones", [1, 128], F32)
    onescol = sb("onescol", [128, 1], F32)
    onescol_b = sb("onescol_b", [128, 1], BF16)
    identf = sb("identf", [128, 128], F32)
    zeros = sb("zeros", [128, 128], F32)
    bq_s = sb("bq_s", [128, 16], F32)
    bk_s = sb("bk_s", [128, 8], F32)
    bv_s = sb("bv_s", [128, 8], F32)
    bg2p = sb("bg2p", [128, 8], F32)
    bg2n = sb("bg2n", [128, 8], F32)
    gnws = sb("gnws", [128, 1], F32)
    flags = sb("flags", [128, 1], F32)
    epsc = sb("epsc", [128, 1], F32)
    wr_s = sb("wr_s", [128, DCH, NE], F32)
    br_s = sb("br_s", [1, NE], F32)
    sel4 = sb("sel4", [NE, NE * 128], F32)
    wg1_s = sb("wg1_s", [128, DCH, GATE], F32R)
    small_s = sb("small_s", [GATE, 2048], F32)
    xg1s = small_s[:, 0:1024]
    wg2_s = small_s[:, 1024:2048]
    rows3 = small_s[0:1, 0:1024]
    rows5 = small_s[0:1, 0:1024]
    mxs = sb("mxs", [128, 112], F32)
    maskfm = small_s[0:4, 1024:2048]

    ar1 = sb("ar1", [128, DCH, TT], F32)
    ar2 = sb("ar2", [128, 16, TT], BF16)
    ar3 = sb("ar3", [128, 8, TT], BF16)
    ar4 = sb("ar4", [128, 8, TT], BF16)
    ar5 = sb("ar5", [128, 8, TT], BF16)
    ar6 = sb("ar6", [128, 4096], F32)
    ar7 = sb("ar7", [128, 4096], F32R)
    ar8 = sb("ar8", [128, 3088], F32)
    s_t = sb("s_t", [128, KVH, HD], F32)
    s_bf = sb("s_bf", [128, KVH, HD], BF16)

    xv = ar1
    qv = ar2
    y1bf = ar2
    kv = ar3
    hbuf = ar3
    vv = ar4
    wdtile = ar4.rearrange("p h t -> p (h t)").rearrange(
        "p (c m) -> p c m", c=FBLK)
    vfm = ar5
    gkv = ar5
    wo4 = ar5.rearrange("p h t -> p (h t)").rearrange(
        "p (s m) -> p s m", s=4)
    ar6b = ar6.bitcast(BF16)
    wgtile = ar6b[:, 0:4096].rearrange("p (b d m) -> p b d m", b=2, d=DCH)
    wutile = ar6b[:, 4096:8192].rearrange("p (b d m) -> p b d m", b=2, d=DCH)
    wstrr = ar7.rearrange("p (b d m) -> p b d m", b=2, d=DCH)
    xvr = ar1.bitcast(F32R)

    def onv(qh):
        return qv[:, qh, :]

    gbw = ar8
    gbb = ar8.bitcast(BF16)
    b8 = gbw[:, 0:1024].rearrange("p (h c) -> p h c", h=8)
    am8 = gbb[:, 0:2048]
    of8 = gbb[:, 0:2048]
    expb8 = gbw[:, 1024:2048].rearrange("p (h c) -> p h c", h=8)
    sq8 = gbb[:, 2048:4096]
    expnb8 = gbw[:, 2048:3072].rearrange("p (h c) -> p h c", h=8)
    qs8 = gbb[:, 4096:6144]
    ebl8 = gbw[:, 3072:3088]
    a6b = ar6.bitcast(BF16)
    ks8 = a6b[:, 0:1024].rearrange("p (h c) -> p h c", h=8)
    ksT8 = a6b[:, 1024:2048].rearrange("p (h c) -> p h c", h=8)
    amask = ar6[:, 2048:2176]
    identb = a6b[:, 4352:4480]
    chk_sb = [a6b[:, 2048 + 1024 * i:3072 + 1024 * i].rearrange(
        "p (h c) -> p h c", h=8) for i in range(2)]
    sq8p2 = a6b[:, 4480:6528]
    repsb = ar6[:, 3264:3776]

    gkch = [ar8[:, 0:1024], ar8[:, 1024:2048]]
    y1scr = ar8[:, 0:512]
    sqscr = [ar8[:, 1024:1536], ar8[:, 1536:2048]]
    mask_rep = ar8[:, 0:1024]
    sgscr = ar8.bitcast(BF16)[:, 2048:4096]
    ytm = ar8[:, 0:2048]

    pb = [st.enter_context(nc.psum_tensor(f"pb{i}", [128, HFW], F32))
          for i in range(8)]

    p = P(nc, st)

    for (dst, src) in [(ones, onesd), (identf, identd), (amask, amaskd),
                       (bq_s, bqt), (bk_s, bkt), (bv_s, bvt), (bg2p, bg2t),
                       (gnws, gnwd), (flags, flagd), (wr_s, wrd),
                       (br_s, brd), (sel4, seld),
                       (wg2_s, wg2d)]:
        nd = len(dst.shape)
        p.dma("sync", "s_c", dst[(slice(None),) * nd], src[(slice(None),) * nd])
    n_c = p.c("s_c")
    p.dma("gpsimd", "s_cb", identb[:, :], identd[:, :])
    p.dma("gpsimd", "s_cb", wg1_s[:, :, :], wg1d[:, :, :])
    for d in range(DCH):
        p.dma("gpsimd", "s_x", xvr[:, d, :], xpt_r[:, d, :])
    n_xp = p.c("s_x")
    p.wait("vector", "s_c", n_c)
    p.op("vector", "s_c0", lambda e: e.memset(zeros[:, :], 0.0))
    p.op("vector", "s_c0", lambda e: e.memset(onescol[:, :], 1.0))
    p.op("vector", "s_c0", lambda e: e.memset(onescol_b[:, :], 1.0))
    p.op("vector", "s_c0", lambda e: e.memset(epsc[:, :], EPS))
    p.op("vector", "s_c0",
         lambda e: e.memset(s_t.rearrange("p h t -> p (h t)"), 0.0))
    p.op("vector", "s_c0",
         lambda e: e.tensor_scalar_mul(bq_s[:, :], bq_s[:, :], QSCALE))
    p.op("vector", "s_c0",
         lambda e: e.tensor_scalar_mul(bg2n[:, :], bg2p[:, :], -1.0))
    n_c0 = p.c("s_c0")
    for eng in ("tensor", "scalar", "gpsimd"):
        p.wait(eng, "s_c0", n_c0)

    wu = {"u": 0, "mm": {}}

    def proj_matmuls(w_c, j, xload, dst, func, bias, scale):
        u = wu["u"]
        wu["u"] += 1
        buf = u % 2
        if u - 2 in wu["mm"]:
            p.wait("gpsimd", "s_pm", wu["mm"][u - 2])
        ld = p.dma("gpsimd", "s_wa", wstrr[:, buf, :, 0:128],
                   w_c[:, :, j * 128:(j + 1) * 128])
        p.wait("tensor", "s_wa", ld)
        p.wait("tensor", "s_x", xload)
        bis = [0 + (u % 2) * 2, 1 + (u % 2) * 2]
        for bi in bis:
            p.guard("tensor", ("bank", bi))
        for half, bi in ((0, bis[0]), (1, bis[1])):
            bank = pb[bi]
            for d in range(DCH):
                last = d == DCH - 1
                p.op("tensor", "s_pm" if last else None,
                     lambda e, bank=bank, half=half, buf=buf, d=d,
                     s=(d == 0), z=last:
                     e.matmul(bank[:, 0:HFW], wstrr[:, buf, d, :],
                              xvr[:, d, half * HFW:(half + 1) * HFW],
                              start=s, stop=z))
        wu["mm"][u] = p.c("s_pm")
        for half, bi in ((0, bis[0]), (1, bis[1])):
            bank = pb[bi]
            p.wcur("scalar", "s_pm")
            ev = p.op("scalar", "s_pe",
                      lambda e, a=bank[:, 0:HFW],
                      o=dst[:, j, half * HFW:(half + 1) * HFW],
                      f=func, sc=scale, bi_=bias[:, j:j + 1]:
                      e.activation(o, a, f, scale=sc, bias=bi_))
            p.set_guard(("bank", bi), "s_pe", ev)

    def proj_pass(xload, do_q):
        for j in range(KVH):
            proj_matmuls(wk_c, j, xload, kv, AF.Relu, bk_s, 1.0)
        for j in range(KVH):
            proj_matmuls(wv_c, j, xload, vfm, AF.Identity, bv_s, 1.0)
        p.wcur("tensor", "s_pe")
        p.wait("tensor", "s_cb", 16)
        for tc in range(NCH):
            bi = 4 + tc % 2
            bank = pb[bi]
            p.guard("tensor", ("bank", bi))
            bankb = bank.bitcast(BF16)
            for h in range(KVH):
                p.op("tensor", "s_vt" if h == KVH - 1 else None,
                     lambda e, o=bankb[:, h * 128:(h + 1) * 128],
                     i=vfm[:, h, tc * 128:(tc + 1) * 128]:
                     e.transpose(o, i, identb[:, :]))
            p.wcur("vector", "s_vt")
            ev = p.op("vector", "s_vte",
                      lambda e, o=vv[:, tc, :], a=bankb[:, 0:1024]:
                      e.tensor_copy(o, a))
            p.set_guard(("bank", bi), "s_vte", ev)
        p.wcur("tensor", "s_vte")
        for half in range(2):
            p.guard("tensor", ("bank", 6))
            for d in range(DCH):
                last = d == DCH - 1
                p.op("tensor", "s_pm" if last else None,
                     lambda e, d=d, half=half, s=(d == 0), z=last:
                     e.matmul(pb[6][0:GATE, 0:HFW], wg1_s[:, d, :],
                              xvr[:, d, half * HFW:(half + 1) * HFW],
                              start=s, stop=z))
            p.wcur("vector", "s_pm")
            p.guard("vector", ("xg1s", half))
            ev = p.op("vector", "s_g1",
                      lambda e, half=half:
                      e.tensor_copy(xg1s[0:GATE, half * HFW:(half + 1) * HFW],
                                    pb[6][0:GATE, 0:HFW]))
            p.set_guard(("bank", 6), "s_g1", ev)
        for j in range(KVH):
            for half in range(2):
                par = (j * 2 + half) % 2
                bi = 5 + 2 * par
                bank = pb[bi]
                p.guard("tensor", ("bank", bi))
                p.wcur("tensor", "s_g1")
                gm = p.op("tensor", "s_gm",
                          lambda e, j=j, half=half, bank=bank:
                          e.matmul(bank[:, 0:HFW],
                                   wg2_s[0:GATE, j * 128:(j + 1) * 128],
                                   xg1s[0:GATE,
                                        half * HFW:(half + 1) * HFW],
                                   start=True, stop=True))
                p.set_guard(("xg1s", half), "s_gm", gm)
                sa = gkch[par][:, 0:512]
                sbb = gkch[par][:, 512:1024]
                p.wait("scalar", "s_gm", gm)
                p.guard("scalar", ("gkch", par))
                p.op("scalar", None,
                     lambda e, a=bank[:, 0:HFW], o=sa, j=j:
                     e.activation(o, a, AF.Relu, scale=-1.0,
                                  bias=bg2n[:, j:j + 1]))
                ab = p.op("scalar", "s_gk",
                          lambda e, a=bank[:, 0:HFW], o=sbb, j=j:
                          e.activation(o, a, AF.Abs, bias=bg2p[:, j:j + 1]))
                p.set_guard(("bank", bi), "s_gk", ab)
                p.op("scalar", None,
                     lambda e, o=sbb: e.activation(o, o, AF.Exp, scale=-1.0))
                gk2 = p.op("scalar", "s_gk",
                           lambda e, o=sbb:
                           e.activation(o, o, AF.Ln, bias=1.0))
                p.wait("vector", "s_gk", gk2)
                p.op("vector", None,
                     lambda e, a=sa, b_=sbb: e.tensor_add(a, a, b_))
                gv_ = p.op("vector", "s_gv",
                           lambda e, a=sa,
                           o=gkv[:, j, half * HFW:(half + 1) * HFW]:
                           e.tensor_scalar_mul(o, a, -1.0 / 16.0))
                p.set_guard(("gkch", par), "s_gv", gv_)
        if do_q:
            for j in range(QHN):
                proj_matmuls(wq_c, j, xload, qv, AF.Relu, bq_s, QSCALE)

    def scan_chunk_b(c, full, sadd_wait=None):
        cs0, cs1 = c * CH, (c + 1) * CH
        p.guard("vector", ("gR1w",))
        for h in range(KVH):
            p.op("vector", "s_sc" if h == KVH - 1 else None,
                 lambda e, o=b8[:, h, :], i=gkv[:, h, cs0:cs1]:
                 e.tensor_tensor_scan(o, i, zeros[:, :], 0.0,
                                      ALU.add, ALU.add))
        nsc = p.c("s_sc")
        p.wait("scalar", "s_sc", nsc)
        p.guard("scalar", ("gR2w",))
        p.guard("scalar", ("gR3w",))
        p.guard("scalar", ("gEblw",))
        if full:
            p.op("scalar", None,
                 lambda e, o=expb8.rearrange("p h c -> p (h c)"),
                 i=b8.rearrange("p h c -> p (h c)"):
                 e.activation(o, i, AF.Exp))
        p.op("scalar", None,
             lambda e, o=expnb8.rearrange("p h c -> p (h c)"),
             i=b8.rearrange("p h c -> p (h c)"):
             e.activation(o, i, AF.Exp, scale=-1.0))
        ex = p.op("scalar", "s_ex",
                  lambda e, o=ebl8[:, 0:8], i=b8[:, :, CH - 1]:
                  e.activation(o, i, AF.Exp))
        p.guards[("gR1w",)] = [("s_ex", ex)]
        p.wait("vector", "s_ex", ex)
        p.guard("vector", ("gKSw",))
        p.op("vector", "s_qk",
             lambda e, o=ks8[:, :, :], a=kv[:, :, cs0:cs1],
             b_=expnb8[:, :, :]: e.tensor_mul(o, a, b_))
        if full:
            p.op("vector", "s_qk",
                 lambda e,
                 o=qs8.rearrange("p (h q c) -> p h q c", h=8, q=2),
                 a=qv[:, :, cs0:cs1].rearrange("p (h q) c -> p h q c", q=2),
                 b_=expb8[:, :, :].unsqueeze(2).broadcast_to(
                     [128, 8, 2, 128]):
                 e.tensor_mul(o, a, b_))
        nqk = p.c("s_qk")
        p.wait("tensor", "s_qk", nqk)
        p.guard("tensor", ("bank", 0))
        for h in range(KVH):
            p.op("tensor", "s_kt" if h == KVH - 1 else None,
                 lambda e, o=pb[0].bitcast(BF16)[:, h * 128:(h + 1) * 128],
                 i=ks8[:, h, :]: e.transpose(o, i, identb[:, :]))
        nkt = p.c("s_kt")
        p.wait("vector", "s_kt", nkt)
        p.guard("vector", ("gKTw",))
        kte = p.op("vector", "s_kte",
                   lambda e, o=ksT8.rearrange("p h c -> p (h c)"),
                   a=pb[0].bitcast(BF16)[:, 0:1024]: e.tensor_copy(o, a))
        p.set_guard(("bank", 0), "s_kte", kte)
        if full:
            p.wait("tensor", "s_sbf", 1 + c if c >= 1 else 1)
            for pp in range(4):
                biA = 1 + pp % 2
                p.guard("tensor", ("bank", biA))
                for hh in range(2):
                    h = pp * 2 + hh
                    p.op("tensor", "s_am" if hh == 1 else None,
                         lambda e, o=pb[biA][:, hh * 256:(hh + 1) * 256],
                         lh=ks8[:, h, :],
                         rh=qs8[:, h * 256:(h + 1) * 256]:
                         e.matmul(o, lh, rh, start=True, stop=True))
                nam = p.c("s_am")
                p.wait("vector", "s_am", nam)
                ame = p.op("vector", "s_ame",
                           lambda e,
                           o=am8[:, pp * 512:(pp + 1) * 512].rearrange(
                               "p (a c) -> p a c", a=4),
                           a=pb[biA][:, 0:512].rearrange(
                               "p (a c) -> p a c", a=4),
                           m=amask[:, :].unsqueeze(1).broadcast_to(
                               [128, 4, 128]):
                           e.tensor_mul(o, a, m))
                biO = 3 + pp % 2
                p.guard("tensor", ("bank", biO))
                p.wait("tensor", "s_ame", ame)
                for hh in range(2):
                    h = pp * 2 + hh
                    p.op("tensor", None,
                         lambda e, o=pb[biO][:, hh * 256:(hh + 1) * 256],
                         lh=vv[:, c, h * 128:(h + 1) * 128],
                         rh=am8[:, h * 256:(h + 1) * 256]:
                         e.matmul(o, lh, rh, start=True, stop=False))
                    p.op("tensor", "s_om" if hh == 1 else None,
                         lambda e, o=pb[biO][:, hh * 256:(hh + 1) * 256],
                         lh=s_bf[:, h, :],
                         rh=qs8[:, h * 256:(h + 1) * 256]:
                         e.matmul(o, lh, rh, start=False, stop=True))
                nom = p.c("s_om")
                p.wait("scalar", "s_om", nom)
                p.op("scalar", None,
                     lambda e, o=of8[:, pp * 512:(pp + 1) * 512],
                     a=pb[biO][:, 0:512]: e.activation(o, a, AF.Copy))
                oev = p.op("scalar", "s_oev",
                           lambda e, o=sq8[:, pp * 512:(pp + 1) * 512],
                           a=pb[biO][:, 0:512]:
                           e.activation(o, a, AF.Square))
                p.set_guard(("bank", biO), "s_oev", oev)
                p.wait("tensor", "s_oev", oev)
                p.guard("tensor", ("rowb",))
                rc = p.op("tensor", "s_rc",
                          lambda e, o=pb[7][0:1, 0:512],
                          lh=onescol_b[:, :],
                          rh=sq8[:, pp * 512:(pp + 1) * 512]:
                          e.matmul(o, lh, rh, start=True, stop=True))
                p.wait("scalar", "s_rc", rc)
                p.op("scalar", None,
                     lambda e,
                     o=rows3[0:1, (pp % 2) * 512:(pp % 2) * 512 + 512],
                     a=pb[7][0:1, 0:512]:
                     e.activation(o, a, AF.Ln, scale=1.0 / HD,
                                  bias=epsc[0:1, 0:1]))
                rw = p.op("scalar", "s_rw",
                          lambda e,
                          o=rows3[0:1, (pp % 2) * 512:(pp % 2) * 512 + 512],
                          i=rows3[0:1, (pp % 2) * 512:(pp % 2) * 512 + 512]:
                          e.activation(o, i, AF.Exp, scale=-0.5))
                p.set_guard(("rowb",), "s_rw", rw)
                p.wait("tensor", "s_rw", rw)
                rp = p.op("tensor", "s_rp",
                          lambda e, o=pb[biA][:, 0:512],
                          lh=ones[0:1, 0:128],
                          rh=rows3[0:1,
                                   (pp % 2) * 512:(pp % 2) * 512 + 512]:
                          e.matmul(o, lh, rh, start=True, stop=True))
                p.wait("vector", "s_rp", rp)
                on = p.op("vector", "s_on",
                          lambda e,
                          o=qv[:, 4 * pp:4 * pp + 4, cs0:cs1],
                          a=of8[:, pp * 512:(pp + 1) * 512].rearrange(
                              "p (a c) -> p a c", a=4),
                          r=pb[biA][:, 0:512].rearrange(
                              "p (a c) -> p a c", a=4):
                          e.tensor_mul(o, a, r))
                p.set_guard(("bank", biA), "s_on", on)
            p.guards[("gR1w",)] = [("s_oev", p.c("s_oev")),
                                   ("s_om", p.c("s_om"))]
            p.set_guard(("gR2w",), "s_rc", p.c("s_rc"))
            p.set_guard(("gR3w",), "s_om", p.c("s_om"))
            p.set_guard(("gKSw",), "s_am", p.c("s_am"))
        else:
            p.set_guard(("gKSw",), "s_kt", nkt)
        p.wait("tensor", "s_kte", kte)
        for h4 in range(2):
            bi = 5 + h4
            p.guard("tensor", ("bank", bi))
            for hh in range(4):
                h = h4 * 4 + hh
                p.op("tensor", "s_ds" if hh == 3 else None,
                     lambda e, o=pb[bi][:, hh * 128:(hh + 1) * 128],
                     lh=ksT8[:, h, :], rh=vv[:, c, h * 128:(h + 1) * 128]:
                     e.matmul(o, lh, rh, start=True, stop=True))
        nds = p.c("s_ds")
        p.set_guard(("gKTw",), "s_ds", nds)
        p.wait("vector", "s_ds", nds)
        if sadd_wait is not None:
            p.wait("vector", sadd_wait[0], sadd_wait[1])
        for h4 in range(2):
            p.op("vector", None,
                 lambda e, o=s_t[:, 4 * h4:4 * h4 + 4, :],
                 a=pb[5 + h4][:, 0:512].rearrange("p (a c) -> p a c", a=4):
                 e.tensor_add(o, o, a))
        su = p.op("vector", "s_su",
                  lambda e, o=s_t[:, :, :],
                  sc_=ebl8[:, 0:8].unsqueeze(2).broadcast_to([128, 8, 128]):
                  e.tensor_mul(o, o, sc_))
        p.set_guard(("bank", 5), "s_su", su)
        p.set_guard(("bank", 6), "s_su", su)
        p.set_guard(("gEblw",), "s_su", su)
        if full:
            p.wait("gpsimd", "s_su", su)
            p.wait("gpsimd", "s_om", p.c("s_om"))
            p.op("gpsimd", "s_sbf",
                 lambda e: e.tensor_copy(s_bf.rearrange("p h t -> p (h t)"),
                                         s_t.rearrange("p h t -> p (h t)")))

    def dump_fm(name, src_ap, nd, width):
        if not debug:
            return
        t = dbg_dram(name, [nd * 128, width])
        tr = t.rearrange("(d p) t -> p d t", p=128)
        for d in range(nd):
            p.dma("scalar", "s_dbg", tr[:, d, :], src_ap[:, d, :])

    def dump_bf(name, src_ap, nd, width):
        if not debug:
            return
        t = dbg_dram(name, [nd * 128, width])
        tr = t.rearrange("(d p) t -> p d t", p=128)
        for d in range(nd):
            p.dma("gpsimd", "s_dbg", tr[:, d, :], src_ap[:, d, :])

    p.mark("P1prefix")
    proj_pass(n_xp, do_q=False)
    for c in range(NCH):
        scan_chunk_b(c, full=False)
    p.wcur("vector", "s_su")
    fl = p.op("vector", "s_su",
              lambda e: e.tensor_scalar_mul(
                  s_t.rearrange("p h t -> p (h t)"),
                  s_t.rearrange("p h t -> p (h t)"), flags[:, :]))

    p.mark("P2proj")
    p.wcur("gpsimd", "s_pm")
    p.wcur("gpsimd", "s_gm")
    for d in range(DCH):
        p.dma("gpsimd", "s_x", xvr[:, d, :], xt_r[:, d, :])
    n_x = p.c("s_x")
    p.wcur("scalar", "s_su")
    p.wcur("scalar", "s_kte")
    proj_pass(n_x, do_q=True)

    if debug:
        p.wcur("scalar", "s_pe")
        p.wcur("scalar", "s_gv")
        p.wcur("gpsimd", "s_pe")
        p.wcur("gpsimd", "s_gv")
        p.wcur("gpsimd", "s_su")
        dump_bf("d_q", qv, 16, TT)
        dump_bf("d_k", kv, 8, TT)
        dump_bf("d_v", vv, 8, TT)
        dump_bf("d_gk", gkv, 8, TT)
        dump_fm("d_s", s_t, 8, HD)
        p.wcur("vector", "s_dbg")
        p.wcur("gpsimd", "s_dbg")

    p.mark("P3gla")
    p.wcur("vector", "s_pe")
    p.wcur("vector", "s_gv")
    p.wcur("tensor", "s_vte")
    for c in range(NCH):
        p.wcur("gpsimd", "s_su")
        p.wait("gpsimd", "s_cko", 16 * c)
        cast = p.op("gpsimd", "s_sbf",
                    lambda e: e.tensor_copy(
                        s_bf.rearrange("p h t -> p (h t)"),
                        s_t.rearrange("p h t -> p (h t)")))
        p.dma("gpsimd", "s_cko", chk_d[c],
              s_bf.rearrange("p h t -> p (h t)"))
        scan_chunk_b(c, full=False, sadd_wait=("s_sbf", cast))
    n_cko = p.c("s_cko")
    p.guards[("p2R1",)] = [("s_ex", p.c("s_ex"))]
    p.guards[("p2R3",)] = [("s_ks", 0), ("s_qk", p.c("s_qk"))]
    p.guards[("p2KS",)] = [("s_kt", p.c("s_kt"))]

    def o_front(c):
        cs0, cs1 = c * CH, (c + 1) * CH
        if debug:
            p.wcur("vector", "s_dbg")
            p.wcur("scalar", "s_dbg")
            p.wcur("tensor", "s_dbg")
        p.guard("vector", ("p2R1",))
        for h in range(KVH):
            p.op("vector", "s_sc" if h == KVH - 1 else None,
                 lambda e, o=b8[:, h, :], i=gkv[:, h, cs0:cs1]:
                 e.tensor_tensor_scan(o, i, zeros[:, :], 0.0,
                                      ALU.add, ALU.add))
        nsc = p.c("s_sc")
        p.wait("scalar", "s_sc", nsc)
        p.guard("scalar", ("p2R3",))
        p.op("scalar", "s_ex",
             lambda e, o=expnb8.rearrange("p h c -> p (h c)"),
             i=b8.rearrange("p h c -> p (h c)"):
             e.activation(o, i, AF.Exp, scale=-1.0))
        nex = p.c("s_ex")
        p.wait("vector", "s_ex", nex)
        p.guard("vector", ("p2KS",))
        ks_ = p.op("vector", "s_ks",
                   lambda e, o=ks8[:, :, :], a=kv[:, :, cs0:cs1],
                   b_=expnb8[:, :, :]: e.tensor_mul(o, a, b_))
        p.wait("scalar", "s_ks", ks_)
        p.guard("scalar", ("p2R2",))
        eb = p.op("scalar", "s_ex",
                  lambda e, o=expb8.rearrange("p h c -> p (h c)"),
                  i=b8.rearrange("p h c -> p (h c)"):
                  e.activation(o, i, AF.Exp))
        p.wait("vector", "s_ex", eb)
        qk_ = p.op("vector", "s_qk",
             lambda e,
             o=qs8.rearrange("p (h q c) -> p h q c", h=8, q=2),
             a=qv[:, :, cs0:cs1].rearrange("p (h q) c -> p h q c", q=2),
             b_=expb8[:, :, :].unsqueeze(2).broadcast_to([128, 8, 2, 128]):
             e.tensor_mul(o, a, b_))
        p.set_guard(("p2R2",), "s_qk", qk_)
        if debug and c == 0:
            p.wait("gpsimd", "s_qk", qk_)
            p.dma("gpsimd", "s_dbg", dbg_dram("d_qs", [128, 2048])[:, :],
                  qs8[:, :])
            p.dma("gpsimd", "s_dbg",
                  dbg_dram("d_ks", [128, 1024])[:, :],
                  ks8.rearrange("p h c -> p (h c)"))
        p.guard("gpsimd", ("chk", c % 2))
        p.wait("gpsimd", "s_cko", 16 * (c + 1))
        p.dma("gpsimd", "s_cki",
              chk_sb[c % 2].rearrange("p h t -> p (h t)"), chk_d[c])

    def o_mid(c):
        cs0, cs1 = c * CH, (c + 1) * CH
        p.wait("tensor", "s_qk", p.c("s_qk"))
        p.wait("tensor", "s_cki", 16 * (c + 1))
        for pp in range(4):
            biA = 1 + pp % 2
            p.guard("tensor", ("bank", biA))
            for hh in range(2):
                h = pp * 2 + hh
                p.op("tensor", "s_am" if hh == 1 else None,
                     lambda e, o=pb[biA][:, hh * 256:(hh + 1) * 256],
                     lh=ks8[:, h, :], rh=qs8[:, h * 256:(h + 1) * 256]:
                     e.matmul(o, lh, rh, start=True, stop=True))
            nam = p.c("s_am")
            p.wait("vector", "s_am", nam)
            ame = p.op("vector", "s_ame",
                       lambda e,
                       o=am8[:, pp * 512:(pp + 1) * 512].rearrange(
                           "p (a c) -> p a c", a=4),
                       a=pb[biA][:, 0:512].rearrange("p (a c) -> p a c", a=4),
                       m=amask[:, :].unsqueeze(1).broadcast_to([128, 4, 128]):
                       e.tensor_mul(o, a, m))
            p.set_guard(("bank", biA), "s_ame", ame)
            biO = 3 + pp
            p.guard("tensor", ("bank", biO))
            p.wait("tensor", "s_ame", ame)
            for hh in range(2):
                h = pp * 2 + hh
                p.op("tensor", None,
                     lambda e, o=pb[biO][:, hh * 256:(hh + 1) * 256],
                     lh=vv[:, c, h * 128:(h + 1) * 128],
                     rh=am8[:, h * 256:(h + 1) * 256]:
                     e.matmul(o, lh, rh, start=True, stop=False))
                p.op("tensor", "s_om" if hh == 1 else None,
                     lambda e, o=pb[biO][:, hh * 256:(hh + 1) * 256],
                     lh=chk_sb[c % 2][:, h, :],
                     rh=qs8[:, h * 256:(h + 1) * 256]:
                     e.matmul(o, lh, rh, start=False, stop=True))
        if debug and c == 0:
            p.wait("gpsimd", "s_ame", p.c("s_ame"))
            p.dma("gpsimd", "s_dbg", dbg_dram("d_am", [128, 2048])[:, :],
                  am8[:, :])
        p.set_guard(("p2R1",), "s_om", p.c("s_om"))
        p.set_guard(("p2R3",), "s_om", p.c("s_om"))
        p.set_guard(("p2KS",), "s_am", p.c("s_am"))
        p.set_guard(("chk", c % 2), "s_om", p.c("s_om"))

    def o_tail(c, om_end, tlvl=6):
        cs0, cs1 = c * CH, (c + 1) * CH
        if debug:
            p.wcur("scalar", "s_dbg")
            p.wcur("vector", "s_dbg")
        for pp in range(4):
            biO = 3 + pp
            biA = 1 + pp % 2
            p.wait("scalar", "s_om", om_end - 3 + pp)
            p.guard("scalar", ("p2SQ",))
            oev = p.op("scalar", "s_oev",
                       lambda e, o=sq8p2[:, pp * 512:(pp + 1) * 512],
                       a=pb[biO][:, 0:512]: e.activation(o, a, AF.Square))
            if tlvl < 2:
                continue
            p.wait("tensor", "s_oev", oev)
            p.guard("tensor", ("rowb",))
            p.guard("tensor", ("bank", 7))
            rc = p.op("tensor", "s_rc",
                      lambda e, o=pb[7][0:1, 0:512], lh=onescol_b[:, :],
                      rh=sq8p2[:, pp * 512:(pp + 1) * 512]:
                      e.matmul(o, lh, rh, start=True, stop=True))
            if tlvl < 3:
                continue
            p.wait("scalar", "s_rc", rc)
            p.op("scalar", None,
                 lambda e, o=rows3[0:1, (pp % 2) * 512:(pp % 2) * 512 + 512],
                 a=pb[7][0:1, 0:512]:
                 e.activation(o, a, AF.Ln, scale=1.0 / HD,
                              bias=epsc[0:1, 0:1]))
            rw = p.op("scalar", "s_rw",
                      lambda e,
                      o=rows3[0:1, (pp % 2) * 512:(pp % 2) * 512 + 512],
                      i=rows3[0:1, (pp % 2) * 512:(pp % 2) * 512 + 512]:
                      e.activation(o, i, AF.Exp, scale=-0.5))
            p.set_guard(("rowb",), "s_rw", rw)
            if tlvl < 4:
                continue
            p.wait("tensor", "s_rw", rw)
            p.guard("tensor", ("bank", biA))
            rp = p.op("tensor", "s_rp",
                      lambda e, o=pb[biA][:, 0:512], lh=ones[0:1, 0:128],
                      rh=rows3[0:1, (pp % 2) * 512:(pp % 2) * 512 + 512]:
                      e.matmul(o, lh, rh, start=True, stop=True))
            if tlvl < 5:
                continue
            p.wait("vector", "s_rp", rp)
            rpe = p.op("vector", "s_rpe",
                       lambda e, o=repsb[:, 0:512], a=pb[biA][:, 0:512]:
                       e.tensor_copy(o, a))
            p.set_guard(("bank", biA), "s_rpe", rpe)
            if tlvl < 6:
                continue
            ofc = p.op("scalar", "s_oev",
                       lambda e, o=of8[:, pp * 512:(pp + 1) * 512],
                       a=pb[biO][:, 0:512]: e.activation(o, a, AF.Copy))
            p.set_guard(("bank", biO), "s_oev", ofc)
            p.wait("vector", "s_oev", ofc)
            if debug and c == 0:
                p.wait("gpsimd", "s_oev", ofc)
                p.dma("gpsimd", "s_dbg",
                      dbg_dram("d_of", [128, 2048])[:,
                          pp * 512:(pp + 1) * 512],
                      of8[:, pp * 512:(pp + 1) * 512])
            on = p.op("vector", "s_on",
                      lambda e, o=qv[:, 4 * pp:4 * pp + 4, cs0:cs1],
                      a=of8[:, pp * 512:(pp + 1) * 512].rearrange(
                          "p (a c) -> p a c", a=4),
                      r=repsb[:, 0:512].rearrange("p (a c) -> p a c", a=4):
                      e.tensor_mul(o, a, r))
        p.set_guard(("p2SQ",), "s_rc", p.c("s_rc"))
        p.set_guard(("bank", 7), "s_rw", p.c("s_rw"))
        if tlvl >= 6:
            p.guards[("p2R1",)] = [("s_oev", p.c("s_oev")),
                                   ("s_on", p.c("s_on"))]

    if debug:
        t = dbg_dram("d_chk", [NCH * 128, KVH * HD])
        p.wcur("gpsimd", "s_cko")
        for c in range(NCH):
            p.dma("gpsimd", "s_dbg", t[c * 128:(c + 1) * 128, :], chk_d[c])
    om_ends = {}
    SWP2 = False
    if SWP2:
        o_front(0)
        o_mid(0)
        om_ends[0] = p.c("s_om")
        for c in range(1, NCH):
            o_front(c)
            o_tail(c - 1, om_ends[c - 1])
            o_mid(c)
            om_ends[c] = p.c("s_om")
        o_tail(NCH - 1, om_ends[NCH - 1])
    else:
        PASS2 = 3
        TLVL = 6
        for c in range(NCH):
            if PASS2 >= 1:
                o_front(c)
            if PASS2 >= 2:
                o_mid(c)
            if PASS2 >= 3:
                o_tail(c, p.c("s_om"), tlvl=TLVL)

    p.mark("P4wo")
    n_on_all = p.c("s_on")
    p.wcur("gpsimd", "s_sc")
    p.wcur("gpsimd", "s_qk")
    for hp in range(8):
        sc_ = None
        for k2 in range(2):
            h = 2 * hp + k2
            slot = (hp % 2) * 2 + k2
            p.guard("gpsimd", ("wo", slot))
            ld = p.dma("gpsimd", "s_wo", wo4[:, slot, :], wo_r[:, h, :])
            p.wait("gpsimd", "s_wo", ld)
            sc_ = p.op("gpsimd", "s_wos",
                       lambda e, t=wo4[:, slot, :]:
                       e.tensor_scalar_mul(t, t, gnws[:, :]))
        p.wait("tensor", "s_wos", sc_)
        p.wait("tensor", "s_on", n_on_all - 4 + hp // 2 + 1)
        s0 = (hp % 2) * 2
        for d in range(DCH):
            for half in range(2):
                bi = (d * 2 + half) % 4
                bank = pb[bi]
                p.guard("tensor", ("bank", bi))
                p.op("tensor", None,
                     lambda e, o=bank[:, 0:HFW],
                     lh=wo4[:, s0, d * 128:(d + 1) * 128],
                     rh=onv(2 * hp)[:, half * HFW:(half + 1) * HFW]:
                     e.matmul(o, lh, rh, start=True, stop=False))
                wm = p.op("tensor", "s_wm",
                          lambda e, o=bank[:, 0:HFW],
                          lh=wo4[:, s0 + 1, d * 128:(d + 1) * 128],
                          rh=onv(2 * hp + 1)[:, half * HFW:(half + 1) * HFW]:
                          e.matmul(o, lh, rh, start=False, stop=True))
                p.wait("vector", "s_wm", wm)
                pr = p.op("vector", "s_pr",
                          lambda e, o=xvr[:, d, half * HFW:(half + 1) * HFW],
                          i=xv[:, d, half * HFW:(half + 1) * HFW],
                          a=bank[:, 0:HFW]:
                          e.tensor_add(o, i, a))
                p.set_guard(("bank", bi), "s_pr", pr)
        p.set_guard(("wo", s0), "s_wm", p.c("s_wm"))
        p.set_guard(("wo", s0 + 1), "s_wm", p.c("s_wm"))

    def ln_pass(is_ln1):
        tagm = "s_l1m" if is_ln1 else "s_l2m"
        tagq = "s_l1q" if is_ln1 else "s_l2q"
        tagr = "s_l1r" if is_ln1 else "s_l2r"
        tagp = "s_l1p" if is_ln1 else "s_l2p"
        tagt = "s_l1t" if is_ln1 else "s_l2t"
        for half in range(2):
            p.guard("tensor", ("bank", 4))
            p.guard("tensor", ("bank", 5))
            for d in range(DCH):
                last = d == DCH - 1
                p.guard("scalar", ("sqscr", d % 2))
                sq = p.op("scalar", tagq,
                          lambda e, o=sqscr[d % 2],
                          i=xv[:, d, half * HFW:(half + 1) * HFW]:
                          e.activation(o, i, AF.Square))
                p.op("tensor", None,
                     lambda e, o=pb[4][0:1, 0:HFW], lh=onescol[:, :],
                     rh=xv[:, d, half * HFW:(half + 1) * HFW],
                     s=(d == 0), z=last:
                     e.matmul(o, lh, rh, start=s, stop=z))
                p.wait("tensor", tagq, sq)
                mm = p.op("tensor", tagm,
                          lambda e, o=pb[5][0:1, 0:HFW], lh=onescol[:, :],
                          rh=sqscr[d % 2], s=(d == 0), z=last:
                          e.matmul(o, lh, rh, start=s, stop=z))
                p.set_guard(("sqscr", d % 2), tagm, mm)
            nm = p.c(tagm)
            p.wait("scalar", tagm, nm)
            p.wcur("scalar", tagp)
            p.op("scalar", None,
                 lambda e, o=rows5[0:1, 0:512], a=pb[4][0:1, 0:HFW]:
                 e.activation(o, a, AF.Copy, scale=1.0 / 2048.0))
            mu2 = p.op("scalar", tagr,
                       lambda e, o=rows5[0:1, 512:1024], i=rows5[0:1, 0:512]:
                       e.activation(o, i, AF.Square))
            p.wait("vector", tagr, mu2)
            vr = p.op("vector", tagr,
                      lambda e, o=rows5[0:1, 512:1024], a=pb[5][0:1, 0:HFW],
                      m2=rows5[0:1, 512:1024]:
                      e.scalar_tensor_tensor(o, a, 1.0 / 2048.0, m2,
                                             ALU.mult, ALU.subtract))
            p.wait("scalar", tagr, vr)
            p.op("scalar", None,
                 lambda e, o=rows5[0:1, 512:1024], i=rows5[0:1, 512:1024]:
                 e.activation(o, i, AF.Ln, bias=epsc[0:1, 0:1]))
            rs = p.op("scalar", tagr,
                      lambda e, o=rows5[0:1, 512:1024],
                      i=rows5[0:1, 512:1024]:
                      e.activation(o, i, AF.Exp, scale=-0.5))
            p.wait("tensor", tagr, rs)
            p.guard("tensor", ("bank", 6))
            p.guard("tensor", ("bank", 7))
            p.op("tensor", None,
                 lambda e, o=pb[6][:, 0:HFW], lh=ones[0:1, 0:128],
                 rh=rows5[0:1, 0:512]:
                 e.matmul(o, lh, rh, start=True, stop=True))
            rp = p.op("tensor", tagp,
                      lambda e, o=pb[7][:, 0:HFW], lh=ones[0:1, 0:128],
                      rh=rows5[0:1, 512:1024]:
                      e.matmul(o, lh, rh, start=True, stop=True))
            p.wait("vector", tagp, rp)
            for d in range(DCH):
                p.op("vector", None,
                     lambda e, o=y1scr,
                     i=xv[:, d, half * HFW:(half + 1) * HFW],
                     m=pb[6][:, 0:HFW]:
                     e.tensor_sub(o, i, m))
                tr = p.op("vector", tagt,
                          lambda e, o=xvr[:, d, half * HFW:(half + 1) * HFW],
                          i=y1scr, r=pb[7][:, 0:HFW]:
                          e.tensor_mul(o, i, r))
                if is_ln1:
                    p.wait("scalar", tagt, tr)
                    p.op("scalar", "s_l1b",
                         lambda e, o=y1bf[:, d, half * HFW:(half + 1) * HFW],
                         i=xv[:, d, half * HFW:(half + 1) * HFW]:
                         e.activation(o, i, AF.Copy))
            if is_ln1:
                p.wait("tensor", tagt, p.c(tagt))
                for ti in range(4):
                    col = (half * 4 + ti) * NE
                    for d in range(DCH):
                        p.op("tensor", None,
                             lambda e, o=pb[2][:, col:col + NE],
                             lh=xv[:, d, half * HFW + ti * 128:
                                   half * HFW + (ti + 1) * 128],
                             rh=wr_s[:, d, :], s=(d == 0), z=False:
                             e.matmul(o, lh, rh, start=s, stop=z))
                    p.op("tensor", "s_rt",
                         lambda e, o=pb[2][:, col:col + NE],
                         lh=ones[0:1, 0:128], rh=br_s[0:1, :]:
                         e.matmul(o, lh, rh, start=False, stop=True))
            p.set_guard(("bank", 6), tagt, p.c(tagt))
            p.set_guard(("bank", 7), tagt, p.c(tagt))
            p.set_guard(("bank", 4), tagr, p.c(tagr))
            p.set_guard(("bank", 5), tagr, p.c(tagr))

    p.mark("P5ln1")
    n_pr = p.c("s_pr")
    for eng in ("tensor", "scalar", "vector"):
        p.wait(eng, "s_pr", n_pr)
    if debug:
        p.wait("scalar", "s_pr", n_pr)
        dump_fm("d_pre", xv, DCH, TT)
        p.wcur("gpsimd", "s_wm")
        dump_bf("d_on", qv, 16, TT)
        p.wcur("vector", "s_dbg")
    p.wcur("tensor", "s_rw")
    p.guard("tensor", ("bank", 2))
    ln_pass(is_ln1=True)
    n_rt = p.c("s_rt")
    p.wait("vector", "s_rt", n_rt)
    mxv_ = mxs[:, 0:8]
    esl = mxs[:, 8:40]
    sms = mxs[:, 40:48]
    inv = mxs[:, 48:56]
    mxe = mxs[:, 56:64]
    tpw = mxs[:, 64:72]
    msc = mxs[:, 72:104]
    for ti in range(8):
        p.op("vector", None,
             lambda e, o=mxv_[:, ti:ti + 1],
             i=pb[2][:, ti * NE:(ti + 1) * NE]:
             e.reduce_max(o, i, axis=AX.X))
    for ti in range(8):
        p.op("vector", "s_sm" if ti == 7 else None,
             lambda e, o=esl[:, ti * 4:(ti + 1) * 4],
             i=pb[2][:, ti * NE:(ti + 1) * NE], m=mxv_[:, ti:ti + 1]:
             e.tensor_scalar_sub(o, i, m))
    sm0 = p.c("s_sm")
    p.set_guard(("bank", 2), "s_sm", sm0)
    p.wait("scalar", "s_sm", sm0)
    se = p.op("scalar", "s_sme",
              lambda e, o=esl[:, :], i=esl[:, :]: e.activation(o, i, AF.Exp))
    p.wait("vector", "s_sme", se)
    for ti in range(8):
        p.op("vector", None,
             lambda e, o=sms[:, ti:ti + 1], i=esl[:, ti * 4:(ti + 1) * 4]:
             e.reduce_sum(o, i, axis=AX.X))
    for ti in range(8):
        p.op("vector", None,
             lambda e, o=mxe[:, ti:ti + 1], i=esl[:, ti * 4:(ti + 1) * 4]:
             e.reduce_max(o, i, axis=AX.X))
    for ti in range(8):
        p.op("vector", None,
             lambda e, o=inv[:, ti:ti + 1], i=sms[:, ti:ti + 1]:
             e.reciprocal(o, i))
    for ti in range(8):
        p.op("vector", None,
             lambda e, o=tpw[:, ti:ti + 1], a=mxe[:, ti:ti + 1],
             b_=inv[:, ti:ti + 1]: e.tensor_mul(o, a, b_))
    for ti in range(8):
        p.op("vector", None,
             lambda e, o=msc[:, ti * 4:(ti + 1) * 4],
             i=esl[:, ti * 4:(ti + 1) * 4], m=mxe[:, ti:ti + 1]:
             e.tensor_single_scalar(o, i, m, ALU.is_ge))
    for ti in range(8):
        p.op("vector", "s_sm" if ti == 7 else None,
             lambda e, o=msc[:, ti * 4:(ti + 1) * 4],
             i=msc[:, ti * 4:(ti + 1) * 4], m=tpw[:, ti:ti + 1]:
             e.tensor_scalar_mul(o, i, m))
    smn = p.c("s_sm")
    p.wait("tensor", "s_sm", smn)
    p.guard("tensor", ("bank", 3))
    p.guard("tensor", ("bank", 4))
    for ti in range(8):
        bi = 3 if ti < 4 else 4
        p.op("tensor", "s_mt" if ti % 4 == 3 else None,
             lambda e, o=pb[bi][0:4, (ti % 4) * 128:(ti % 4 + 1) * 128],
             i=msc[:, ti * 4:(ti + 1) * 4]:
             e.transpose(o, i, identf[:, :]))
    nmt = p.c("s_mt")
    p.wait("vector", "s_mt", nmt)
    p.op("vector", None,
         lambda e, o=maskfm[0:4, 0:512], a=pb[3][0:4, 0:512]:
         e.tensor_copy(o, a))
    mf = p.op("vector", "s_mf",
              lambda e, o=maskfm[0:4, 512:1024], a=pb[4][0:4, 0:512]:
              e.tensor_copy(o, a))
    p.set_guard(("bank", 3), "s_mf", mf)
    p.set_guard(("bank", 4), "s_mf", mf)

    if debug:
        p.wcur("scalar", "s_l1t")
        dump_fm("d_y1", xv, DCH, TT)
        p.wcur("scalar", "s_sm")
        p.dma("scalar", "s_dbg", dbg_dram("d_msc", [128, 112])[:, :],
              mxs[:, :]) if debug else None
        p.wcur("vector", "s_dbg")

    p.mark("P6ffn")
    n_l1b = p.c("s_l1b")
    p.wait("tensor", "s_l1b", n_l1b)
    p.wait("vector", "s_mf", p.c("s_mf"))
    p.wcur("gpsimd", "s_wm")
    p.wcur("gpsimd", "s_pm")
    p.wcur("gpsimd", "s_om")
    p.wcur("gpsimd", "s_ds")
    p.wcur("gpsimd", "s_vte")
    p.wcur("gpsimd", "s_pr")
    ldwg = {}
    ldwu = {}
    state6 = {"pend": None, "ldwd": None}

    def emit_maskrep(e_):
        for half in range(2):
            p.guard("tensor", ("bank", 7))
            p.wcur("tensor", "s_mf")
            mr = p.op("tensor", "s_mr",
                      lambda e, o=pb[7][:, 0:HFW],
                      lh=sel4[0:NE, e_ * 128:(e_ + 1) * 128],
                      rh=maskfm[0:NE, half * HFW:(half + 1) * HFW]:
                      e.matmul(o, lh, rh, start=True, stop=True))
            p.wait("vector", "s_mr", mr)
            mre = p.op("vector", "s_mre",
                       lambda e, o=mask_rep[:, half * HFW:(half + 1) * HFW],
                       a=pb[7][:, 0:HFW]:
                       e.tensor_copy(o, a))
            p.set_guard(("bank", 7), "s_mre", mre)

    def emit_down(part):
        (de, dblk, dbpar, dldwd, dnhv) = state6["pend"]
        if part == 0:
            p.wait("tensor", "s_hv", dnhv)
            p.wait("tensor", "s_fw", dldwd)
        rng = range(0, DCH // 2) if part == 0 else range(DCH // 2, DCH)
        for d in rng:
            for half in range(2):
                bi = 4 + (d * 2 + half) % 3
                bank = pb[bi]
                p.guard("tensor", ("bank", bi))
                for f in range(FBLK):
                    last = f == FBLK - 1
                    p.op("tensor", "s_dn" if last else None,
                         lambda e, o=bank[:, 0:HFW],
                         lh=wdtile[:, f, d * 128:(d + 1) * 128],
                         rh=hbuf[:, dbpar * FBLK + f,
                                 half * HFW:(half + 1) * HFW],
                         s=(f == 0), z=last:
                         e.matmul(o, lh, rh, start=s, stop=z))
                dn = p.c("s_dn")
                p.wait("vector", "s_dn", dn)
                da = p.op("vector", "s_da",
                          lambda e,
                          o=xvr[:, d, half * HFW:(half + 1) * HFW],
                          i=xv[:, d, half * HFW:(half + 1) * HFW],
                          a=bank[:, 0:HFW]: e.tensor_add(o, i, a))
                p.set_guard(("bank", bi), "s_da", da)
        if part == 1:
            p.set_guard(("wd",), "s_dn", p.c("s_dn"))
            p.set_guard(("h", dbpar), "s_dn", p.c("s_dn"))
            state6["pend"] = None

    for e_ in range(NE):
        emit_maskrep(e_)
        for blk in range(NBLK):
            bpar = blk % 2
            for f2 in range(FBLK):
                fi2 = blk * FBLK + f2
                gi = e_ * 64 + fi2
                tpar = fi2 % 2
                p.wait("gpsimd", "s_gu", 2 * (gi - 1))
                ldwg[fi2] = p.dma(
                    "gpsimd", "s_fw", wgtile[:, tpar, :, :],
                    wgl_c[e_][:, :, fi2 * 128:(fi2 + 1) * 128])
                ldwu[fi2] = p.dma(
                    "gpsimd", "s_fw", wutile[:, tpar, :, :],
                    wul_c[e_][:, :, fi2 * 128:(fi2 + 1) * 128])
            p.guard("vector", ("h", bpar))
            for f in range(FBLK):
                fi = blk * FBLK + f
                tpar = fi % 2
                fo = 0
                p.wait("tensor", "s_fw", ldwu[fi])
                for half in range(2):
                    bgi, bui = half, 2 + half
                    p.guard("tensor", ("bank", bgi))
                    p.guard("tensor", ("bank", bui))
                    for d in range(DCH):
                        last = d == DCH - 1
                        p.op("tensor", None,
                             lambda e, o=pb[bgi][:, 0:HFW],
                             lh=wgtile[:, tpar, d, fo:fo + 128],
                             rh=y1bf[:, d, half * HFW:(half + 1) * HFW],
                             s=(d == 0), z=last:
                             e.matmul(o, lh, rh, start=s, stop=z))
                    for d in range(DCH):
                        last = d == DCH - 1
                        p.op("tensor", "s_gu" if last else None,
                             lambda e, o=pb[bui][:, 0:HFW],
                             lh=wutile[:, tpar, d, fo:fo + 128],
                             rh=y1bf[:, d, half * HFW:(half + 1) * HFW],
                             s=(d == 0), z=last:
                             e.matmul(o, lh, rh, start=s, stop=z))
                    gu = p.c("s_gu")
                    sgsl = sgscr[:, ((f % 2) * 2 + half) * 512:
                                 ((f % 2) * 2 + half + 1) * 512]
                    p.wait("scalar", "s_gu", gu)
                    p.guard("scalar", ("sg", (f % 2) * 2 + half))
                    hh = p.op("scalar", "s_hh",
                              lambda e, o=sgsl, a=pb[bgi][:, 0:HFW]:
                              e.activation(o, a, AF.Silu))
                    p.set_guard(("bank", bgi), "s_hh", hh)
                    p.wait("vector", "s_hh", hh)
                    p.op("vector", None,
                         lambda e, o=sgsl, i=sgsl,
                         m=mask_rep[:, half * HFW:(half + 1) * HFW]:
                         e.tensor_mul(o, i, m))
                    hv = p.op("vector", "s_hv",
                              lambda e,
                              o=hbuf[:, bpar * FBLK + f,
                                     half * HFW:(half + 1) * HFW],
                              i=sgsl, a=pb[bui][:, 0:HFW]:
                              e.tensor_mul(o, i, a))
                    p.set_guard(("bank", bui), "s_hv", hv)
                    p.set_guard(("sg", (f % 2) * 2 + half), "s_hv", hv)
                if f == 0 and state6["pend"] is not None:
                    emit_down(0)
                if f == 1:
                    if state6["pend"] is not None:
                        emit_down(1)
                    p.guard("gpsimd", ("wd",))
                    state6["ldwd"] = p.dma(
                        "gpsimd", "s_fw", wdtile[:, :, :],
                        wdl_r[e_][:, blk * FBLK:(blk + 1) * FBLK, :])
            state6["pend"] = (e_, blk, bpar, state6["ldwd"], p.c("s_hv"))
    emit_down(0)
    emit_down(1)

    p.mark("P7ln2")
    n_da = p.c("s_da")
    for eng in ("tensor", "scalar", "vector"):
        p.wait(eng, "s_da", n_da)
    if debug:
        p.wait("scalar", "s_da", n_da)
        dump_fm("d_y2a", xv, DCH, TT)
        p.wcur("vector", "s_dbg")
    p.wcur("scalar", "s_hv")
    p.wcur("vector", "s_hv")
    ln_pass(is_ln1=False)

    p.mark("P8out")
    n_l2t = p.c("s_l2t")
    p.wait("tensor", "s_l2t", n_l2t)
    p.wcur("vector", "s_mre")
    for ti in range(8):
        p.guard("vector", ("ytm",))
        for k4 in range(4):
            bi = k4
            p.guard("tensor", ("bank", bi))
            for dd in range(4):
                d = k4 * 4 + dd
                p.op("tensor", "s_tr" if dd == 3 else None,
                     lambda e, o=pb[bi][:, dd * 128:(dd + 1) * 128],
                     i=xv[:, d, ti * 128:(ti + 1) * 128]:
                     e.transpose(o, i, identf[:, :]))
            ntr = p.c("s_tr")
            p.wait("vector", "s_tr", ntr)
            te = p.op("vector", "s_te",
                      lambda e, o=ytm[:, k4 * 512:(k4 + 1) * 512],
                      a=pb[bi][:, 0:HFW]: e.tensor_copy(o, a))
            p.set_guard(("bank", bi), "s_te", te)
        nte = p.c("s_te")
        p.wait("sync", "s_te", nte)
        od = p.dma("sync", "s_out", yout[ti * 128:(ti + 1) * 128, :],
                   ytm[:, :])
        p.set_guard(("ytm",), "s_out", od)
    p.wait("sync", "s_out", p.c("s_out"))
    if debug:
        p.wcur("sync", "s_dbg")
        p.wcur("scalar", "s_dbg")
        p.wcur("gpsimd", "s_dbg")

    with nc.Block() as block:
        @block.sync
        def _(e):
            p.replay("sync", e)

        @block.scalar
        def _(e):
            p.replay("scalar", e)

        @block.vector
        def _(e):
            p.replay("vector", e)

        @block.tensor
        def _(e):
            p.replay("tensor", e)

        @block.gpsimd
        def _(e):
            p.replay("gpsimd", e)
    return nc, st


def kernel(**inputs):
    f32 = np.float32

    def g(name):
        return np.ascontiguousarray(np.asarray(inputs[name], f32))

    x = g("L0_attn_hidden_states")
    shared = {
        "gnwd": g("L0_attn_g_norm_w").reshape(128, 1),
        "bqt": np.ascontiguousarray(g("L0_attn_bq").reshape(16, 128).T),
        "bkt": np.ascontiguousarray(g("L0_attn_bk").reshape(8, 128).T),
        "bvt": np.ascontiguousarray(g("L0_attn_bv").reshape(8, 128).T),
        "bg2t": np.ascontiguousarray(g("L0_attn_bg2").reshape(8, 128).T),
        "onesd": np.ones((1, 128), f32),
        "identd": np.eye(128, dtype=f32),
        "amaskd": np.triu(np.ones((128, 128), f32)),
        "wq": g("L0_attn_Wq"),
        "wk": g("L0_attn_Wk"),
        "wv": g("L0_attn_Wv"),
        "wg1d": np.ascontiguousarray(
            g("L0_attn_Wg1").reshape(DCH, 128, GATE).transpose(1, 0, 2)),
        "wg2d": g("L0_attn_Wg2"),
        "wo": g("L0_attn_Wo"),
        "wrd": np.ascontiguousarray(
            g("L0_ffn_Wr").reshape(DCH, 128, NE).transpose(1, 0, 2)),
        "brd": g("L0_ffn_br").reshape(1, NE),
        "seld": np.repeat(np.eye(NE, dtype=f32), 128, axis=1).reshape(
            NE, NE * 128),
        "wgl": g("L0_ffn_Wg"),
        "wul": g("L0_ffn_Wu"),
        "wdl": g("L0_ffn_Wd"),
    }
    in_maps = []
    for c in range(8):
        b, hf = divmod(c, 2)
        m = dict(shared)
        m["xt"] = np.ascontiguousarray(x[b, hf * TT:(hf + 1) * TT, :].T)
        m["xpt"] = np.ascontiguousarray(x[b, 0:TT, :].T)
        m["flagd"] = np.full((128, 1), float(hf), f32)
        in_maps.append(m)
    debug = bool(inputs.pop("_debug", False))
    nc, st = build_program(debug=debug)
    res = run_bass_kernel_spmd(nc, in_maps, core_ids=list(range(8)))
    if debug:
        kernel._dbg = res.results
    st.close()
    out = np.empty((B, L, D), f32)
    for c in range(8):
        b, hf = divmod(c, 2)
        out[b, hf * TT:(hf + 1) * TT, :] = res.results[c]["y"]
    return out
